# version 24
# baseline (speedup 1.0000x reference)
import numpy as np
import ml_dtypes
import concourse.bass as bass
import concourse.mybir as mybir
from concourse.bass_utils import run_bass_kernel_spmd
from contextlib import ExitStack

F32 = mybir.dt.float32
BF16 = mybir.dt.bfloat16
FP16 = mybir.dt.float16
FP8 = mybir.dt.float8e4
AF = mybir.ActivationFunctionType
ALU = mybir.AluOpType

NCORES = 8
SERIALIZE_SP = True
D = 1024
FC = 8
LN2 = float(np.log(2.0))
MAGIC = float(3 << 22)
EPS_LN = 1e-5
L0_EPS = 1e-6

ENGS = ("sp", "gp", "pe", "dve", "act")
SEM_OF = {"pe": "spe", "dve": "sdve", "act": "sact", "gp": "spool", "sp": "dsp"}


class Prog:
    def __init__(self, sems):
        self.ops = {e: [] for e in ENGS}
        self.sem = sems
        self.cnt = {k: 0 for k in sems}
        self.lastw = {}
        self.wait_high = {}
        self.quiesced = {}

    def wait(self, eng, tok):
        if tok is None:
            return
        sem, val = tok
        if sem in ("dsp", "dgp"):
            val = self.cnt[sem]
        if val <= 0:
            return
        if self.lastw.get((eng, sem), -1) >= val:
            return
        self.lastw[(eng, sem)] = val
        self.wait_high[sem] = max(self.wait_high.get(sem, 0), val)
        h = self.sem[sem]
        self.ops[eng].append(lambda E, h=h, v=val: E.wait_ge(h, v))

    def waits(self, eng, toks):
        for t in toks:
            self.wait(eng, t)

    def emit(self, eng, fn, sem=None, n=1):
        if sem is None:
            self.ops[eng].append(fn)
            return None
        self.cnt[sem] += n
        v = self.cnt[sem]
        h = self.sem[sem]
        self.ops[eng].append(lambda E, f=fn, h=h, m=n: f(E).then_inc(h, m))
        return (sem, v)

    def tok(self, sem):
        return (sem, self.cnt[sem])


class Builder:

    def __init__(self, T, rg_cores, dump):
        self.T = T
        self.TT = T // 128
        self.TG = T // 512
        self.dump = dump
        self.rg = [list(range(rg_cores))]
        self.nc = bass.Bass(target_bir_lowering=False)
        self.dumps = {}

    def din(self, name, shape, dtype=F32):
        return self.nc.dram_tensor(name, shape, dtype, kind="ExternalInput").ap()

    def ddump(self, name, shape, dtype=F32):
        if name in self.dump:
            ap = self.nc.dram_tensor("dbg_" + name, shape, dtype, kind="ExternalOutput").ap()
            self.dumps[name] = ap
            return ap
        return None

    def dma(self, eng, out, in_, **kw):
        P = self.P
        sem = "dsp" if eng == "sp" else "dgp"
        wh = P.wait_high.get(sem, 0)
        if wh > P.quiesced.get(sem, 0) and P.cnt[sem] >= wh:
            h = P.sem[sem]
            P.ops[eng].append(lambda E, h=h, v=wh: E.wait_ge(h, v))
            P.quiesced[sem] = wh
        return P.emit(eng, lambda E, o=out, i=in_, k=dict(kw): E.dma_start(out=o, in_=i, **k), sem, 16)

    def op(self, eng, fn, mark=True):
        if eng in ("dve", "act", "gp"):
            self.P.wait(eng, self.P.tok(SEM_OF[eng]))
        return self.P.emit(eng, fn, SEM_OF[eng] if mark else None)

    def selfw(self, eng):
        self.P.wait(eng, self.P.tok(SEM_OF[eng]))

    def barrier(self):
        toks = [self.P.tok(s) for s in self.P.cnt]
        for e in ENGS:
            self.P.waits(e, toks)

    def v3(self, t, a, b, off=0):
        if off:
            return t.ap()[:, off:off + a * b].rearrange("p (a b) -> p a b", a=a, b=b)
        return t.ap()[:, 0:a * b].rearrange("p (a b) -> p a b", a=a, b=b)

    def transp8(self, srcs, dst_ap, eng="act"):
        P, op = self.P, self.op
        bi = self._tp_flip
        self._tp_flip ^= 1
        bank = self.PT if bi == 0 else self.PT2
        key = ("tp", bi)
        self.bank_wait(key)
        n = len(srcs)
        tmm = None
        for i, srcv in enumerate(srcs):
            pv = bank.ap()[:, i * 128:(i + 1) * 128]
            tmm = P.emit("pe", lambda E, sv=srcv, pv=pv, st=(i == 0), sp=(i == n - 1): E.matmul(
                out=pv, lhsT=sv, rhs=self.ident.ap(), start=st, stop=sp, is_transpose=True),
                "spe" if i == n - 1 else None)
        P.wait(eng, tmm)
        bview = bank.ap()[:, 0:n * 128].rearrange("p (a b) -> p a b", b=128)
        if eng == "act":
            tev = op("act", lambda E, d=dst_ap, bv=bview: E.copy(out=d, in_=bv), mark=True)
        else:
            tev = op("dve", lambda E, d=dst_ap, bv=bview: E.tensor_copy(out=d, in_=bv), mark=True)
        self.bank_done(key, tev)
        return tev

    def transp(self, src, dst, slot, eng="act"):
        return self.transp8([src], dst.unsqueeze(1) if dst.ndim == 2 else dst, eng=eng)

    def bank_wait(self, key):
        t = self._banktok.get(key)
        if t is not None:
            if isinstance(t, list):
                self.P.waits("pe", t)
            else:
                self.P.wait("pe", t)

    def bank_done(self, key, tok):
        self._banktok[key] = tok

    def alloc(self):
        self._banktok = {}
        self._pt_tok = [None] * 8
        self._tp_flip = 0
        nc, E = self.nc, self.stk.enter_context
        T, TT = self.T, self.TT
        self.G0 = E(nc.sbuf_tensor("G0", [128, max(8 * T, 16384)], BF16))
        self.G1 = E(nc.sbuf_tensor("G1", [128, 8 * T], BF16))
        self.G2 = E(nc.sbuf_tensor("G2", [128, max(8 * T, 16384)], BF16))
        self.G3 = E(nc.sbuf_tensor("G3", [128, max(8 * T, 7 * T + 1536)], BF16))
        self.G4 = E(nc.sbuf_tensor("G4", [128, max(4 * T, 8192)], BF16))
        self.H0 = E(nc.sbuf_tensor("H0", [128, 2048 + 2048 + 1024], F32))
        self.XS = E(nc.sbuf_tensor("XS", [128, 1024], F32))
        self.cpool = E(nc.sbuf_tensor("cpool", [128, 16], F32))
        self.ident = E(nc.sbuf_tensor("ident", [128, 128], BF16))
        self.identf = E(nc.sbuf_tensor("identf", [128, 128], F32))
        self.maskt = E(nc.sbuf_tensor("maskt", [128, 64], F32))
        self.onesf = E(nc.sbuf_tensor("onesf", [128, 4], F32))
        self.onesb = E(nc.sbuf_tensor("onesb", [128, 4], BF16))
        self.costt = E(nc.sbuf_tensor("costt", [128, TT * 32], BF16))
        self.sintt = E(nc.sbuf_tensor("sintt", [128, TT * 32], BF16))
        self.bcols = E(nc.sbuf_tensor("bcols", [128, 96], F32))
        self.statc = E(nc.sbuf_tensor("statc", [128, 32], F32))
        self.vkTb = E(nc.sbuf_tensor("vkTb", [128, 8 * 65], BF16))
        self.vkTf = E(nc.sbuf_tensor("vkTf", [128, 8 * 65], F32))
        self.rowsc = E(nc.sbuf_tensor("rowsc", [1, 1024], F32))
        self.rowb2 = E(nc.sbuf_tensor("rowb2", [1, 1024], FP16))
        self.ones16 = E(nc.sbuf_tensor("ones16", [1, 128], FP16))
        self.Ssb = E(nc.sbuf_tensor("Ssb", [128, 4 * 128], F32))
        self.Dsb = E(nc.sbuf_tensor("Dsb", [128, 4 * 32], F32))
        self.Csb = E(nc.sbuf_tensor("Csb", [128, 4 * 32], F32))
        self.Esb = E(nc.sbuf_tensor("Esb", [128, 4 * 32], F32))
        self.Msb = E(nc.sbuf_tensor("Msb", [128, 4 * 32], F32))
        self.Gsb = E(nc.sbuf_tensor("Gsb", [128, 516], F32))
        self.kndup = E(nc.sbuf_tensor("kndup", [128, max(2048, 2 * T)], BF16))
        self.f16t = E(nc.sbuf_tensor("f16t", [128, 1024], FP16))
        self.b2f16 = E(nc.sbuf_tensor("b2f16", [128, 8], FP16))
        self.h8t = E(nc.sbuf_tensor("h8t", [128, 2048], FP8))
        self.w8t = E(nc.sbuf_tensor("w8t", [128, 128], FP8))
        self.wst = E(nc.sbuf_tensor("wst", [128, 2048], BF16))
        self.wlps = E(nc.sbuf_tensor("wlps", [128, 64], F32))
        self.wlpr = E(nc.sbuf_tensor("wlpr", [128, 64], F32))
        self.PB = [E(nc.psum_tensor(f"pb{i}", [128, 512], F32)) for i in range(5)]
        self.PT = E(nc.psum_tensor("pt", [128, 1024], BF16))
        self.PT2 = E(nc.psum_tensor("pt2", [128, 1024], BF16))
        self.PM = E(nc.psum_tensor("pm", [128, 512], F32))
        sems = {}
        for s in ("dsp", "dgp", "spe", "sdve", "sact", "spool", "scc"):
            sems[s] = E(nc.semaphore(s))
        self.P = Prog(sems)

    def declare_io(self):
        T = self.T
        din = self.din
        nc = self.nc
        self.x_in = din("x", [T, D])
        self.wqk = din("wqk", [D, 2048])
        self.wvo = din("wvo", [D, 2048])
        self.w1 = din("w1", [4096, D])
        self.b1v = din("b1v", [4096])
        self.w2 = din("w2", [D, 4096])
        self.b2v = din("b2v", [1024])
        self.cos32 = din("cos32", [T, 32])
        self.sin32 = din("sin32", [T, 32])
        self.identin = din("identin", [128, 128], BF16)
        self.rmaskin = din("rmaskin", [128, 16])
        self.maskin = din("maskin", [128, 64])
        self.l1wt = din("l1wt", [D, 896])
        self.l1qc = din("l1qc", [64, 512])
        self.l1vc = din("l1vc", [64, 1024])
        self.l1wo = din("l1wo", [D, D])
        self.l1wab = din("l1wab", [512])
        self.l1gamma = din("l1gamma", [D])
        self.l1beta = din("l1beta", [D])
        self.wgate = din("wgate", [3072, D])
        self.wup = din("wup", [3072, D])
        self.wdown = din("wdown", [D, 3072])
        self.y_out = nc.dram_tensor("y", [T, D], F32, kind="ExternalOutput").ap()
        self.xadr = nc.dram_tensor("xadr", [T, D], F32, kind="Internal").ap()
        self.x1dr = nc.dram_tensor("x1dr", [T, D], F32, kind="Internal").ap()
        self.xn1dr = nc.dram_tensor("xn1dr", [T, D], BF16, kind="Internal").ap()
        self.updr = nc.dram_tensor("updr", [3072, T], BF16, kind="Internal").ap()
        self.ksumdr = nc.dram_tensor("ksumdr", [1024], F32, kind="Internal").ap()
        self.cc_vk_i = nc.dram_tensor("cc_vk_i", [128, 520], F32, kind="Internal").ap()
        self.cc_vk_o = nc.dram_tensor("cc_vk_o", [128, 520], F32, kind="Internal").ap()
        self.cc_g_i = nc.dram_tensor("cc_g_i", [128, 516], F32, kind="Internal").ap()
        self.cc_g_o = nc.dram_tensor("cc_g_o", [128 * len(self.rg[0]), 516], F32, kind="Internal").ap()
        self.cc_a_i = nc.dram_tensor("cc_a_i", [1, 4], F32, kind="Internal").ap()
        self.cc_a_o = nc.dram_tensor("cc_a_o", [1, 4], F32, kind="Internal").ap()
        self.betadr = nc.dram_tensor("betadr", [896], F32, kind="Internal").ap()
        self.wq2dr = nc.dram_tensor("wq2dr", [256, 16384], BF16, kind="Internal").ap()
        self.w1tdr = nc.dram_tensor("w1tdr", [128, 32 * 1024], BF16, kind="Internal").ap()

    def ph_init(self):
        P, op, dma = self.P, self.op, self.dma
        dma("sp", self.ident.ap(), self.identin)
        dma("sp", self.maskt.ap(), self.maskin)
        dma("gp", self.costt.ap().rearrange("p (tt f) -> p tt f", f=32), self.cos32.rearrange("(tt p) f -> p tt f", p=128))
        dma("gp", self.sintt.ap().rearrange("p (tt f) -> p tt f", f=32), self.sin32.rearrange("(tt p) f -> p tt f", p=128))
        dma("gp", self.bcols.ap()[:, 0:32], self.b1v.rearrange("(c p) -> p c", p=128), allow_slow_non_contiguous=True)
        dma("gp", self.bcols.ap()[:, 32:40], self.b2v.rearrange("(c p) -> p c", p=128), allow_slow_non_contiguous=True)
        dma("gp", self.bcols.ap()[:, 40:48], self.l1gamma.rearrange("(c p) -> p c", p=128), allow_slow_non_contiguous=True)
        dma("gp", self.bcols.ap()[:, 48:56], self.l1beta.rearrange("(c p) -> p c", p=128), allow_slow_non_contiguous=True)
        dma("sp", self.rowsc.ap()[0:1, 0:1024], self.b2v.unsqueeze(0))
        t_bc = P.tok("dgp")
        t_ld = P.tok("dsp")
        op("dve", lambda E: E.memset(self.cpool.ap()[:, 0:1], EPS_LN))
        op("dve", lambda E: E.memset(self.cpool.ap()[:, 1:2], 1.0))
        op("dve", lambda E: E.memset(self.cpool.ap()[:, 2:3], L0_EPS))
        op("dve", lambda E: E.memset(self.cpool.ap()[:, 3:4], 1e-24))
        op("dve", lambda E: E.memset(self.cpool.ap()[:, 4:5], 1e-12))
        op("dve", lambda E: E.memset(self.cpool.ap()[:, 5:6], -80.0))
        op("dve", lambda E: E.memset(self.cpool.ap()[:, 6:7], 22.0))
        op("dve", lambda E: E.memset(self.cpool.ap()[:, 7:8], -22.0))
        op("dve", lambda E: E.memset(self.onesf.ap(), 1.0))
        op("dve", lambda E: E.memset(self.onesb.ap(), 1.0))
        self.t_ones = op("dve", lambda E: E.memset(self.ones16.ap(), 1.0), mark=True)
        P.wait("act", t_bc)
        op("act", lambda E: E.copy(out=self.b2f16.ap(), in_=self.bcols.ap()[:, 32:40]))
        P.wait("act", t_ld)
        self.t_b2q = op("act", lambda E: E.copy(out=self.rowb2.ap(), in_=self.rowsc.ap()[0:1, 0:1024]), mark=True)
        P.wait("dve", t_bc)
        op("dve", lambda E: E.reciprocal(out=self.statc.ap()[:, 0:8], in_=self.bcols.ap()[:, 40:48]))
        op("dve", lambda E: E.tensor_tensor(out=self.bcols.ap()[:, 56:64], in0=self.bcols.ap()[:, 48:56], in1=self.statc.ap()[:, 0:8], op=ALU.mult))
        P.wait("gp", t_ld)
        self.t_idf = dma("gp", self.identf.ap(), self.ident.ap())

    def ph_ln(self, src_dr, dst_tm, spill_bf=None):
        P, op, dma = self.P, self.op, self.dma
        toks = []
        last3 = {}
        for tt in range(self.TT):
            xb = self.XS.ap() if tt % 2 == 0 else self.H0.ap()[:, 0:1024]
            lp = last3.get(tt % 2)
            if lp is not None:
                P.wait("sp", lp)
            tl = dma("sp", xb, src_dr[tt * 128:(tt + 1) * 128, :])
            P.wait("dve", tl)
            op("dve", lambda E, xb=xb: E.bn_stats(out=self.statc.ap()[:, 8:14], in_=xb[:, 0:512]))
            op("dve", lambda E, xb=xb: E.bn_stats(out=self.statc.ap()[:, 14:20], in_=xb[:, 512:1024]))
            self.selfw("dve")
            t1 = op("dve", lambda E: E.bn_aggr(out=self.statc.ap()[:, 20:22], in_=self.statc.ap()[:, 8:20]), mark=True)
            P.wait("act", t1)
            op("act", lambda E: E.activation(out=self.statc.ap()[:, 22:23], in_=self.statc.ap()[:, 21:22], func=AF.Ln, bias=self.cpool.ap()[:, 0:1]))
            self.selfw("act")
            t2 = op("act", lambda E: E.activation(out=self.statc.ap()[:, 23:24], in_=self.statc.ap()[:, 22:23], func=AF.Exp, scale=-0.5), mark=True)
            P.wait("dve", t2)
            t3 = op("dve", lambda E, tt=tt, xb=xb: E.tensor_scalar(out=dst_tm[:, tt, :], in0=xb,
                                                            scalar1=self.statc.ap()[:, 20:21], scalar2=self.statc.ap()[:, 23:24],
                                                            op0=ALU.subtract, op1=ALU.mult), mark=True)
            toks.append(t3)
            last3[tt % 2] = t3
            if spill_bf is not None:
                P.wait("sp", t3)
                dma("sp", spill_bf[tt * 128:(tt + 1) * 128, :], dst_tm[:, tt, :])
        return toks

    def ph_transpose_all(self, src_tm, dst_fm, toks=None):
        P = self.P
        for tt in range(self.TT):
            if toks is not None:
                P.wait("pe", toks[tt])
            srcs = [src_tm[:, tt, fc * 128:(fc + 1) * 128] for fc in range(FC)]
            dst = dst_fm[:, :, tt * 128:(tt + 1) * 128]
            self.transp8(srcs, dst)

    def proj_tm(self, xn_fm, wbuf, dst_tm, tok_w, ncols=1024):
        P, op = self.P, self.op
        ng = ncols // 512
        for tt in range(self.TT):
            for g in range(ng):
                bi = (tt * ng + g) % 4
                pb = self.PB[bi]
                P.waits("pe", tok_w)
                self.bank_wait(bi)
                tmm = None
                for fc in range(FC):
                    tmm = P.emit("pe", lambda E, tt=tt, g=g, fc=fc, pb=pb: E.matmul(
                        out=pb.ap(), lhsT=xn_fm[:, fc, tt * 128:(tt + 1) * 128],
                        rhs=wbuf[:, fc, g * 512:(g + 1) * 512],
                        start=(fc == 0), stop=(fc == FC - 1)), "spe" if fc == FC - 1 else None)
                P.wait("act", tmm)
                tev = op("act", lambda E, tt=tt, g=g, pb=pb: E.copy(out=dst_tm[:, tt, g * 512:(g + 1) * 512], in_=pb.ap()), mark=True)
                self.bank_done(bi, tev)
        return P.tok("sact")

    def rope_tile(self, t_tm, tt, prev):
        P, op = self.P, self.op
        src = t_tm[:, tt, :]
        r4 = src.rearrange("p (h q i) -> p h q i", h=16, q=2, i=32)
        Ev = r4[:, :, 0, :]
        Ov = r4[:, :, 1, :]
        cosv = self.costt.ap()[:, tt * 32:(tt + 1) * 32].unsqueeze(1).broadcast_to([128, 16, 32])
        sinv = self.sintt.ap()[:, tt * 32:(tt + 1) * 32].unsqueeze(1).broadcast_to([128, 16, 32])
        sc = self.H0.ap()[:, 0:2048].rearrange("p (k h i) -> p k h i", k=4, h=16, i=32)
        if prev is not None:
            P.wait("dve", prev)
        op("dve", lambda E, Ev=Ev, cosv=cosv, sc=sc: E.tensor_tensor(out=sc[:, 0], in0=Ev, in1=cosv, op=ALU.mult))
        op("dve", lambda E, Ov=Ov, sinv=sinv, sc=sc: E.tensor_tensor(out=sc[:, 1], in0=Ov, in1=sinv, op=ALU.mult))
        op("dve", lambda E, Ev=Ev, sinv=sinv, sc=sc: E.tensor_tensor(out=sc[:, 2], in0=Ev, in1=sinv, op=ALU.mult))
        t0 = op("dve", lambda E, Ov=Ov, cosv=cosv, sc=sc: E.tensor_tensor(out=sc[:, 3], in0=Ov, in1=cosv, op=ALU.mult), mark=True)
        P.wait("dve", t0)
        op("dve", lambda E, sc=sc: E.tensor_tensor(out=sc[:, 0], in0=sc[:, 0], in1=sc[:, 1], op=ALU.subtract))
        t1 = op("dve", lambda E, sc=sc: E.tensor_tensor(out=sc[:, 2], in0=sc[:, 2], in1=sc[:, 3], op=ALU.add), mark=True)
        P.wait("act", t1)
        op("act", lambda E, Ev=Ev, sc=sc: E.activation(out=Ev, in_=sc[:, 0], func=AF.Relu))
        t2 = op("act", lambda E, Ov=Ov, sc=sc: E.activation(out=Ov, in_=sc[:, 2], func=AF.Relu), mark=True)
        return t2

    def rope_relu(self, t_tm):
        P, op = self.P, self.op
        toks = []
        for tt in range(self.TT):
            src = t_tm[:, tt, :]
            r4 = src.rearrange("p (h q i) -> p h q i", h=16, q=2, i=32)
            Ev = r4[:, :, 0, :]
            Ov = r4[:, :, 1, :]
            cosv = self.costt.ap()[:, tt * 32:(tt + 1) * 32].unsqueeze(1).broadcast_to([128, 16, 32])
            sinv = self.sintt.ap()[:, tt * 32:(tt + 1) * 32].unsqueeze(1).broadcast_to([128, 16, 32])
            sc = self.H0.ap()[:, 0:2048].rearrange("p (k h i) -> p k h i", k=4, h=16, i=32)
            op("dve", lambda E, Ev=Ev, cosv=cosv, sc=sc: E.tensor_tensor(out=sc[:, 0], in0=Ev, in1=cosv, op=ALU.mult))
            op("dve", lambda E, Ov=Ov, sinv=sinv, sc=sc: E.tensor_tensor(out=sc[:, 1], in0=Ov, in1=sinv, op=ALU.mult))
            op("dve", lambda E, Ev=Ev, sinv=sinv, sc=sc: E.tensor_tensor(out=sc[:, 2], in0=Ev, in1=sinv, op=ALU.mult))
            t0 = op("dve", lambda E, Ov=Ov, cosv=cosv, sc=sc: E.tensor_tensor(out=sc[:, 3], in0=Ov, in1=cosv, op=ALU.mult), mark=True)
            P.wait("dve", t0)
            op("dve", lambda E, sc=sc: E.tensor_tensor(out=sc[:, 0], in0=sc[:, 0], in1=sc[:, 1], op=ALU.subtract))
            t1 = op("dve", lambda E, sc=sc: E.tensor_tensor(out=sc[:, 2], in0=sc[:, 2], in1=sc[:, 3], op=ALU.add), mark=True)
            P.wait("act", t1)
            op("act", lambda E, Ev=Ev, sc=sc: E.activation(out=Ev, in_=sc[:, 0], func=AF.Relu))
            t2 = op("act", lambda E, Ov=Ov, sc=sc: E.activation(out=Ov, in_=sc[:, 2], func=AF.Relu), mark=True)
            toks.append(t2)
            P.wait("dve", t2)
        return toks

    def build_l0(self, upto):
        P, op, dma = self.P, self.op, self.dma
        T, TT, TG = self.T, self.TT, self.TG
        xn_tm = self.v3(self.G0, TT, 1024)
        xn_fm = self.v3(self.G1, FC, T)
        q_tm = self.v3(self.G3, TT, 1024)
        k_tm = self.v3(self.G2, TT, 1024)
        v_tm = self.v3(self.G0, TT, 1024)
        wbuf = self.v3(self.G4, FC, 1024)
        q_fm = self.v3(self.G0, FC, T)
        attn_tm = self.v3(self.G3, TT, 1024)
        attn_fm = self.v3(self.G2, FC, T)

        wb2 = [self.v3(self.G4, FC, 512), self.v3(self.G4, FC, 512, off=4096)]
        wsrc6 = [(self.wqk, 0), (self.wqk, 512), (self.wqk, 1024), (self.wqk, 1536),
                 (self.wvo, 0), (self.wvo, 512)]
        ldq = {}
        rdtok = {}

        def wload(j):
            ws, coff = wsrc6[j]
            rt = rdtok.get(j % 2)
            if rt is not None:
                P.wait("gp", rt)
            ldq[j] = [dma("gp", wb2[j % 2][:, fc, :], ws[fc * 128:(fc + 1) * 128, coff:coff + 512])
                      for fc in range(FC)][-1]

        wload(0)
        wload(1)
        tok_xn = self.ph_ln(self.x_in, xn_tm)
        d_xn = self.ddump("xn", [T, D], BF16)
        if d_xn is not None:
            for tt in range(TT):
                P.wait("sp", tok_xn[tt])
                dma("sp", d_xn[tt * 128:(tt + 1) * 128, :], xn_tm[:, tt, :])
        P.wait("pe", self.P.tok("dsp"))
        self.ph_transpose_all(xn_tm, xn_fm, tok_xn)
        self.barrier()
        if upto == "ln":
            return

        dst6 = [q_tm, q_tm, k_tm, k_tm, v_tm, v_tm]
        tq_r = []
        tk_r = []
        rprev = None
        for j in range(6):
            dv = dst6[j][:, :, (j % 2) * 512:(j % 2) * 512 + 512]
            for tt in range(TT):
                bi = tt % 4
                pb = self.PB[bi]
                P.waits("pe", [ldq[j]])
                self.bank_wait(bi)
                tmm = None
                for fc in range(FC):
                    tmm = P.emit("pe", lambda E, tt=tt, fc=fc, pb=pb, j=j: E.matmul(
                        out=pb.ap(), lhsT=xn_fm[:, fc, tt * 128:(tt + 1) * 128],
                        rhs=wb2[j % 2][:, fc, :],
                        start=(fc == 0), stop=(fc == FC - 1)), "spe" if fc == FC - 1 else None)
                P.wait("act", tmm)
                tev = op("act", lambda E, tt=tt, dv=dv, pb=pb: E.copy(out=dv[:, tt, :], in_=pb.ap()), mark=True)
                self.bank_done(bi, tev)
                if j == 1:
                    P.wait("dve", tev)
                    rprev = self.rope_tile(q_tm, tt, rprev)
                    tq_r.append(rprev)
                elif j == 3:
                    P.wait("dve", tev)
                    rprev = self.rope_tile(k_tm, tt, rprev)
                    tk_r.append(rprev)
            rdtok[j % 2] = P.tok("spe")
            if j + 2 < 6:
                wload(j + 2)
        tv = P.tok("sact")
        d_q = self.ddump("q_tm", [T, D], BF16)
        if d_q is not None:
            for tt in range(TT):
                P.wait("sp", tq_r[tt])
                dma("sp", d_q[tt * 128:(tt + 1) * 128, :], q_tm[:, tt, :])
        d_k = self.ddump("k_tm", [T, D], BF16)
        if d_k is not None:
            for tt in range(TT):
                P.wait("sp", tk_r[tt])
                dma("sp", d_k[tt * 128:(tt + 1) * 128, :], k_tm[:, tt, :])
        self.barrier()
        if upto == "l0qk":
            return

        self.barrier()

        for h in range(16):
            pb = self.PB[h % 4]
            tmm = None
            for tt in range(TT):
                tmm = P.emit("pe", lambda E, h=h, tt=tt, pb=pb: E.matmul(
                    out=pb.ap()[0:64, 0:64],
                    lhsT=k_tm[:, tt, h * 64:(h + 1) * 64],
                    rhs=v_tm[:, tt, h * 64:(h + 1) * 64],
                    start=(tt == 0), stop=(tt == TT - 1)), "spe" if tt == TT - 1 else None)
            P.wait("dve", tmm)
            tev = op("dve", lambda E, h=h, pb=pb: E.tensor_copy(
                out=self.vkTf.ap()[(h % 2) * 64:(h % 2) * 64 + 64, (h // 2) * 65:(h // 2) * 65 + 64],
                in_=pb.ap()[0:64, 0:64]), mark=True)
            P.wait("pe", tev)
        P.wait("pe", self.t_ones)
        for g in range(2):
            pm = self.PM if g == 0 else self.PB[4]
            tks = None
            for tt in range(TT):
                tks = P.emit("pe", lambda E, g=g, tt=tt, pm=pm: E.matmul(
                    out=pm.ap()[0:1, 0:512], lhsT=self.onesb.ap()[:, 0:1], rhs=k_tm[:, tt, g * 512:(g + 1) * 512],
                    start=(tt == 0), stop=(tt == TT - 1)), "spe" if tt == TT - 1 else None)
            P.wait("act", tks)
            op("act", lambda E, g=g, pm=pm: E.copy(out=self.rowsc.ap()[0:1, g * 512:(g + 1) * 512], in_=pm.ap()[0:1, 0:512]), mark=(g == 1))
        t_ks = P.tok("sact")
        P.wait("sp", t_ks)
        tkd = dma("sp", self.ksumdr.unsqueeze(0), self.rowsc.ap()[0:1, 0:1024])
        P.wait("sp", tkd)
        P.wait("sp", P.tok("sdve"))
        tkl = dma("sp", self.vkTf.ap().rearrange("p (a b) -> p a b", a=8, b=65)[:, :, 64:65],
                  self.ksumdr.rearrange("(a p) -> p a", p=128).unsqueeze(2), allow_slow_non_contiguous=True)
        P.wait("sp", tkl)
        tvw = dma("sp", self.cc_vk_i, self.vkTf.ap())
        P.wait("gp", tvw)
        tcc = P.emit("gp", lambda E: E.collective_compute(
            "AllReduce", ALU.add, replica_groups=self.rg, ins=[self.cc_vk_i], outs=[self.cc_vk_o]), "scc", 1)
        self.ph_transpose_all(q_tm, q_fm, tq_r)
        P.wait("sp", tcc)
        tvr = dma("sp", self.vkTf.ap(), self.cc_vk_o)
        P.wait("dve", tvr)
        op("dve", lambda E: E.tensor_copy(out=self.vkTb.ap(), in_=self.vkTf.ap()), mark=True)
        d_vk = self.ddump("vkT", [128, 520])
        if d_vk is not None:
            P.wait("sp", P.tok("sdve"))
            dma("sp", d_vk, self.vkTf.ap())
        self.barrier()

        vkTb3 = self.vkTb.ap().rearrange("p (a b) -> p a b", a=8, b=65)
        st_ = self.statc.ap()
        for tt in range(TT):
            for pg in range(4):
                par, blk = pg % 2, pg // 2
                heads = [par + 2 * j + 8 * blk for j in range(4)]
                hb = par * 64
                pb = self.PB[pg]
                self.bank_wait(("na", pg))
                tmm = None
                for j, h in enumerate(heads):
                    tmm = P.emit("pe", lambda E, tt=tt, h=h, j=j, pb=pb, hb=hb: E.matmul(
                        out=pb.ap()[:, j * 65:j * 65 + 65],
                        lhsT=q_fm[hb:hb + 64, h // 2, tt * 128:(tt + 1) * 128],
                        rhs=vkTb3[hb:hb + 64, h // 2, :],
                        start=(j == 0), stop=(j == 3)), "spe" if j == 3 else None)
                P.wait("dve", tmm)
                p3 = pb.ap()[:, 0:260].rearrange("p (a c) -> p a c", c=65)
                op("dve", lambda E, p3=p3, pg=pg: E.tensor_scalar(
                    out=st_[:, 16 + 4 * pg:20 + 4 * pg], in0=p3[:, :, 64],
                    scalar1=self.cpool.ap()[:, 2:3], scalar2=None, op0=ALU.add))
                op("dve", lambda E, pg=pg: E.reciprocal(
                    out=st_[:, 16 + 4 * pg:20 + 4 * pg], in_=st_[:, 16 + 4 * pg:20 + 4 * pg]))
                dst3 = attn_tm[:, tt, :].rearrange("p (a c) -> p a c", c=128)[
                    :, 4 * blk:4 * blk + 4, par * 64:par * 64 + 64]
                rrep = st_[:, 16 + 4 * pg:20 + 4 * pg].unsqueeze(2).broadcast_to([128, 4, 64])
                tev = op("dve", lambda E, p3=p3, dst3=dst3, rrep=rrep: E.tensor_tensor(
                    out=dst3, in0=p3[:, :, 0:64], in1=rrep, op=ALU.mult), mark=True)
                self.bank_done(("na", pg), tev)
        self.barrier()
        d_at = self.ddump("attn_tm", [T, D], BF16)
        if d_at is not None:
            for tt in range(TT):
                dma("sp", d_at[tt * 128:(tt + 1) * 128, :], attn_tm[:, tt, :])

        self.ph_transpose_all(attn_tm, attn_fm)
        self.barrier()
        two = [dma("gp", wbuf[:, fc, :], self.wvo[fc * 128:(fc + 1) * 128, 1024:2048]) for fc in range(FC)]
        xstok = {}
        for tt in range(TT):
            bi = tt % 2
            xv = self.XS.ap() if bi == 0 else self.H0.ap()[:, 0:1024]
            lp = xstok.get(bi)
            if lp is not None:
                P.wait("sp", lp)
            tl = dma("sp", xv, self.x_in[tt * 128:(tt + 1) * 128, :])
            P.waits("pe", [two[-1]])
            tmm = None
            for g in range(2):
                pb = self.PB[2 * bi + g]
                self.bank_wait(2 * bi + g)
                for fc in range(FC):
                    tmm = P.emit("pe", lambda E, tt=tt, g=g, fc=fc, pb=pb: E.matmul(
                        out=pb.ap(), lhsT=attn_fm[:, fc, tt * 128:(tt + 1) * 128],
                        rhs=wbuf[:, fc, g * 512:(g + 1) * 512],
                        start=(fc == 0), stop=(fc == FC - 1)), "spe" if (g == 1 and fc == FC - 1) else None)
            P.waits("dve", [tmm, tl])
            op("dve", lambda E, bi=bi, xv=xv: E.tensor_tensor(out=xv[:, 0:512], in0=self.PB[2 * bi].ap(), in1=xv[:, 0:512], op=ALU.add))
            tev = op("dve", lambda E, bi=bi, xv=xv: E.tensor_tensor(out=xv[:, 512:1024], in0=self.PB[2 * bi + 1].ap(), in1=xv[:, 512:1024], op=ALU.add), mark=True)
            self.bank_done(2 * bi, tev)
            self.bank_done(2 * bi + 1, tev)
            P.waits("sp", [tev])
            xstok[bi] = dma("sp", self.xadr[tt * 128:(tt + 1) * 128, :], xv)
        self.barrier()
        d_ao = self.ddump("x_attn", [T, D])
        if d_ao is not None:
            for tt in range(TT):
                tl2 = dma("sp", self.XS.ap(), self.xadr[tt * 128:(tt + 1) * 128, :])
                P.wait("sp", tl2)
                t5 = dma("sp", d_ao[tt * 128:(tt + 1) * 128, :], self.XS.ap())
                P.wait("sp", t5)
        if upto == "l0attn":
            return

        TL = T // 2
        TTL = TL // 128
        NGL = TL // 512
        w1c = self.G4.ap()
        wq2T = self.v3(self.G2, 32, 512)
        d_hq = self.ddump("hq", [128, 32 * 256], BF16)
        for th in range(2):
            hq_a = self.v3(self.G0, 16, TL)
            hq_b = self.v3(self.G3, 16, TL)
            mm_prev = {}
            tr_prev = {}
            ev_tok = {}
            tq8_prev = {}
            sp_prev = {}

            def prep_w1(hc):
                w1v = w1c[:, (hc % 2) * 1024:(hc % 2) * 1024 + 1024]
                mp = mm_prev.get(hc % 2)
                if th == 1:
                    if mp is not None:
                        P.wait("gp", mp)
                    ev_tok[hc % 2] = dma("gp", w1v, self.w1tdr[:, hc * 1024:(hc + 1) * 1024])
                    return
                wb = self.wst.ap()[:, (hc % 2) * 1024:(hc % 2) * 1024 + 1024]
                tp = tr_prev.get(hc % 2)
                if tp is not None:
                    P.wait("gp", tp)
                tw = dma("gp", wb, self.w1[hc * 128:(hc + 1) * 128, :])
                P.wait("pe", tw)
                if mp is not None:
                    P.wait("act", mp)
                sp = sp_prev.get(hc % 2)
                if sp is not None:
                    P.wait("act", sp)
                ev_tok[hc % 2] = self.transp8([wb[:, fc * 128:(fc + 1) * 128] for fc in range(FC)],
                                              w1v.rearrange("p (a b) -> p a b", b=128))
                tr_prev[hc % 2] = P.tok("spe")
                P.wait("sp", ev_tok[hc % 2])
                sp_prev[hc % 2] = dma("sp", self.w1tdr[:, hc * 1024:(hc + 1) * 1024], w1v)

            for hci in range(33):
                if hci < 32:
                    prep_w1(hci)
                if hci == 0:
                    continue
                hc = hci - 1
                w1v = w1c[:, (hc % 2) * 1024:(hc % 2) * 1024 + 1024]
                h8h = self.h8t.ap()[:, (hc % 2) * 1024:(hc % 2) * 1024 + TL]
                P.wait("pe", ev_tok[hc % 2])
                t_prev = None
                tre = None
                for g in range(NGL):
                    bi = (hc * NGL + g) % 4
                    pb = self.PB[bi]
                    self.bank_wait(bi)
                    for fc in range(FC):
                        t_prev = P.emit("pe", lambda E, th=th, g=g, fc=fc, pb=pb, w1v=w1v: E.matmul(
                            out=pb.ap(), lhsT=w1v[:, fc * 128:(fc + 1) * 128],
                            rhs=xn_fm[:, fc, th * TL + g * 512:th * TL + (g + 1) * 512],
                            start=(fc == 0), stop=(fc == FC - 1)), "spe" if fc == FC - 1 else None)
                    P.wait("act", t_prev)
                    if g == 0:
                        t8p = tq8_prev.get(hc % 2)
                        if t8p is not None:
                            P.wait("act", t8p)
                    tre = op("act", lambda E, pb=pb, hc=hc, g=g, h8h=h8h: E.activation(
                        out=h8h[:, g * 512:(g + 1) * 512], in_=pb.ap(), func=AF.Relu,
                        bias=self.bcols.ap()[:, hc:hc + 1]), mark=True)
                    self.bank_done(bi, tre)
                mm_prev[hc % 2] = t_prev
                P.wait("dve", tre)
                dst = hq_a if hc < 16 else hq_b
                tq8 = op("dve", lambda E, dst=dst, hcl=hc % 16, h8h=h8h: E.tensor_copy(
                    out=dst[:, hcl, :], in_=h8h), mark=True)
                tq8_prev[hc % 2] = tq8
            self.barrier()
            if d_hq is not None and th == 0:
                for hc in range(32):
                    dst = hq_a if hc < 16 else hq_b
                    dma("sp", d_hq[:, hc * 256:(hc + 1) * 256], dst[:, hc % 16, 0:256])
                self.barrier()
            xstok = {}
            ftok = {}
            for half in range(2):
                if th == 0:
                    h8prev = {}
                    trprev = {}
                    for oc in range(4):
                        orow = half * 512 + oc * 128
                        for icg in range(8):
                            ti = oc * 8 + icg
                            hb8 = self.h8t.ap()[:, (ti % 2) * 512:(ti % 2) * 512 + 512]
                            wb = self.wst.ap()[:, (ti % 2) * 512:(ti % 2) * 512 + 512]
                            t8p = h8prev.get(ti % 2)
                            if t8p is not None:
                                P.wait("gp", t8p)
                            tw = dma("gp", hb8, self.w2[orow:orow + 128, icg * 512:(icg + 1) * 512])
                            P.wait("dve", tw)
                            tp9 = trprev.get(ti % 2)
                            if tp9 is not None:
                                P.wait("dve", tp9)
                            tq8 = op("dve", lambda E, wb=wb, hb8=hb8: E.tensor_copy(out=wb, in_=hb8), mark=True)
                            h8prev[ti % 2] = tq8
                            P.wait("pe", tq8)
                            self.transp8([wb[:, s2 * 128:(s2 + 1) * 128] for s2 in range(4)],
                                         wq2T[:, icg * 4:(icg + 1) * 4, oc * 128:(oc + 1) * 128])
                            trprev[ti % 2] = P.tok("spe")
                    P.wait("sp", P.tok("sact"))
                    dma("sp", self.wq2dr[half * 128:(half + 1) * 128, :], self.G2.ap()[:, 0:16384])
                    self.barrier()
                else:
                    if half == 1:
                        P.wait("gp", P.tok("spe"))
                    trl = dma("gp", self.G2.ap()[:, 0:16384], self.wq2dr[half * 128:(half + 1) * 128, :])
                    P.wait("pe", trl)
                for ttl in range(TTL):
                    tt = th * TTL + ttl
                    bi = ttl % 4
                    pb = self.PB[bi]
                    self.bank_wait(bi)
                    P.emit("pe", lambda E, pb=pb, half=half: E.matmul(
                        out=pb.ap(), lhsT=self.ones16.ap(), rhs=self.rowb2.ap()[0:1, half * 512:(half + 1) * 512],
                        start=True, stop=False), None)
                    tmm = None
                    for hc in range(32):
                        dst = hq_a if hc < 16 else hq_b
                        tmm = P.emit("pe", lambda E, ttl=ttl, hc=hc, pb=pb, dst=dst: E.matmul(
                            out=pb.ap(), lhsT=dst[:, hc % 16, ttl * 128:(ttl + 1) * 128],
                            rhs=wq2T[:, hc, :], start=False, stop=(hc == 31)), "spe" if hc == 31 else None)
                    src_x = self.xadr if half == 0 else self.x1dr
                    dst_x = self.x1dr if half == 0 else self.xadr
                    xv = self.XS.ap() if ttl % 2 == 0 else self.H0.ap()[:, 0:1024]
                    fv = self.f16t.ap()[:, (ttl % 2) * 512:(ttl % 2) * 512 + 512]
                    lp = xstok.get(ttl % 2)
                    if lp is not None:
                        P.wait("gp", lp)
                    tl = dma("gp", xv, src_x[tt * 128:(tt + 1) * 128, :])
                    P.wait("act", tmm)
                    fp = ftok.get(ttl % 2)
                    if fp is not None:
                        P.wait("act", fp)
                    tf = op("act", lambda E, pb=pb, fv=fv: E.copy(out=fv, in_=pb.ap()), mark=True)
                    self.bank_done(bi, tf)
                    P.waits("dve", [tf, tl])
                    tadd = op("dve", lambda E, half=half, xv=xv, fv=fv: E.tensor_tensor(
                        out=xv[:, half * 512:(half + 1) * 512], in0=fv,
                        in1=xv[:, half * 512:(half + 1) * 512], op=ALU.add), mark=True)
                    ftok[ttl % 2] = tadd
                    P.waits("sp", [tadd])
                    xstok[ttl % 2] = dma("sp", dst_x[tt * 128:(tt + 1) * 128, :], xv)
            self.barrier()
        d_x1 = self.ddump("x1", [T, D])
        if d_x1 is not None:
            for tt in range(TT):
                tl2 = dma("sp", self.XS.ap(), self.xadr[tt * 128:(tt + 1) * 128, :])
                P.wait("sp", tl2)
                t5 = dma("sp", d_x1[tt * 128:(tt + 1) * 128, :], self.XS.ap())
                P.wait("sp", t5)

    def build_l1(self, upto):
        P, op, dma = self.P, self.op, self.dma
        T, TT, TG = self.T, self.TT, self.TG
        NCH = T // 64
        xn1_tm = self.v3(self.G0, TT, 1024)
        xn1_fm = self.v3(self.G1, FC, T)
        xa_fm = self.v3(self.G2, FC, T)
        proj = self.v3(self.G3, 7, T)
        WT = self.v3(self.G4, FC, 896)
        qfm = self.v3(self.G0, 4, T)
        ktil = self.v3(self.G0, 4, T, off=4 * T)
        vtm = self.v3(self.G2, TT, 1024)
        khtm = self.v3(self.G4, 16, 512)
        ofm = self.v3(self.G3, FC, T)
        Qc = self.G3.ap()[0:64, 7 * T:7 * T + 512]
        Vc = self.G3.ap()[0:64, 7 * T + 512:7 * T + 1536]
        st = self.statc.ap()
        H0a = self.H0.ap()[:, 0:T]
        H0b = self.H0.ap()[:, 2048:2048 + T]

        op("dve", lambda E: E.memset(st[:, 30:31], 0.0))
        op("dve", lambda E: E.memset(self.Dsb.ap(), 0.0))
        op("dve", lambda E: E.memset(self.Esb.ap(), 0.0))
        op("dve", lambda E: E.memset(self.Msb.ap(), 0.0))
        op("dve", lambda E: E.memset(self.Csb.ap(), 0.0))
        P.wait("dve", self.t_b2q)
        op("dve", lambda E: E.memset(self.rowsc.ap()[0:1, 0:128], 1.0))
        dma("gp", self.bcols.ap()[:, 72:88], self.rmaskin)
        twtld = [dma("gp", WT[:, fc, :], self.l1wt[fc * 128:(fc + 1) * 128, :]) for fc in range(FC)]
        lnt = {}
        for tt in range(TT):
            xv = self.XS.ap() if tt % 2 == 0 else self.H0.ap()[:, 0:1024]
            lp = lnt.get(tt % 2)
            if lp is not None:
                P.wait("sp", lp)
            tl = dma("sp", xv, self.xadr[tt * 128:(tt + 1) * 128, :])
            P.wait("dve", tl)
            op("dve", lambda E, xv=xv: E.bn_stats(out=st[:, 8:14], in_=xv[:, 0:512]))
            op("dve", lambda E, xv=xv: E.bn_stats(out=st[:, 14:20], in_=xv[:, 512:1024]))
            self.selfw("dve")
            t1 = op("dve", lambda E: E.bn_aggr(out=st[:, 20:22], in_=st[:, 8:20]))
            P.wait("act", t1)
            op("act", lambda E: E.activation(out=st[:, 22:23], in_=st[:, 21:22], func=AF.Ln, bias=self.cpool.ap()[:, 0:1]))
            self.selfw("act")
            t2 = op("act", lambda E: E.activation(out=st[:, 23:24], in_=st[:, 22:23], func=AF.Exp, scale=-0.5))
            P.wait("dve", t2)
            t3 = op("dve", lambda E, tt=tt, xv=xv: E.tensor_scalar(out=xn1_tm[:, tt, :], in0=xv,
                                                            scalar1=st[:, 20:21], scalar2=st[:, 23:24],
                                                            op0=ALU.subtract, op1=ALU.mult))
            lnt[tt % 2] = t3
            P.wait("dve", t3)
            op("dve", lambda E, tt=tt: E.tensor_reduce(out=st[:, 29:30], in_=xn1_tm[:, tt, :],
                                                       axis=mybir.AxisListType.X, op=ALU.max, apply_absolute_value=True))
            self.selfw("dve")
            tam = op("dve", lambda E: E.tensor_tensor(out=st[:, 30:31], in0=st[:, 30:31], in1=st[:, 29:30], op=ALU.max))
            op("dve", lambda E: E.tensor_tensor(out=st[:, 24:25], in0=st[:, 21:22], in1=st[:, 23:24], op=ALU.mult))
            self.selfw("dve")
            t4 = op("dve", lambda E: E.tensor_tensor(out=st[:, 24:25], in0=st[:, 24:25], in1=st[:, 23:24], op=ALU.mult))
            P.wait("act", t4)
            op("act", lambda E: E.activation(out=st[:, 25:26], in_=st[:, 24:25], func=AF.Ln, bias=self.cpool.ap()[:, 0:1]))
            self.selfw("act")
            tf = op("act", lambda E: E.activation(out=st[:, 25:26], in_=st[:, 25:26], func=AF.Exp, scale=-0.5))
            P.wait("gp", t3)
            tsp = dma("gp", self.xn1dr[tt * 128:(tt + 1) * 128, :], xn1_tm[:, tt, :])
            P.wait("pe", t3)
            self.transp8([xn1_tm[:, tt, fc * 128:(fc + 1) * 128] for fc in range(FC)],
                         xn1_fm[:, :, tt * 128:(tt + 1) * 128])
            t_tr = P.tok("sact")
            P.waits("dve", [tsp, t_tr, tf])
            txa = op("dve", lambda E, tt=tt: E.tensor_scalar(out=xn1_tm[:, tt, :], in0=xn1_tm[:, tt, :],
                                                             scalar1=st[:, 25:26], scalar2=None, op0=ALU.mult))
            P.wait("pe", txa)
            self.transp8([xn1_tm[:, tt, fc * 128:(fc + 1) * 128] for fc in range(FC)],
                         xa_fm[:, :, tt * 128:(tt + 1) * 128])
            P.wait("act", tam)
        self.barrier()

        tmm = P.emit("pe", lambda E: E.transpose(out=self.PM.ap()[0:1, 0:128], in_=st[:, 30:31], identity=self.identf.ap()), "spe", 1)
        P.wait("act", tmm)
        t5 = op("act", lambda E: E.copy(out=self.H0.ap()[0:1, 0:128], in_=self.PM.ap()[0:1, 0:128]))
        P.wait("dve", t5)
        op("dve", lambda E: E.memset(self.H0.ap()[0:1, 128:132], 0.0))
        self.selfw("dve")
        t6 = op("dve", lambda E: E.tensor_reduce(out=self.H0.ap()[0:1, 128:129], in_=self.H0.ap()[0:1, 0:128],
                                                 axis=mybir.AxisListType.X, op=ALU.max))
        P.wait("sp", t6)
        t7 = dma("sp", self.cc_a_i[0:1, 0:4], self.H0.ap()[0:1, 128:132])
        P.wait("gp", t7)
        tcc = P.emit("gp", lambda E: E.collective_compute("AllReduce", ALU.max, replica_groups=self.rg,
                                                          ins=[self.cc_a_i], outs=[self.cc_a_o]), "scc", 1)

        twt = []
        for fc in range(FC):
            P.wait("dve", twtld[fc])
            tg = op("dve", lambda E, fc=fc: E.tensor_scalar(out=WT[:, fc, :], in0=WT[:, fc, :],
                                                            scalar1=self.bcols.ap()[:, 40 + fc:41 + fc], scalar2=None, op0=ALU.mult))
            twt.append(tg)
        tbg = op("dve", lambda E: E.tensor_copy(out=self.wst.ap()[:, 0:8], in_=self.bcols.ap()[:, 56:64]))
        P.waits("pe", twt + [tbg, t5])
        for g in range(2):
            pm = self.PM if g == 0 else self.PB[4]
            nn = 512 if g == 0 else 384
            tmm = None
            for fc in range(FC):
                tmm = P.emit("pe", lambda E, g=g, fc=fc, pm=pm, nn=nn: E.matmul(
                    out=pm.ap()[0:1, 0:nn], lhsT=self.wst.ap()[:, fc:fc + 1], rhs=WT[:, fc, g * 512:g * 512 + nn],
                    start=(fc == 0), stop=(fc == FC - 1)), "spe" if fc == FC - 1 else None)
            P.wait("act", tmm)
            op("act", lambda E, g=g, pm=pm, nn=nn: E.copy(out=self.H0.ap()[0:1, 1024 + g * 512:1024 + g * 512 + nn], in_=pm.ap()[0:1, 0:nn]), mark=(g == 1))
        tb = P.tok("sact")
        P.wait("sp", tb)
        tbd = dma("sp", self.betadr.unsqueeze(0), self.H0.ap()[0:1, 1024:1920])
        P.wait("gp", tbd)
        tbl = dma("gp", self.bcols.ap()[:, 64:71], self.betadr.rearrange("(c p) -> p c", p=128), allow_slow_non_contiguous=True)
        twab = dma("gp", st[:, 0:4], self.l1wab.rearrange("(c p) -> p c", p=128), allow_slow_non_contiguous=True)
        P.waits("dve", [tbl, twab])
        tbl = op("dve", lambda E: E.tensor_tensor(out=self.bcols.ap()[:, 67:71], in0=self.bcols.ap()[:, 67:71], in1=st[:, 0:4], op=ALU.add))

        P.wait("sp", tcc)
        t8 = dma("sp", self.H0.ap()[0:1, 130:134], self.cc_a_o)
        P.waits("pe", [t8, P.tok("sdve"), P.tok("sact")])
        tmm = P.emit("pe", lambda E: E.matmul(out=self.PM.ap()[:, 0:1], lhsT=self.rowsc.ap()[0:1, 0:128],
                                              rhs=self.H0.ap()[0:1, 130:131], start=True, stop=True), "spe", 1)
        P.wait("dve", tmm)
        op("dve", lambda E: E.tensor_copy(out=st[:, 26:27], in_=self.PM.ap()[:, 0:1]))
        self.selfw("dve")
        op("dve", lambda E: E.tensor_scalar(out=st[:, 27:28], in0=st[:, 26:27], scalar1=float(1.0 / 32767.0),
                                            scalar2=self.cpool.ap()[:, 4:5], op0=ALU.mult, op1=ALU.max))
        self.selfw("dve")
        t9 = op("dve", lambda E: E.reciprocal(out=st[:, 28:29], in_=st[:, 27:28]))
        P.wait("pe", t9)

        for fc in range(FC):
            ta = op("dve", lambda E, fc=fc: E.tensor_scalar(out=H0a, in0=xn1_fm[:, fc, :], scalar1=st[:, 28:29],
                                                            scalar2=MAGIC, op0=ALU.mult, op1=ALU.add))
            P.wait("dve", ta)
            tb = op("dve", lambda E, fc=fc: E.tensor_scalar(out=xn1_fm[:, fc, :], in0=H0a, scalar1=MAGIC,
                                                            scalar2=st[:, 27:28], op0=ALU.subtract, op1=ALU.mult))
            P.wait("dve", tb)
        xq_fm = xn1_fm
        self.barrier()
        if upto == "l1ln":
            return

        P.waits("pe", [tbl])
        for oc in range(7):
            for g in range(TG):
                pb = self.PB[(oc * TG + g) % 4]
                tmm = None
                for fc in range(FC):
                    tmm = P.emit("pe", lambda E, oc=oc, g=g, fc=fc, pb=pb: E.matmul(
                        out=pb.ap(), lhsT=WT[:, fc, oc * 128:(oc + 1) * 128],
                        rhs=xa_fm[:, fc, g * 512:(g + 1) * 512],
                        start=(fc == 0), stop=(fc == FC - 1)), "spe" if fc == FC - 1 else None)
                P.wait("dve", tmm)
                tev = op("dve", lambda E, oc=oc, g=g, pb=pb: E.tensor_scalar(
                    out=proj[:, oc, g * 512:(g + 1) * 512], in0=pb.ap(),
                    scalar1=self.bcols.ap()[:, 64 + oc:65 + oc], scalar2=None, op0=ALU.add))
                P.wait("pe", tev)
        self.barrier()

        if upto == "l1proj":
            return
        tqc = dma("gp", Qc, self.l1qc)
        tvc = dma("gp", Vc, self.l1vc)

        P.waits("pe", [tqc, tvc])
        for h in range(8):
            hb = (h % 2) * 64
            for g in range(TG):
                pb = self.PB[g % 4]
                tmm = P.emit("pe", lambda E, h=h, g=g, pb=pb, hb=hb: E.matmul(
                    out=pb.ap()[hb:hb + 64, :], lhsT=Qc[:, h * 64:(h + 1) * 64],
                    rhs=proj[0:64, 0, g * 512:(g + 1) * 512], start=True, stop=True), "spe", 1)
                P.wait("dve", tmm)
                tev = op("dve", lambda E, h=h, g=g, pb=pb, hb=hb: E.tensor_copy(
                    out=qfm[hb:hb + 64, h // 2, g * 512:(g + 1) * 512], in_=pb.ap()[hb:hb + 64, :]))
                P.wait("pe", tev)
        for h in range(8):
            for tt in range(TT):
                pb = self.PB[tt % 4]
                tmm = P.emit("pe", lambda E, h=h, tt=tt, pb=pb: E.matmul(
                    out=pb.ap()[:, 0:128], lhsT=proj[0:64, 1, tt * 128:(tt + 1) * 128],
                    rhs=Vc[:, h * 128:(h + 1) * 128], start=True, stop=True), "spe", 1)
                P.wait("dve", tmm)
                tev = op("dve", lambda E, h=h, tt=tt, pb=pb: E.tensor_copy(
                    out=vtm[:, tt, h * 128:(h + 1) * 128], in_=pb.ap()[:, 0:128]))
                P.wait("pe", tev)
        op("act", lambda E: E.activation(out=self.H0.ap()[0:64, 0:T], in_=proj[0:64, 2, :], func=AF.Square))
        tk2 = op("act", lambda E: E.activation(out=self.H0.ap()[64:128, 0:T], in_=proj[64:128, 0, :], func=AF.Square))
        P.wait("pe", tk2)
        for g in range(TG):
            pb = self.PB[g % 4]
            tmm = P.emit("pe", lambda E, g=g, pb=pb: E.matmul(
                out=pb.ap()[0:1, :], lhsT=self.onesf.ap()[0:64, 0:1], rhs=self.H0.ap()[0:64, g * 512:(g + 1) * 512],
                start=True, stop=True), "spe", 1)
            P.wait("act", tmm)
            op("act", lambda E, g=g, pb=pb: E.copy(out=self.H0.ap()[0:1, 2048 + g * 512:2048 + (g + 1) * 512], in_=pb.ap()[0:1, :]), mark=(g == TG - 1))
        tss = P.tok("sact")
        P.wait("dve", tss)
        tq1 = op("dve", lambda E: E.tensor_scalar(out=self.H0.ap()[0:1, 2048:2048 + T], in0=self.H0.ap()[0:1, 2048:2048 + T],
                                                  scalar1=1e-24, scalar2=None, op0=ALU.max))
        P.wait("act", tq1)
        op("act", lambda E: E.activation(out=self.H0.ap()[0:1, 2048:2048 + T], in_=self.H0.ap()[0:1, 2048:2048 + T], func=AF.Ln))
        self.selfw("act")
        trn = op("act", lambda E: E.activation(out=self.H0.ap()[0:1, 2048:2048 + T], in_=self.H0.ap()[0:1, 2048:2048 + T], func=AF.Exp, scale=-0.5))
        P.wait("pe", trn)
        for g in range(TG):
            tmm = P.emit("pe", lambda E, g=g: E.matmul(
                out=self.PM.ap(), lhsT=self.rowsc.ap()[0:1, 0:128], rhs=self.H0.ap()[0:1, 2048 + g * 512:2048 + (g + 1) * 512],
                start=True, stop=True), "spe", 1)
            P.wait("dve", tmm)
            op("dve", lambda E, g=g: E.tensor_tensor(out=self.kndup.ap()[0:64, g * 512:(g + 1) * 512],
                                                     in0=proj[0:64, 2, g * 512:(g + 1) * 512], in1=self.PM.ap()[0:64, :], op=ALU.mult))
            self.selfw("dve")
            tev = op("dve", lambda E, g=g: E.tensor_tensor(out=self.kndup.ap()[64:128, g * 512:(g + 1) * 512],
                                                           in0=proj[64:128, 0, g * 512:(g + 1) * 512], in1=self.PM.ap()[64:128, :], op=ALU.mult))
            P.wait("pe", tev)
        self.barrier()
        d_kn = self.ddump("kndup", [128, T], BF16)
        if d_kn is not None:
            dma("sp", d_kn, self.kndup.ap()[:, 0:T])

        if upto == "l1heads":
            return
        for j in range(4):
            zj = proj[:, 3 + j, :]
            taz = op("act", lambda E, zj=zj: E.activation(out=H0a, in_=zj, func=AF.Abs))
            P.wait("act", taz)
            te1 = op("act", lambda E: E.activation(out=H0b, in_=H0a, func=AF.Exp, scale=-1.0))
            P.wait("act", te1)
            tl1 = op("act", lambda E: E.activation(out=H0b, in_=H0b, func=AF.Ln, bias=self.cpool.ap()[:, 1:2]))
            P.waits("dve", [taz, tl1])
            tlg = op("dve", lambda E, zj=zj: E.scalar_tensor_tensor(
                out=H0a, in0=zj, scalar=0.0, in1=H0b, op0=ALU.min, op1=ALU.subtract))
            P.wait("dve", tlg)
            tR = None
            for cc in range(NCH):
                tR = op("dve", lambda E, cc=cc: E.tensor_tensor_scan(
                    out=H0b[:, cc * 64:(cc + 1) * 64], data0=H0a[:, cc * 64:(cc + 1) * 64],
                    data1=H0a[:, cc * 64:(cc + 1) * 64], initial=0.0, op0=ALU.add, op1=ALU.bypass))
            P.waits("act", [tR])
            lastv = H0b.rearrange("p (c s) -> p c s", s=64)[:, :, 63]
            midv = H0b.rearrange("p (c s) -> p c s", s=64)[:, :, 31]
            tD = op("act", lambda E, j=j, lastv=lastv: E.activation(out=self.Dsb.ap()[:, j * 32:j * 32 + NCH], in_=lastv, func=AF.Exp))
            P.waits("dve", [tD])
            tRc = op("dve", lambda E, lastv=lastv, NCH=NCH: E.tensor_tensor_scan(
                out=st[:, 0:NCH], data0=lastv, data1=lastv, initial=0.0, op0=ALU.add, op1=ALU.bypass))
            tRc2 = op("dve", lambda E: E.tensor_scalar(out=st[:, 0:NCH], in0=st[:, 0:NCH], scalar1=-80.0, scalar2=None, op0=ALU.max))
            P.wait("act", tRc2)
            tC = op("act", lambda E, j=j: E.activation(out=self.Csb.ap()[:, j * 32:j * 32 + NCH], in_=st[:, 0:NCH], func=AF.Exp))
            P.wait("dve", tC)
            t_e = op("dve", lambda E, lastv=lastv, midv=midv: E.tensor_tensor(out=st[:, 0:NCH], in0=lastv, in1=midv, op=ALU.subtract))
            P.wait("act", t_e)
            op("act", lambda E, j=j: E.activation(out=self.Esb.ap()[:, j * 32:j * 32 + NCH], in_=st[:, 0:NCH], func=AF.Exp, bias=self.cpool.ap()[:, 6:7]))
            tM = op("act", lambda E, j=j, midv=midv: E.activation(out=self.Msb.ap()[:, j * 32:j * 32 + NCH], in_=midv, func=AF.Exp, bias=self.cpool.ap()[:, 6:7]))
            H0a3 = H0a.rearrange("p (c s) -> p c s", s=64)
            H0b3 = H0b.rearrange("p (c s) -> p c s", s=64)
            mrep = midv.unsqueeze(2).broadcast_to([128, NCH, 64])
            op("dve", lambda E, H0a3=H0a3, H0b3=H0b3, mrep=mrep: E.tensor_tensor(
                out=H0a3, in0=H0b3, in1=mrep, op=ALU.subtract))
            tclamp = op("dve", lambda E: E.tensor_scalar(out=H0a, in0=H0a, scalar1=66.0, scalar2=-66.0, op0=ALU.min, op1=ALU.max))
            P.wait("act", tclamp)
            texp = op("act", lambda E: E.activation(out=H0b, in_=H0a, func=AF.Exp, bias=self.cpool.ap()[:, 7:8]))
            P.wait("dve", texp)
            tqt = op("dve", lambda E, j=j: E.tensor_tensor(out=qfm[:, j, :], in0=qfm[:, j, :], in1=H0b, op=ALU.mult))
            P.wait("act", tqt)
            tem = op("act", lambda E: E.activation(out=H0b, in_=H0a, func=AF.Exp, scale=-1.0, bias=self.cpool.ap()[:, 7:8]))
            P.wait("dve", tem)
            tkt = op("dve", lambda E, j=j: E.tensor_tensor(out=ktil[:, j, :], in0=self.kndup.ap()[:, 0:T], in1=H0b, op=ALU.mult))
            pj3 = proj[:, 3 + j, :].rearrange("p (c s) -> p c s", s=64)
            kt3 = ktil[:, j, :].rearrange("p (c s) -> p c s", s=64)
            erep = self.Esb.ap()[:, j * 32:j * 32 + NCH].unsqueeze(2).broadcast_to([128, NCH, 64])
            op("dve", lambda E, pj3=pj3, kt3=kt3, erep=erep: E.tensor_tensor(
                out=pj3, in0=kt3, in1=erep, op=ALU.mult))
            tkh = P.tok("sdve")
            P.waits("act", [tkh])
            P.waits("pe", [tkh])
        self.barrier()

        d_qt = self.ddump("qt", [128, 4 * T], BF16)
        if d_qt is not None:
            dma("sp", d_qt, self.G0.ap()[:, 0:4 * T])
        d_kt = self.ddump("kt", [128, 4 * T], BF16)
        if d_kt is not None:
            dma("sp", d_kt, self.G0.ap()[:, 4 * T:8 * T])
        d_dec = self.ddump("dec", [128, 5 * 128])
        if d_dec is not None:
            dma("sp", d_dec[:, 0:128], self.Dsb.ap())
            dma("sp", d_dec[:, 128:256], self.Esb.ap())
            dma("sp", d_dec[:, 256:384], self.Msb.ap())
            dma("sp", d_dec[:, 384:512], self.Csb.ap())
        d_vt = self.ddump("vt", [128, TT * 1024], BF16)
        if d_vt is not None:
            dma("sp", d_vt, self.G2.ap()[:, 0:TT * 1024])
        if upto == "l1gate":
            return
        for tt in range(TT):
            self.transp8([proj[:, 3 + j, tt * 128:(tt + 1) * 128] for j in range(4)],
                         khtm[:, tt, 0:512].rearrange("p (a b) -> p a b", b=128))
        self.barrier()
        if upto == "l1khtm":
            return

        def state_pass(collect_o, nch=None):
            stt_last = None
            ti1_last = None
            intra_last = {}
            comb_last = None
            mask_tok = {}
            for cc in range(NCH if nch is None else nch):
                b2 = (cc % 2) * 64
                par = cc % 2
                tc = cc // 2
                t_sb16 = None
                if collect_o:
                    if ti1_last is not None:
                        P.wait("dve", ti1_last)
                    for j9 in range(4):
                        t_sb16 = op("dve", lambda E, j9=j9, cc=cc: E.tensor_scalar(
                            out=self.wst.ap()[:, 512 + j9 * 128:512 + (j9 + 1) * 128],
                            in0=self.Ssb.ap()[:, j9 * 128:(j9 + 1) * 128],
                            scalar1=self.Msb.ap()[:, j9 * 32 + cc:j9 * 32 + cc + 1], scalar2=None, op0=ALU.mult))
                if stt_last is not None:
                    P.wait("pe", stt_last)
                tU = None
                for par2 in range(2):
                    for j2 in range(4):
                        h = 2 * j2 + par2
                        tU = P.emit("pe", lambda E, h=h, par2=par2, j2=j2, b2=b2, tc=tc: E.matmul(
                            out=self.PM.ap()[par2 * 64:par2 * 64 + 64, j2 * 128:(j2 + 1) * 128],
                            lhsT=khtm[b2:b2 + 64, tc, h * 64:(h + 1) * 64],
                            rhs=vtm[b2:b2 + 64, tc, h * 128:(h + 1) * 128],
                            start=(j2 == 0), stop=(j2 == 3)), "spe" if j2 == 3 else None)
                if collect_o:
                    tsce = {}
                    for par2 in range(2):
                        hb = par2 * 64
                        bankS = self.PB[4 - par2]
                        mt = mask_tok.get((par, par2))
                        if mt is not None:
                            P.wait("pe", mt)
                        tt2 = None
                        for j2 in range(4):
                            tt2 = P.emit("pe", lambda E, j2=j2, hb=hb, b2=b2, cc=cc, bankS=bankS: E.matmul(
                                out=bankS.ap()[b2:b2 + 64, j2 * 64:(j2 + 1) * 64],
                                lhsT=ktil[hb:hb + 64, j2, cc * 64:(cc + 1) * 64],
                                rhs=qfm[hb:hb + 64, j2, cc * 64:(cc + 1) * 64],
                                start=(j2 == 0), stop=(j2 == 3)), "spe" if j2 == 3 else None)
                        tsce[par2] = tt2
                    tmsk = {}
                    for par2 in range(2):
                        bankS = self.PB[4 - par2]
                        P.wait("dve", tsce[par2])
                        it = intra_last.get(par)
                        if it is not None and par2 == 0:
                            P.wait("dve", it)
                        wsl = self.wst.ap()[b2:b2 + 64, par2 * 256:par2 * 256 + 256].rearrange(
                            "p (a c) -> p a c", c=64)
                        p4v = bankS.ap()[b2:b2 + 64, 0:256].rearrange("p (a c) -> p a c", c=64)
                        mv = self.maskt.ap()[b2:b2 + 64, :].unsqueeze(1).broadcast_to([64, 4, 64])
                        tmsk[par2] = op("dve", lambda E, wsl=wsl, p4v=p4v, mv=mv: E.tensor_tensor(
                            out=wsl, in0=p4v, in1=mv, op=ALU.mult))
                        mask_tok[(par, par2)] = tmsk[par2]
                    P.wait("dve", tU)
                    for j2 in range(4):
                        stt_last = op("dve", lambda E, j2=j2, cc=cc: E.scalar_tensor_tensor(
                            out=self.Ssb.ap()[:, j2 * 128:(j2 + 1) * 128],
                            in0=self.Ssb.ap()[:, j2 * 128:(j2 + 1) * 128],
                            scalar=self.Dsb.ap()[:, j2 * 32 + cc:j2 * 32 + cc + 1],
                            in1=self.PM.ap()[:, j2 * 128:(j2 + 1) * 128], op0=ALU.mult, op1=ALU.add))
                    P.wait("pe", t_sb16)
                    if comb_last is not None:
                        P.wait("pe", comb_last)
                    ti_o = {}
                    for par2 in range(2):
                        hb = par2 * 64
                        bankA = self.PB[par2]
                        tx2 = None
                        for j2 in range(4):
                            tx2 = P.emit("pe", lambda E, j2=j2, hb=hb, cc=cc, bankA=bankA: E.matmul(
                                out=bankA.ap()[:, j2 * 64:(j2 + 1) * 64],
                                lhsT=self.wst.ap()[hb:hb + 64, 512 + j2 * 128:512 + (j2 + 1) * 128],
                                rhs=qfm[hb:hb + 64, j2, cc * 64:(cc + 1) * 64],
                                start=(j2 == 0), stop=(j2 == 3)), "spe" if j2 == 3 else None)
                        ti_o[par2] = tx2
                    ti1_last = ti_o[1]
                    P.waits("pe", [tmsk[0], tmsk[1]])
                    tin = None
                    for h in range(8):
                        woff = (h % 2) * 256 + (h // 2) * 64
                        tin = P.emit("pe", lambda E, h=h, b2=b2, tc=tc, woff=woff: E.matmul(
                            out=self.PB[2].ap()[:, h * 64:(h + 1) * 64],
                            lhsT=vtm[b2:b2 + 64, tc, h * 128:(h + 1) * 128],
                            rhs=self.wst.ap()[b2:b2 + 64, woff:woff + 64],
                            start=(h == 0), stop=(h == 7)), "spe" if h == 7 else None)
                    intra_last[par] = tin
                    of4 = ofm[:, :, cc * 64:(cc + 1) * 64].rearrange("p (a b) c -> p a b c", b=2)
                    pb2v = self.PB[2].ap()[:, 0:512].rearrange("p (a b c) -> p a b c", b=2, c=64)
                    P.wait("act", ti_o[0])
                    ta0 = op("act", lambda E, of4=of4: E.copy(
                        out=of4[:, :, 0, :], in_=self.PB[0].ap()[:, 0:256].rearrange("p (a c) -> p a c", c=64)))
                    P.wait("act", ti_o[1])
                    ta1 = op("act", lambda E, of4=of4: E.copy(
                        out=of4[:, :, 1, :], in_=self.PB[1].ap()[:, 0:256].rearrange("p (a c) -> p a c", c=64)))
                    P.waits("dve", [ta0, tin])
                    op("dve", lambda E, of4=of4, pb2v=pb2v: E.scalar_tensor_tensor(
                        out=of4[:, :, 0, :], in0=pb2v[:, :, 0, :], scalar=1.0,
                        in1=of4[:, :, 0, :], op0=ALU.mult, op1=ALU.add))
                    P.wait("dve", ta1)
                    comb_last = op("dve", lambda E, of4=of4, pb2v=pb2v: E.scalar_tensor_tensor(
                        out=of4[:, :, 1, :], in0=pb2v[:, :, 1, :], scalar=1.0,
                        in1=of4[:, :, 1, :], op0=ALU.mult, op1=ALU.add))
                if not collect_o:
                    P.wait("dve", tU)
                    for j2 in range(4):
                        stt_last = op("dve", lambda E, j2=j2, cc=cc: E.scalar_tensor_tensor(
                            out=self.Ssb.ap()[:, j2 * 128:(j2 + 1) * 128],
                            in0=self.Ssb.ap()[:, j2 * 128:(j2 + 1) * 128],
                            scalar=self.Dsb.ap()[:, j2 * 32 + cc:j2 * 32 + cc + 1],
                            in1=self.PM.ap()[:, j2 * 128:(j2 + 1) * 128], op0=ALU.mult, op1=ALU.add))
            P.waits("dve", [stt_last])
            P.waits("pe", [P.tok("sdve")])

        if upto == "l1u":
            for cc in range(NCH):
                b2 = (cc % 2) * 64
                tc = cc // 2
                for j2 in range(4):
                    pmU = self.PM.ap()[:, (j2 % 2) * 128:(j2 % 2) * 128 + 128]
                    tU = None
                    for hh in range(2):
                        h = j2 * 2 + hh
                        hb = (h % 2) * 64
                        tU = P.emit("pe", lambda E, h=h, hb=hb, b2=b2, tc=tc, pmU=pmU: E.matmul(
                            out=pmU[hb:hb + 64, :], lhsT=khtm[b2:b2 + 64, tc, h * 64:(h + 1) * 64],
                            rhs=vtm[b2:b2 + 64, tc, h * 128:(h + 1) * 128], start=True, stop=True), "spe", 1)
                    P.wait("dve", tU)
                    tcp = op("dve", lambda E, j2=j2, cc=cc, pmU=pmU: E.scalar_tensor_tensor(
                        out=self.Ssb.ap()[:, j2 * 128:(j2 + 1) * 128], in0=self.Ssb.ap()[:, j2 * 128:(j2 + 1) * 128],
                        scalar=self.Dsb.ap()[:, j2 * 32 + cc:j2 * 32 + cc + 1],
                        in1=pmU, op0=ALU.mult, op1=ALU.add))
                    P.wait("pe", tcp)
            self.barrier()
            return
        tz = op("dve", lambda E: E.memset(self.Ssb.ap(), 0.0))
        P.wait("pe", tz)
        if upto.startswith("l1sp"):
            state_pass(collect_o=(upto == "l1spo"), nch=4)
            self.barrier()
            return
        state_pass(collect_o=False)
        self.barrier()
        if upto == "l1scan0":
            return

        d_m = self.ddump("mcore", [128, 512])
        if d_m is not None:
            dma("sp", d_m, self.Ssb.ap())
        t1 = dma("sp", self.cc_g_i[:, 0:512], self.Ssb.ap())
        t2 = dma("sp", self.cc_g_i[:, 512:516], self.Csb.ap().rearrange("p (j c) -> p j c", c=32)[:, :, NCH - 1], allow_slow_non_contiguous=True)
        P.wait("gp", t2)
        tcc = P.emit("gp", lambda E: E.collective_compute("AllGather", ALU.bypass, replica_groups=self.rg,
                                                          ins=[self.cc_g_i], outs=[self.cc_g_o]), "scc", 1)
        P.wait("dve", tcc)
        P.wait("sp", tcc)
        tz = op("dve", lambda E: E.memset(self.Ssb.ap(), 0.0))
        nco = len(self.rg[0])
        for r in range(nco):
            tgl = dma("sp", self.Gsb.ap(), self.cc_g_o[r * 128:(r + 1) * 128, :])
            P.waits("dve", [tgl, tz])
            tdp = op("dve", lambda E, r=r: E.tensor_scalar(out=st[:, 0:4], in0=self.Gsb.ap()[:, 512:516],
                                                           scalar1=self.bcols.ap()[:, 72 + r:73 + r],
                                                           scalar2=self.bcols.ap()[:, 80 + r:81 + r],
                                                           op0=ALU.mult, op1=ALU.add))
            tmp = op("dve", lambda E, r=r: E.tensor_scalar(out=self.Gsb.ap()[:, 0:512], in0=self.Gsb.ap()[:, 0:512],
                                                           scalar1=self.bcols.ap()[:, 72 + r:73 + r], scalar2=None, op0=ALU.mult))
            P.waits("dve", [tdp, tmp])
            tz = None
            for j2 in range(4):
                tz = op("dve", lambda E, j2=j2: E.scalar_tensor_tensor(
                    out=self.Ssb.ap()[:, j2 * 128:(j2 + 1) * 128], in0=self.Ssb.ap()[:, j2 * 128:(j2 + 1) * 128],
                    scalar=st[:, j2:j2 + 1], in1=self.Gsb.ap()[:, j2 * 128:(j2 + 1) * 128], op0=ALU.mult, op1=ALU.add))
                P.wait("dve", tz)
            P.wait("sp", tz)
        self.barrier()

        if upto == "l1scan1":
            return
        state_pass(collect_o=True)
        self.barrier()
        d_gla = self.ddump("gla", [T, D], BF16)
        if d_gla is not None:
            for ocx in range(8):
                dma("sp", d_gla[ocx * 128:(ocx + 1) * 128, 0:1024] if T >= 1024 else d_gla[ocx * 128:(ocx + 1) * 128, 0:T], ofm[:, ocx, 0:1024 if T >= 1024 else T])
            self.barrier()

        two = [dma("gp", self.v3(self.G4, FC, 1024)[:, fc, :], self.l1wo[fc * 128:(fc + 1) * 128, :]) for fc in range(FC)]
        wo1b = self.v3(self.G4, FC, 1024)
        xstok = {}
        for tt in range(TT):
            bi = tt % 2
            xv = self.XS.ap() if bi == 0 else self.H0.ap()[:, 0:1024]
            wv = self.wst.ap()[:, bi * 1024:bi * 1024 + 1024]
            lp = xstok.get(bi)
            if lp is not None:
                P.wait("sp", lp)
            tl = dma("sp", xv, self.xadr[tt * 128:(tt + 1) * 128, :])
            tn = dma("sp", wv, self.xn1dr[tt * 128:(tt + 1) * 128, :])
            tmm = None
            P.waits("pe", [two[-1]])
            for g in range(2):
                pb = self.PB[2 * bi + g]
                self.bank_wait(2 * bi + g)
                for fc in range(FC):
                    tmm = P.emit("pe", lambda E, tt=tt, g=g, fc=fc, pb=pb: E.matmul(
                        out=pb.ap(), lhsT=ofm[:, fc, tt * 128:(tt + 1) * 128],
                        rhs=wo1b[:, fc, g * 512:(g + 1) * 512],
                        start=(fc == 0), stop=(fc == FC - 1)), "spe" if (g == 1 and fc == FC - 1) else None)
            P.waits("dve", [tmm, tl, tn])
            op("dve", lambda E, xv=xv, wv=wv: E.tensor_tensor(out=xv, in0=xv, in1=wv, op=ALU.add))
            op("dve", lambda E, bi=bi, xv=xv: E.tensor_tensor(out=xv[:, 0:512], in0=self.PB[2 * bi].ap(), in1=xv[:, 0:512], op=ALU.add))
            tev = op("dve", lambda E, bi=bi, xv=xv: E.tensor_tensor(out=xv[:, 512:1024], in0=self.PB[2 * bi + 1].ap(), in1=xv[:, 512:1024], op=ALU.add))
            self.bank_done(2 * bi, tev)
            self.bank_done(2 * bi + 1, tev)
            P.waits("sp", [tev])
            xstok[bi] = dma("sp", self.x1dr[tt * 128:(tt + 1) * 128, :], xv)
        self.barrier()
        d_x2p = self.ddump("x2p", [T, D])
        if d_x2p is not None:
            for tt in range(TT):
                tl2 = dma("sp", self.XS.ap(), self.x1dr[tt * 128:(tt + 1) * 128, :])
                P.wait("sp", tl2)
                t5 = dma("sp", d_x2p[tt * 128:(tt + 1) * 128, :], self.XS.ap())
                P.wait("sp", t5)
        if upto == "l1attn":
            return
        self.build_l1ffn(upto)

    def build_l1ffn(self, upto="full"):
        P, op, dma = self.P, self.op, self.dma
        T, TT, TG = self.T, self.TT, self.TG
        st = self.statc.ap()
        xq_fm = self.v3(self.G1, FC, T)
        H0f = self.H0.ap()
        gv = [self.v3(self.G0, 8, T), self.v3(self.G2, 8, T), self.v3(self.G3, 8, T)]

        def gch(c):
            return gv[c // 8][:, c % 8, :]

        ubf = self.kndup.ap()[:, 0:2 * T]
        MQ = float(3 << 22)
        XA = mybir.AxisListType.X
        cb = 4096
        W = T // 2

        def Rg(i):
            return H0f[:, i * W:(i + 1) * W]

        op("dve", lambda E: E.memset(st[:, 0:2], 0.0))

        wtok = {}

        def prep_tile(wrows, b, dst3, dst_wait=None, src_b=None, dq="gp"):
            if src_b is None:
                src_b = b
            srcv = H0f[:, src_b * 1024:(src_b + 1) * 1024] if src_b < 2 else self.XS.ap()
            src3 = srcv.rearrange("p (k i) -> p k i", i=32)
            q2 = H0f[:, 2048 + b * 1024:2048 + (b + 1) * 1024]
            q3 = q2.rearrange("p (k i) -> p k i", i=32)
            sc = self.wlps.ap()[:, b * 32:(b + 1) * 32]
            rc = self.wlpr.ap()[:, b * 32:(b + 1) * 32]
            tr = wtok.get(("src", src_b))
            if tr is not None:
                P.wait(dq, tr)
            tw = dma(dq, srcv, wrows)
            P.wait("dve", tw)
            op("dve", lambda E, src3=src3, sc=sc: E.tensor_reduce(
                out=sc.unsqueeze(2), in_=src3, axis=XA, op=ALU.max, apply_absolute_value=True))
            op("dve", lambda E, sc=sc: E.tensor_scalar(
                out=sc, in0=sc, scalar1=float(1.0 / 7.0), scalar2=self.cpool.ap()[:, 4:5],
                op0=ALU.mult, op1=ALU.max))
            op("dve", lambda E, sc=sc, rc=rc: E.reciprocal(out=rc, in_=sc))
            tq = wtok.get(("q", b))
            if tq is not None:
                P.wait("dve", tq)
            rrep = rc.unsqueeze(2).broadcast_to([128, 32, 32])
            srep = sc.unsqueeze(2).broadcast_to([128, 32, 32])
            t4 = op("dve", lambda E, q3=q3, src3=src3, rrep=rrep: E.tensor_tensor(
                out=q3, in0=src3, in1=rrep, op=ALU.mult))
            wtok[("src", src_b)] = t4
            P.wait("act", t4)
            t5 = op("act", lambda E, q2=q2: E.activation(out=q2, in_=q2, func=AF.Copy, bias=MQ))
            P.wait("dve", t5)
            t6 = op("dve", lambda E, q3=q3, srep=srep: E.scalar_tensor_tensor(
                out=q3, in0=q3, scalar=MQ, in1=srep, op0=ALU.subtract, op1=ALU.mult))
            P.wait("pe", t6)
            tmm = None
            tev = None
            for half in range(2):
                bank = self.PM if half == 0 else self.PB[4]
                key = ("wq", half)
                self.bank_wait(key)
                for i in range(4):
                    fc = half * 4 + i
                    pv = bank.ap()[:, i * 128:(i + 1) * 128]
                    tmm = P.emit("pe", lambda E, fc=fc, pv=pv, q2=q2, st_=(i == 0), sp_=(i == 3): E.matmul(
                        out=pv, lhsT=q2[:, fc * 128:(fc + 1) * 128], rhs=self.identf.ap(),
                        start=st_, stop=sp_, is_transpose=True), "spe" if i == 3 else None)
                P.wait("act", tmm)
                if dst_wait is not None and half == 0:
                    P.wait("act", dst_wait)
                bv = bank.ap()[:, 0:512].rearrange("p (a b) -> p a b", b=128)
                tev = op("act", lambda E, dst3=dst3, half=half, bv=bv: E.copy(
                    out=dst3[:, half * 4:(half + 1) * 4, :], in_=bv))
                self.bank_done(key, tev)
            wtok[("q", b)] = tmm
            return tev

        def mm_tile(b, which, oc, parts=15):
            wb = self.wst.ap()[:, b * 1024:(b + 1) * 1024]
            tprep = wtok.get(("wstready", b))
            if tprep is not None:
                P.wait("pe", tprep)
            last = None
            for g in range(TG):
                pb = self.PB[g % 4]
                self.bank_wait(g % 4)
                tmm = None
                for fc in range(FC):
                    tmm = P.emit("pe", lambda E, g=g, fc=fc, pb=pb, wb=wb: E.matmul(
                        out=pb.ap(), lhsT=wb[:, fc * 128:(fc + 1) * 128],
                        rhs=xq_fm[:, fc, g * 512:(g + 1) * 512],
                        start=(fc == 0), stop=(fc == FC - 1)), "spe" if fc == FC - 1 else None)
                last = tmm
                tev = trd = None
                if parts & 2:
                    P.wait("act", tmm)
                    if which == 0:
                        if g == 0:
                            tub = wtok.get(("ubf", oc % 2))
                            if tub is not None:
                                P.waits("act", tub)
                        dstv = ubf[:, (oc % 2) * T + g * 512:(oc % 2) * T + (g + 1) * 512]
                    else:
                        dstv = gch(oc)[:, g * 512:(g + 1) * 512]
                    tev = op("act", lambda E, dstv=dstv, pb=pb: E.copy(out=dstv, in_=pb.ap()))
                if parts & 4:
                    P.wait("dve", tev)
                    trd = op("dve", lambda E, dstv=dstv: E.tensor_reduce(
                        out=st[:, 2:3], in_=dstv, axis=XA, op=ALU.max, apply_absolute_value=True))
                    op("dve", lambda E, which=which: E.tensor_tensor(
                        out=st[:, which:which + 1], in0=st[:, which:which + 1], in1=st[:, 2:3], op=ALU.max))
                self.bank_done(g % 4, [tev if tev is not None else tmm])
                if parts & 8 and which == 0 and g == TG - 1:
                    sq = "gp" if oc % 2 == 0 else "sp"
                    P.wait(sq, tev)
                    tst = dma(sq, self.updr[oc * 128:(oc + 1) * 128, 0:T],
                              ubf[:, (oc % 2) * T:(oc % 2) * T + T])
                    wtok[("ubf", oc % 2)] = [tst, trd] if trd is not None else [tst]
            return last

        wsrcs = [self.wup, self.wgate]
        order = []
        for oc in range(24):
            order.append((0, oc))
            order.append((1, oc))
        wstmm = {}
        prev = None
        lim = int(upto[4:]) if upto.startswith("l1fu") else None
        parts = 15
        if upto.startswith("l1fm"):
            lim, parts = 2, int(upto[4:])
        for i, (which, oc) in enumerate(order):
            if lim is not None and i >= lim:
                return
            b = i % 2
            dst3 = self.wst.ap()[:, b * 1024:(b + 1) * 1024].rearrange("p (a c) -> p a c", c=128)
            wtok[("wstready", b)] = prep_tile(wsrcs[which][oc * 128:(oc + 1) * 128, :], b, dst3,
                                              dst_wait=wstmm.get(b), src_b=i % 3,
                                              dq="gp" if i % 2 == 0 else "sp")
            if prev is not None:
                pw, po, pb_ = prev
                wstmm[pb_] = mm_tile(pb_, pw, po, parts)
            prev = (which, oc, b)
        pw, po, pb_ = prev
        tok_ulast = mm_tile(pb_, pw, po)
        if upto == "l1f0":
            return

        def amax_part1(cols):
            P.wait("pe", P.tok("sdve"))
            tz9 = op("dve", lambda E: E.memset(H0f[0:1, cb + 136:cb + 140], 0.0))
            t5 = None
            self.bank_wait(("wq", 0))
            for i, c in enumerate(cols):
                tmm = P.emit("pe", lambda E, c=c: E.transpose(
                    out=self.PM.ap()[0:1, 0:128], in_=st[:, c:c + 1], identity=self.identf.ap()), "spe", 1)
                P.wait("act", tmm)
                t5 = op("act", lambda E, i=i: E.copy(
                    out=H0f[0:1, cb + 140 + i * 256:cb + 140 + i * 256 + 128], in_=self.PM.ap()[0:1, 0:128]))
                P.waits("dve", [t5, tz9])
                t6 = op("dve", lambda E, i=i: E.tensor_reduce(
                    out=H0f[0:1, cb + 136 + i:cb + 137 + i],
                    in_=H0f[0:1, cb + 140 + i * 256:cb + 140 + i * 256 + 128], axis=XA, op=ALU.max))
                P.wait("sp", t6)
                P.wait("pe", t6)
            self.bank_done(("wq", 0), t5)
            t7 = dma("sp", self.cc_a_i[0:1, 0:4], H0f[0:1, cb + 136:cb + 140])
            P.wait("gp", t7)
            tcc = P.emit("gp", lambda E: E.collective_compute(
                "AllReduce", ALU.max, replica_groups=self.rg, ins=[self.cc_a_i], outs=[self.cc_a_o]), "scc", 1)
            return tcc

        def amax_part2(cols, outbase, tcc):
            P.wait("sp", tcc)
            t8 = dma("sp", H0f[0:1, cb + 130:cb + 130 + 4], self.cc_a_o)
            trc = None
            for i in range(len(cols)):
                P.waits("pe", [t8, P.tok("sdve"), P.tok("sact")])
                self.bank_wait(("wq", 0))
                tmm = P.emit("pe", lambda E, i=i: E.matmul(
                    out=self.PM.ap()[:, 0:1], lhsT=self.rowsc.ap()[0:1, 0:128],
                    rhs=H0f[0:1, cb + 130 + i:cb + 131 + i], start=True, stop=True), "spe", 1)
                P.wait("dve", tmm)
                tcp = op("dve", lambda E, i=i, outbase=outbase: E.tensor_scalar(
                    out=st[:, outbase + 2 * i:outbase + 2 * i + 1], in0=self.PM.ap()[:, 0:1],
                    scalar1=float(1.0 / 32767.0), scalar2=self.cpool.ap()[:, 4:5], op0=ALU.mult, op1=ALU.max))
                trc = op("dve", lambda E, i=i, outbase=outbase: E.reciprocal(
                    out=st[:, outbase + 2 * i + 1:outbase + 2 * i + 2],
                    in_=st[:, outbase + 2 * i:outbase + 2 * i + 1]))
                self.bank_done(("wq", 0), tcp)
                P.wait("pe", trc)
            return trc

        tcc1 = amax_part1([0, 1])

        if 8 * T >= 16384:
            bases = [(self.G1, 0), (self.G1, 8192), (self.G4, 0)]
        else:
            bases = [(self.G1, 0), (self.G4, 0), (self.G0, 8 * T)]
        sgv = [self.v3(t, 8, 1024, off=o) for (t, o) in bases]

        def wdT(ic):
            return sgv[ic // 8][:, ic % 8, :]

        for p in range(3):
            for rt in range(8):
                i9 = 48 + p * 8 + rt
                dst3 = sgv[p][:, :, rt * 128:(rt + 1) * 128]
                prep_tile(self.wdown[rt * 128:(rt + 1) * 128, p * 1024:(p + 1) * 1024],
                          i9 % 2, dst3,
                          dst_wait=tok_ulast if p * 8 + rt < 2 else None, src_b=i9 % 3,
                          dq="gp" if i9 % 2 == 0 else "sp")
        amax_part2([0, 1], 8, tcc1)
        if upto == "l1f1":
            return

        t17 = op("dve", lambda E: E.tensor_scalar(out=st[:, 17:18], in0=st[:, 10:11],
                                                  scalar1=-MQ, scalar2=None, op0=ALU.mult))
        op("dve", lambda E: E.memset(st[:, 3:4], 0.0))
        P.wait("act", t17)
        chtok = {}
        for ci in range(48):
            c, hh = ci // 2, ci % 2
            gsl = gch(c)[:, hh * W:(hh + 1) * W]
            Xv, Yv = Rg(hh), Rg(2 + hh)
            tx = chtok.get(hh)
            if tx is not None:
                P.wait("act", tx)
            op("act", lambda E, gsl=gsl, Xv=Xv: E.activation(
                out=Xv, in_=gsl, func=AF.Copy, scale=st[:, 11:12], bias=MQ))
            t2 = op("act", lambda E, Xv=Xv, Yv=Yv: E.activation(
                out=Yv, in_=Xv, func=AF.Sigmoid, scale=st[:, 10:11], bias=st[:, 17:18]))
            t3 = op("act", lambda E, Yv=Yv: E.activation(
                out=Yv, in_=Yv, func=AF.Copy, scale=65536.0, bias=MQ))
            P.wait("dve", t2)
            op("dve", lambda E, Xv=Xv: E.tensor_scalar(
                out=Xv, in0=Xv, scalar1=MQ, scalar2=None, op0=ALU.subtract))
            P.wait("dve", t3)
            op("dve", lambda E, Yv=Yv: E.tensor_scalar(
                out=Yv, in0=Yv, scalar1=MQ, scalar2=65535.0, op0=ALU.subtract, op1=ALU.min))
            t6 = op("dve", lambda E, gsl=gsl, Xv=Xv, Yv=Yv: E.scalar_tensor_tensor(
                out=gsl, in0=Yv, scalar=float(2.0 ** -16), in1=Xv, op0=ALU.mult, op1=ALU.mult))
            op("dve", lambda E, gsl=gsl: E.tensor_reduce(
                out=st[:, 2:3], in_=gsl, axis=XA, op=ALU.max, apply_absolute_value=True))
            op("dve", lambda E: E.tensor_tensor(out=st[:, 3:4], in0=st[:, 3:4], in1=st[:, 2:3], op=ALU.max))
            chtok[hh] = t6

        if upto == "l1f2":
            return
        op("dve", lambda E: E.tensor_tensor(out=st[:, 3:4], in0=st[:, 3:4], in1=st[:, 10:11], op=ALU.mult))
        tcc2 = amax_part1([3])
        h0t, h1t = wtok.get(("ubf", 0)), wtok.get(("ubf", 1))
        uptok = {0: h0t, 1: h0t, 2: h1t, 3: h1t}
        ldtok = {}

        def upload(u):
            c9, h9 = u // 2, u % 2
            sl = u % 4
            tu = uptok.get(sl)
            if tu is not None:
                P.waits("gp", tu if isinstance(tu, list) else [tu])
            ldtok[u] = dma("gp", ubf[:, sl * W:(sl + 1) * W],
                           self.updr[c9 * 128:(c9 + 1) * 128, h9 * W:(h9 + 1) * W])

        for u in range(4):
            upload(u)
        amax_part2([3], 12, tcc2)

        op("dve", lambda E: E.memset(st[:, 4:5], 0.0))
        op("dve", lambda E: E.tensor_tensor(out=st[:, 16:17], in0=st[:, 8:9], in1=st[:, 12:13], op=ALU.mult))
        t18 = op("dve", lambda E: E.tensor_tensor(out=st[:, 18:19], in0=st[:, 10:11], in1=st[:, 13:14], op=ALU.mult))
        P.wait("act", t18)
        for ci in range(48):
            c, hh = ci // 2, ci % 2
            sl = ci % 4
            gsl = gch(c)[:, hh * W:(hh + 1) * W]
            usl = ubf[:, sl * W:(sl + 1) * W]
            Xv, Yv = Rg(hh), Rg(2 + hh)
            tx = chtok.get(hh)
            if tx is not None:
                P.wait("act", tx)
            P.wait("act", ldtok[ci])
            t1 = op("act", lambda E, usl=usl, Xv=Xv: E.activation(
                out=Xv, in_=usl, func=AF.Copy, scale=st[:, 9:10], bias=MQ))
            uptok[sl] = t1
            if ci + 4 < 48:
                upload(ci + 4)
            t2 = op("act", lambda E, gsl=gsl, Yv=Yv: E.activation(
                out=Yv, in_=gsl, func=AF.Copy, scale=st[:, 18:19], bias=MQ))
            P.wait("dve", t1)
            op("dve", lambda E, Xv=Xv: E.tensor_scalar(
                out=Xv, in0=Xv, scalar1=MQ, scalar2=st[:, 16:17], op0=ALU.subtract, op1=ALU.mult))
            P.wait("dve", t2)
            t4 = op("dve", lambda E, gsl=gsl, Xv=Xv, Yv=Yv: E.scalar_tensor_tensor(
                out=gsl, in0=Yv, scalar=MQ, in1=Xv, op0=ALU.subtract, op1=ALU.mult))
            op("dve", lambda E, gsl=gsl: E.tensor_reduce(
                out=st[:, 2:3], in_=gsl, axis=XA, op=ALU.max, apply_absolute_value=True))
            op("dve", lambda E: E.tensor_tensor(out=st[:, 4:5], in0=st[:, 4:5], in1=st[:, 2:3], op=ALU.max))
            chtok[hh] = t4

        if upto == "l1f3":
            return
        tcc3 = amax_part1([4])
        trc3 = amax_part2([4], 14, tcc3)

        P.wait("act", trc3)
        for ci in range(48):
            c, hh = ci // 2, ci % 2
            gsl = gch(c)[:, hh * W:(hh + 1) * W]
            Xv = Rg(hh)
            tx = chtok.get(hh)
            if tx is not None:
                P.wait("act", tx)
            t1 = op("act", lambda E, gsl=gsl, Xv=Xv: E.activation(
                out=Xv, in_=gsl, func=AF.Copy, scale=st[:, 15:16], bias=MQ))
            P.wait("dve", t1)
            t2 = op("dve", lambda E, gsl=gsl, Xv=Xv: E.tensor_scalar(
                out=gsl, in0=Xv, scalar1=MQ, scalar2=st[:, 14:15], op0=ALU.subtract, op1=ALU.mult))
            chtok[hh] = t2
        self.barrier()
        if upto == "l1f4":
            return

        d_off = self.ddump("offn", [24 * 128, 256], BF16)
        if d_off is not None:
            for c in range(24):
                t9 = dma("sp", d_off[c * 128:(c + 1) * 128, :], gch(c)[:, 0:256])
                P.wait("sp", t9)

        xstok = {}
        for tt in range(TT):
            bi = tt % 2
            xv = self.XS.ap() if bi == 0 else H0f[:, 0:1024]
            lp = xstok.get(bi)
            if lp is not None:
                P.wait("sp", lp)
            tlx = dma("sp", xv, self.x1dr[tt * 128:(tt + 1) * 128, :])
            tmm = None
            for g in range(2):
                pb = self.PB[2 * bi + g]
                self.bank_wait(2 * bi + g)
                for ic in range(24):
                    tmm = P.emit("pe", lambda E, g=g, ic=ic, pb=pb, tt=tt: E.matmul(
                        out=pb.ap(), lhsT=gch(ic)[:, tt * 128:(tt + 1) * 128],
                        rhs=wdT(ic)[:, g * 512:(g + 1) * 512],
                        start=(ic == 0), stop=(ic == 23)), "spe" if ic == 23 else None)
            P.waits("dve", [tmm, tlx])
            op("dve", lambda E, bi=bi, xv=xv: E.tensor_tensor(
                out=xv[:, 0:512], in0=self.PB[2 * bi].ap(), in1=xv[:, 0:512], op=ALU.add))
            tev = op("dve", lambda E, bi=bi, xv=xv: E.tensor_tensor(
                out=xv[:, 512:1024], in0=self.PB[2 * bi + 1].ap(), in1=xv[:, 512:1024], op=ALU.add))
            self.bank_done(2 * bi, tev)
            self.bank_done(2 * bi + 1, tev)
            P.waits("sp", [tev])
            xstok[bi] = dma("sp", self.y_out[tt * 128:(tt + 1) * 128, :], xv)
        self.barrier()

    def finish(self):
        P = self.P
        toks = [P.tok(s) for s in ("dsp", "dgp", "scc", "spe", "sdve", "sact", "spool")]
        P.waits("sp", toks)
        with self.nc.Block() as block:
            @block.sync
            def _(E):
                for f in P.ops["sp"]:
                    f(E)

            @block.gpsimd
            def _(E):
                for f in P.ops["gp"]:
                    f(E)

            @block.tensor
            def _(E):
                for f in P.ops["pe"]:
                    f(E)

            @block.vector
            def _(E):
                for f in P.ops["dve"]:
                    f(E)

            @block.scalar
            def _(E):
                for f in P.ops["act"]:
                    f(E)


def build(T=2048, upto="full", dump=(), rg_cores=NCORES):
    b = Builder(T, rg_cores, dump)
    with ExitStack() as stk:
        b.stk = stk
        b.declare_io()
        b.alloc()
        b.ph_init()
        b.build_l0(upto)
        if upto.startswith("l1") or upto == "full":
            b.build_l1(upto)
        b.finish()
    return b.nc, b.dumps



def rope_tables(T, core, ncores=NCORES):
    S = T * ncores
    pos = np.arange(core * T, (core + 1) * T, dtype=np.float64)
    freqs = 1.0 / 10000.0 ** (np.arange(0, 64, 2, dtype=np.float64) / 64.0)
    ang = pos[:, None] * freqs[None, :]
    return np.cos(ang).astype(np.float32), np.sin(ang).astype(np.float32)


def perm_qk():
    perm = np.zeros(1024, dtype=np.int64)
    for h in range(16):
        for p in range(2):
            for i in range(32):
                perm[h * 64 + p * 32 + i] = h * 64 + 2 * i + p
    return perm


def prep_core_inputs(inputs, core, T=2048):
    perm = perm_qk()
    x = np.ascontiguousarray(np.asarray(inputs["L0_attn_x"]).reshape(-1, 1024)[core * T:(core + 1) * T])
    wq = np.asarray(inputs["L0_attn_Wq"])[:, perm]
    wk = np.asarray(inputs["L0_attn_Wk"])[:, perm]
    wv = np.asarray(inputs["L0_attn_Wv"])
    wo = np.asarray(inputs["L0_attn_Wo"])
    cos, sin = rope_tables(T, core)
    ident = np.eye(128, dtype=ml_dtypes.bfloat16)
    mask = np.zeros((128, 64), dtype=np.float32)
    tri = np.triu(np.ones((64, 64), dtype=np.float32)) * np.float32(np.exp(44.0))
    mask[0:64] = tri
    mask[64:128] = tri
    wq1 = np.asarray(inputs["L1_attn_Wq"]).T
    wk1 = np.asarray(inputs["L1_attn_Wk"]).T
    wv1 = np.asarray(inputs["L1_attn_Wv"]).T
    wa1 = np.asarray(inputs["L1_attn_Wa_w"]).T
    l1wt = np.ascontiguousarray(np.concatenate([wq1, wk1, wv1, wk1, wk1, wk1, wa1], axis=1))
    qc = np.asarray(inputs["L1_attn_Qc"])
    vc = np.asarray(inputs["L1_attn_Vc"])
    l1qc = np.ascontiguousarray(qc.transpose(1, 0, 2).reshape(64, 512))
    l1vc = np.ascontiguousarray(vc.transpose(1, 0, 2).reshape(64, 1024))
    rmask = np.zeros((128, 16), dtype=np.float32)
    for r in range(8):
        rmask[:, r] = 1.0 if r < core else 0.0
        rmask[:, 8 + r] = 0.0 if r < core else 1.0
    return {
        "x": x,
        "rmaskin": rmask,
        "wqk": np.ascontiguousarray(np.concatenate([wq, wk], axis=1)),
        "wvo": np.ascontiguousarray(np.concatenate([wv, wo], axis=1)),
        "w1": np.asarray(inputs["L0_ffn_w1"]),
        "b1v": np.asarray(inputs["L0_ffn_b1"]),
        "w2": np.asarray(inputs["L0_ffn_w2"]),
        "b2v": np.asarray(inputs["L0_ffn_b2"]),
        "cos32": cos, "sin32": sin,
        "identin": ident, "maskin": mask,
        "l1wt": l1wt,
        "l1qc": l1qc,
        "l1vc": l1vc,
        "l1wo": np.ascontiguousarray(np.asarray(inputs["L1_attn_Wo"]).T),
        "l1wab": np.asarray(inputs["L1_attn_Wa_b"]),
        "l1gamma": np.asarray(inputs["L1_attn_gamma"]),
        "l1beta": np.asarray(inputs["L1_attn_beta"]),
        "wgate": np.asarray(inputs["L1_ffn_w_gate"]),
        "wup": np.asarray(inputs["L1_ffn_w_up"]),
        "wdown": np.asarray(inputs["L1_ffn_w_down"]),
    }


def kernel(**inputs):
    T = 2048
    nc, _ = build(T=T, upto="full")
    in_maps = [prep_core_inputs(inputs, c, T) for c in range(NCORES)]
    res = run_bass_kernel_spmd(nc, in_maps, core_ids=list(range(NCORES)))
    y = np.concatenate([res.results[c]["y"] for c in range(NCORES)], axis=0)
    return y.reshape(1, 16384, 1024)


# version 25
# speedup vs baseline: 1.0079x; 1.0079x over previous
import numpy as np
import ml_dtypes
import concourse.bass as bass
import concourse.mybir as mybir
from concourse.bass_utils import run_bass_kernel_spmd
from contextlib import ExitStack

F32 = mybir.dt.float32
BF16 = mybir.dt.bfloat16
FP16 = mybir.dt.float16
FP8 = mybir.dt.float8e4
AF = mybir.ActivationFunctionType
ALU = mybir.AluOpType

NCORES = 8
SERIALIZE_SP = True
D = 1024
FC = 8
LN2 = float(np.log(2.0))
MAGIC = float(3 << 22)
EPS_LN = 1e-5
L0_EPS = 1e-6

ENGS = ("sp", "gp", "pe", "dve", "act")
SEM_OF = {"pe": "spe", "dve": "sdve", "act": "sact", "gp": "spool", "sp": "dsp"}


class Prog:
    def __init__(self, sems):
        self.ops = {e: [] for e in ENGS}
        self.sem = sems
        self.cnt = {k: 0 for k in sems}
        self.lastw = {}
        self.wait_high = {}
        self.quiesced = {}

    def wait(self, eng, tok):
        if tok is None:
            return
        sem, val = tok
        if sem in ("dsp", "dgp"):
            val = self.cnt[sem]
        if val <= 0:
            return
        if self.lastw.get((eng, sem), -1) >= val:
            return
        self.lastw[(eng, sem)] = val
        self.wait_high[sem] = max(self.wait_high.get(sem, 0), val)
        h = self.sem[sem]
        self.ops[eng].append(lambda E, h=h, v=val: E.wait_ge(h, v))

    def waits(self, eng, toks):
        for t in toks:
            self.wait(eng, t)

    def emit(self, eng, fn, sem=None, n=1):
        if sem is None:
            self.ops[eng].append(fn)
            return None
        self.cnt[sem] += n
        v = self.cnt[sem]
        h = self.sem[sem]
        self.ops[eng].append(lambda E, f=fn, h=h, m=n: f(E).then_inc(h, m))
        return (sem, v)

    def tok(self, sem):
        return (sem, self.cnt[sem])


class Builder:

    def __init__(self, T, rg_cores, dump):
        self.T = T
        self.TT = T // 128
        self.TG = T // 512
        self.dump = dump
        self.rg = [list(range(rg_cores))]
        self.nc = bass.Bass(target_bir_lowering=False)
        self.dumps = {}

    def din(self, name, shape, dtype=F32):
        return self.nc.dram_tensor(name, shape, dtype, kind="ExternalInput").ap()

    def ddump(self, name, shape, dtype=F32):
        if name in self.dump:
            ap = self.nc.dram_tensor("dbg_" + name, shape, dtype, kind="ExternalOutput").ap()
            self.dumps[name] = ap
            return ap
        return None

    def dma(self, eng, out, in_, **kw):
        P = self.P
        sem = "dsp" if eng == "sp" else "dgp"
        wh = P.wait_high.get(sem, 0)
        if wh > P.quiesced.get(sem, 0) and P.cnt[sem] >= wh:
            h = P.sem[sem]
            P.ops[eng].append(lambda E, h=h, v=wh: E.wait_ge(h, v))
            P.quiesced[sem] = wh
        return P.emit(eng, lambda E, o=out, i=in_, k=dict(kw): E.dma_start(out=o, in_=i, **k), sem, 16)

    def op(self, eng, fn, mark=True):
        if eng in ("dve", "act", "gp"):
            self.P.wait(eng, self.P.tok(SEM_OF[eng]))
        return self.P.emit(eng, fn, SEM_OF[eng] if mark else None)

    def selfw(self, eng):
        self.P.wait(eng, self.P.tok(SEM_OF[eng]))

    def barrier(self):
        toks = [self.P.tok(s) for s in self.P.cnt]
        for e in ENGS:
            self.P.waits(e, toks)

    def v3(self, t, a, b, off=0):
        if off:
            return t.ap()[:, off:off + a * b].rearrange("p (a b) -> p a b", a=a, b=b)
        return t.ap()[:, 0:a * b].rearrange("p (a b) -> p a b", a=a, b=b)

    def transp8(self, srcs, dst_ap, eng="act"):
        P, op = self.P, self.op
        bi = self._tp_flip
        self._tp_flip ^= 1
        bank = self.PT if bi == 0 else self.PT2
        key = ("tp", bi)
        self.bank_wait(key)
        n = len(srcs)
        tmm = None
        for i, srcv in enumerate(srcs):
            pv = bank.ap()[:, i * 128:(i + 1) * 128]
            tmm = P.emit("pe", lambda E, sv=srcv, pv=pv, st=(i == 0), sp=(i == n - 1): E.matmul(
                out=pv, lhsT=sv, rhs=self.ident.ap(), start=st, stop=sp, is_transpose=True),
                "spe" if i == n - 1 else None)
        P.wait(eng, tmm)
        bview = bank.ap()[:, 0:n * 128].rearrange("p (a b) -> p a b", b=128)
        if eng == "act":
            tev = op("act", lambda E, d=dst_ap, bv=bview: E.copy(out=d, in_=bv), mark=True)
        else:
            tev = op("dve", lambda E, d=dst_ap, bv=bview: E.tensor_copy(out=d, in_=bv), mark=True)
        self.bank_done(key, tev)
        return tev

    def transp(self, src, dst, slot, eng="act"):
        return self.transp8([src], dst.unsqueeze(1) if dst.ndim == 2 else dst, eng=eng)

    def bank_wait(self, key):
        t = self._banktok.get(key)
        if t is not None:
            if isinstance(t, list):
                self.P.waits("pe", t)
            else:
                self.P.wait("pe", t)

    def bank_done(self, key, tok):
        self._banktok[key] = tok

    def alloc(self):
        self._banktok = {}
        self._pt_tok = [None] * 8
        self._tp_flip = 0
        nc, E = self.nc, self.stk.enter_context
        T, TT = self.T, self.TT
        self.G0 = E(nc.sbuf_tensor("G0", [128, max(8 * T, 16384)], BF16))
        self.G1 = E(nc.sbuf_tensor("G1", [128, 8 * T], BF16))
        self.G2 = E(nc.sbuf_tensor("G2", [128, max(8 * T, 16384)], BF16))
        self.G3 = E(nc.sbuf_tensor("G3", [128, max(8 * T, 7 * T + 1536)], BF16))
        self.G4 = E(nc.sbuf_tensor("G4", [128, max(4 * T, 8192)], BF16))
        self.H0 = E(nc.sbuf_tensor("H0", [128, 2048 + 2048 + 1024], F32))
        self.XS = E(nc.sbuf_tensor("XS", [128, 1024], F32))
        self.cpool = E(nc.sbuf_tensor("cpool", [128, 16], F32))
        self.ident = E(nc.sbuf_tensor("ident", [128, 128], BF16))
        self.identf = E(nc.sbuf_tensor("identf", [128, 128], F32))
        self.maskt = E(nc.sbuf_tensor("maskt", [128, 64], F32))
        self.onesf = E(nc.sbuf_tensor("onesf", [128, 4], F32))
        self.onesb = E(nc.sbuf_tensor("onesb", [128, 4], BF16))
        self.costt = E(nc.sbuf_tensor("costt", [128, TT * 32], BF16))
        self.sintt = E(nc.sbuf_tensor("sintt", [128, TT * 32], BF16))
        self.bcols = E(nc.sbuf_tensor("bcols", [128, 96], F32))
        self.statc = E(nc.sbuf_tensor("statc", [128, 32], F32))
        self.vkTb = E(nc.sbuf_tensor("vkTb", [128, 8 * 65], BF16))
        self.vkTf = E(nc.sbuf_tensor("vkTf", [128, 8 * 65], F32))
        self.rowsc = E(nc.sbuf_tensor("rowsc", [1, 1024], F32))
        self.rowb2 = E(nc.sbuf_tensor("rowb2", [1, 1024], FP16))
        self.ones16 = E(nc.sbuf_tensor("ones16", [1, 128], FP16))
        self.Ssb = E(nc.sbuf_tensor("Ssb", [128, 4 * 128], F32))
        self.Dsb = E(nc.sbuf_tensor("Dsb", [128, 4 * 32], F32))
        self.Csb = E(nc.sbuf_tensor("Csb", [128, 4 * 32], F32))
        self.Esb = E(nc.sbuf_tensor("Esb", [128, 4 * 32], F32))
        self.Msb = E(nc.sbuf_tensor("Msb", [128, 4 * 32], F32))
        self.Gsb = E(nc.sbuf_tensor("Gsb", [128, 516], F32))
        self.kndup = E(nc.sbuf_tensor("kndup", [128, max(2048, 2 * T)], BF16))
        self.f16t = E(nc.sbuf_tensor("f16t", [128, 1024], FP16))
        self.b2f16 = E(nc.sbuf_tensor("b2f16", [128, 8], FP16))
        self.h8t = E(nc.sbuf_tensor("h8t", [128, 2048], FP8))
        self.w8t = E(nc.sbuf_tensor("w8t", [128, 128], FP8))
        self.wst = E(nc.sbuf_tensor("wst", [128, 2048], BF16))
        self.wlps = E(nc.sbuf_tensor("wlps", [128, 64], F32))
        self.wlpr = E(nc.sbuf_tensor("wlpr", [128, 64], F32))
        self.PB = [E(nc.psum_tensor(f"pb{i}", [128, 512], F32)) for i in range(5)]
        self.PT = E(nc.psum_tensor("pt", [128, 1024], BF16))
        self.PT2 = E(nc.psum_tensor("pt2", [128, 1024], BF16))
        self.PM = E(nc.psum_tensor("pm", [128, 512], F32))
        sems = {}
        for s in ("dsp", "dgp", "spe", "sdve", "sact", "spool", "scc"):
            sems[s] = E(nc.semaphore(s))
        self.P = Prog(sems)

    def declare_io(self):
        T = self.T
        din = self.din
        nc = self.nc
        self.x_in = din("x", [T, D])
        self.wqk = din("wqk", [D, 2048])
        self.wvo = din("wvo", [D, 2048])
        self.w1 = din("w1", [4096, D])
        self.b1v = din("b1v", [4096])
        self.w2 = din("w2", [D, 4096])
        self.b2v = din("b2v", [1024])
        self.cos32 = din("cos32", [T, 32])
        self.sin32 = din("sin32", [T, 32])
        self.identin = din("identin", [128, 128], BF16)
        self.rmaskin = din("rmaskin", [128, 16])
        self.maskin = din("maskin", [128, 64])
        self.l1wt = din("l1wt", [D, 896])
        self.l1qc = din("l1qc", [64, 512])
        self.l1vc = din("l1vc", [64, 1024])
        self.l1wo = din("l1wo", [D, D])
        self.l1wab = din("l1wab", [512])
        self.l1gamma = din("l1gamma", [D])
        self.l1beta = din("l1beta", [D])
        self.wgate = din("wgate", [3072, D])
        self.wup = din("wup", [3072, D])
        self.wdown = din("wdown", [D, 3072])
        self.y_out = nc.dram_tensor("y", [T, D], F32, kind="ExternalOutput").ap()
        self.xadr = nc.dram_tensor("xadr", [T, D], F32, kind="Internal").ap()
        self.x1dr = nc.dram_tensor("x1dr", [T, D], F32, kind="Internal").ap()
        self.xn1dr = nc.dram_tensor("xn1dr", [T, D], BF16, kind="Internal").ap()
        self.updr = nc.dram_tensor("updr", [3072, T], BF16, kind="Internal").ap()
        self.ksumdr = nc.dram_tensor("ksumdr", [1024], F32, kind="Internal").ap()
        self.cc_vk_i = nc.dram_tensor("cc_vk_i", [128, 520], F32, kind="Internal").ap()
        self.cc_vk_o = nc.dram_tensor("cc_vk_o", [128, 520], F32, kind="Internal").ap()
        self.cc_g_i = nc.dram_tensor("cc_g_i", [128, 516], F32, kind="Internal").ap()
        self.cc_g_o = nc.dram_tensor("cc_g_o", [128 * len(self.rg[0]), 516], F32, kind="Internal").ap()
        self.cc_a_i = nc.dram_tensor("cc_a_i", [1, 4], F32, kind="Internal").ap()
        self.cc_a_o = nc.dram_tensor("cc_a_o", [1, 4], F32, kind="Internal").ap()
        self.betadr = nc.dram_tensor("betadr", [896], F32, kind="Internal").ap()
        self.wq2dr = nc.dram_tensor("wq2dr", [256, 16384], BF16, kind="Internal").ap()
        self.w1tdr = nc.dram_tensor("w1tdr", [128, 32 * 1024], BF16, kind="Internal").ap()

    def ph_init(self):
        P, op, dma = self.P, self.op, self.dma
        dma("sp", self.ident.ap(), self.identin)
        dma("sp", self.maskt.ap(), self.maskin)
        dma("gp", self.costt.ap().rearrange("p (tt f) -> p tt f", f=32), self.cos32.rearrange("(tt p) f -> p tt f", p=128))
        dma("gp", self.sintt.ap().rearrange("p (tt f) -> p tt f", f=32), self.sin32.rearrange("(tt p) f -> p tt f", p=128))
        dma("gp", self.bcols.ap()[:, 0:32], self.b1v.rearrange("(c p) -> p c", p=128), allow_slow_non_contiguous=True)
        dma("gp", self.bcols.ap()[:, 32:40], self.b2v.rearrange("(c p) -> p c", p=128), allow_slow_non_contiguous=True)
        dma("gp", self.bcols.ap()[:, 40:48], self.l1gamma.rearrange("(c p) -> p c", p=128), allow_slow_non_contiguous=True)
        dma("gp", self.bcols.ap()[:, 48:56], self.l1beta.rearrange("(c p) -> p c", p=128), allow_slow_non_contiguous=True)
        dma("sp", self.rowsc.ap()[0:1, 0:1024], self.b2v.unsqueeze(0))
        t_bc = P.tok("dgp")
        t_ld = P.tok("dsp")
        op("dve", lambda E: E.memset(self.cpool.ap()[:, 0:1], EPS_LN))
        op("dve", lambda E: E.memset(self.cpool.ap()[:, 1:2], 1.0))
        op("dve", lambda E: E.memset(self.cpool.ap()[:, 2:3], L0_EPS))
        op("dve", lambda E: E.memset(self.cpool.ap()[:, 3:4], 1e-24))
        op("dve", lambda E: E.memset(self.cpool.ap()[:, 4:5], 1e-12))
        op("dve", lambda E: E.memset(self.cpool.ap()[:, 5:6], -80.0))
        op("dve", lambda E: E.memset(self.cpool.ap()[:, 6:7], 22.0))
        op("dve", lambda E: E.memset(self.cpool.ap()[:, 7:8], -22.0))
        op("dve", lambda E: E.memset(self.onesf.ap(), 1.0))
        op("dve", lambda E: E.memset(self.onesb.ap(), 1.0))
        self.t_ones = op("dve", lambda E: E.memset(self.ones16.ap(), 1.0), mark=True)
        P.wait("act", t_bc)
        op("act", lambda E: E.copy(out=self.b2f16.ap(), in_=self.bcols.ap()[:, 32:40]))
        P.wait("act", t_ld)
        self.t_b2q = op("act", lambda E: E.copy(out=self.rowb2.ap(), in_=self.rowsc.ap()[0:1, 0:1024]), mark=True)
        P.wait("dve", t_bc)
        op("dve", lambda E: E.reciprocal(out=self.statc.ap()[:, 0:8], in_=self.bcols.ap()[:, 40:48]))
        op("dve", lambda E: E.tensor_tensor(out=self.bcols.ap()[:, 56:64], in0=self.bcols.ap()[:, 48:56], in1=self.statc.ap()[:, 0:8], op=ALU.mult))
        P.wait("gp", t_ld)
        self.t_idf = dma("gp", self.identf.ap(), self.ident.ap())

    def ph_ln(self, src_dr, dst_tm, spill_bf=None):
        P, op, dma = self.P, self.op, self.dma
        toks = []
        last3 = {}
        for tt in range(self.TT):
            xb = self.XS.ap() if tt % 2 == 0 else self.H0.ap()[:, 0:1024]
            w = self.statc.ap()[:, 8:24] if tt % 2 == 0 else self.wlps.ap()[:, 0:16]
            lp = last3.get(tt % 2)
            if lp is not None:
                P.wait("sp", lp)
            tl = dma("sp", xb, src_dr[tt * 128:(tt + 1) * 128, :])
            P.wait("dve", tl)
            if lp is not None:
                P.wait("dve", lp)
            op("dve", lambda E, xb=xb, w=w: E.bn_stats(out=w[:, 0:6], in_=xb[:, 0:512]))
            op("dve", lambda E, xb=xb, w=w: E.bn_stats(out=w[:, 6:12], in_=xb[:, 512:1024]))
            self.selfw("dve")
            t1 = op("dve", lambda E, w=w: E.bn_aggr(out=w[:, 12:14], in_=w[:, 0:12]), mark=True)
            P.wait("act", t1)
            op("act", lambda E, w=w: E.activation(out=w[:, 14:15], in_=w[:, 13:14], func=AF.Ln, bias=self.cpool.ap()[:, 0:1]))
            self.selfw("act")
            t2 = op("act", lambda E, w=w: E.activation(out=w[:, 15:16], in_=w[:, 14:15], func=AF.Exp, scale=-0.5), mark=True)
            P.wait("dve", t2)
            t3 = op("dve", lambda E, tt=tt, xb=xb, w=w: E.tensor_scalar(out=dst_tm[:, tt, :], in0=xb,
                                                            scalar1=w[:, 12:13], scalar2=w[:, 15:16],
                                                            op0=ALU.subtract, op1=ALU.mult), mark=True)
            toks.append(t3)
            last3[tt % 2] = t3
            if spill_bf is not None:
                P.wait("sp", t3)
                dma("sp", spill_bf[tt * 128:(tt + 1) * 128, :], dst_tm[:, tt, :])
        return toks

    def ph_transpose_all(self, src_tm, dst_fm, toks=None):
        P = self.P
        for tt in range(self.TT):
            if toks is not None:
                P.wait("pe", toks[tt])
            srcs = [src_tm[:, tt, fc * 128:(fc + 1) * 128] for fc in range(FC)]
            dst = dst_fm[:, :, tt * 128:(tt + 1) * 128]
            self.transp8(srcs, dst)

    def proj_tm(self, xn_fm, wbuf, dst_tm, tok_w, ncols=1024):
        P, op = self.P, self.op
        ng = ncols // 512
        for tt in range(self.TT):
            for g in range(ng):
                bi = (tt * ng + g) % 4
                pb = self.PB[bi]
                P.waits("pe", tok_w)
                self.bank_wait(bi)
                tmm = None
                for fc in range(FC):
                    tmm = P.emit("pe", lambda E, tt=tt, g=g, fc=fc, pb=pb: E.matmul(
                        out=pb.ap(), lhsT=xn_fm[:, fc, tt * 128:(tt + 1) * 128],
                        rhs=wbuf[:, fc, g * 512:(g + 1) * 512],
                        start=(fc == 0), stop=(fc == FC - 1)), "spe" if fc == FC - 1 else None)
                P.wait("act", tmm)
                tev = op("act", lambda E, tt=tt, g=g, pb=pb: E.copy(out=dst_tm[:, tt, g * 512:(g + 1) * 512], in_=pb.ap()), mark=True)
                self.bank_done(bi, tev)
        return P.tok("sact")

    def rope_tile(self, t_tm, tt, prev):
        P, op = self.P, self.op
        src = t_tm[:, tt, :]
        r4 = src.rearrange("p (h q i) -> p h q i", h=16, q=2, i=32)
        Ev = r4[:, :, 0, :]
        Ov = r4[:, :, 1, :]
        cosv = self.costt.ap()[:, tt * 32:(tt + 1) * 32].unsqueeze(1).broadcast_to([128, 16, 32])
        sinv = self.sintt.ap()[:, tt * 32:(tt + 1) * 32].unsqueeze(1).broadcast_to([128, 16, 32])
        sc = self.H0.ap()[:, 0:2048].rearrange("p (k h i) -> p k h i", k=4, h=16, i=32)
        if prev is not None:
            P.wait("dve", prev)
        op("dve", lambda E, Ev=Ev, cosv=cosv, sc=sc: E.tensor_tensor(out=sc[:, 0], in0=Ev, in1=cosv, op=ALU.mult))
        op("dve", lambda E, Ov=Ov, sinv=sinv, sc=sc: E.tensor_tensor(out=sc[:, 1], in0=Ov, in1=sinv, op=ALU.mult))
        op("dve", lambda E, Ev=Ev, sinv=sinv, sc=sc: E.tensor_tensor(out=sc[:, 2], in0=Ev, in1=sinv, op=ALU.mult))
        t0 = op("dve", lambda E, Ov=Ov, cosv=cosv, sc=sc: E.tensor_tensor(out=sc[:, 3], in0=Ov, in1=cosv, op=ALU.mult), mark=True)
        P.wait("dve", t0)
        op("dve", lambda E, sc=sc: E.tensor_tensor(out=sc[:, 0], in0=sc[:, 0], in1=sc[:, 1], op=ALU.subtract))
        t1 = op("dve", lambda E, sc=sc: E.tensor_tensor(out=sc[:, 2], in0=sc[:, 2], in1=sc[:, 3], op=ALU.add), mark=True)
        P.wait("act", t1)
        op("act", lambda E, Ev=Ev, sc=sc: E.activation(out=Ev, in_=sc[:, 0], func=AF.Relu))
        t2 = op("act", lambda E, Ov=Ov, sc=sc: E.activation(out=Ov, in_=sc[:, 2], func=AF.Relu), mark=True)
        return t2

    def rope_relu(self, t_tm):
        P, op = self.P, self.op
        toks = []
        for tt in range(self.TT):
            src = t_tm[:, tt, :]
            r4 = src.rearrange("p (h q i) -> p h q i", h=16, q=2, i=32)
            Ev = r4[:, :, 0, :]
            Ov = r4[:, :, 1, :]
            cosv = self.costt.ap()[:, tt * 32:(tt + 1) * 32].unsqueeze(1).broadcast_to([128, 16, 32])
            sinv = self.sintt.ap()[:, tt * 32:(tt + 1) * 32].unsqueeze(1).broadcast_to([128, 16, 32])
            sc = self.H0.ap()[:, 0:2048].rearrange("p (k h i) -> p k h i", k=4, h=16, i=32)
            op("dve", lambda E, Ev=Ev, cosv=cosv, sc=sc: E.tensor_tensor(out=sc[:, 0], in0=Ev, in1=cosv, op=ALU.mult))
            op("dve", lambda E, Ov=Ov, sinv=sinv, sc=sc: E.tensor_tensor(out=sc[:, 1], in0=Ov, in1=sinv, op=ALU.mult))
            op("dve", lambda E, Ev=Ev, sinv=sinv, sc=sc: E.tensor_tensor(out=sc[:, 2], in0=Ev, in1=sinv, op=ALU.mult))
            t0 = op("dve", lambda E, Ov=Ov, cosv=cosv, sc=sc: E.tensor_tensor(out=sc[:, 3], in0=Ov, in1=cosv, op=ALU.mult), mark=True)
            P.wait("dve", t0)
            op("dve", lambda E, sc=sc: E.tensor_tensor(out=sc[:, 0], in0=sc[:, 0], in1=sc[:, 1], op=ALU.subtract))
            t1 = op("dve", lambda E, sc=sc: E.tensor_tensor(out=sc[:, 2], in0=sc[:, 2], in1=sc[:, 3], op=ALU.add), mark=True)
            P.wait("act", t1)
            op("act", lambda E, Ev=Ev, sc=sc: E.activation(out=Ev, in_=sc[:, 0], func=AF.Relu))
            t2 = op("act", lambda E, Ov=Ov, sc=sc: E.activation(out=Ov, in_=sc[:, 2], func=AF.Relu), mark=True)
            toks.append(t2)
            P.wait("dve", t2)
        return toks

    def build_l0(self, upto):
        P, op, dma = self.P, self.op, self.dma
        T, TT, TG = self.T, self.TT, self.TG
        xn_tm = self.v3(self.G0, TT, 1024)
        xn_fm = self.v3(self.G1, FC, T)
        q_tm = self.v3(self.G3, TT, 1024)
        k_tm = self.v3(self.G2, TT, 1024)
        v_tm = self.v3(self.G0, TT, 1024)
        wbuf = self.v3(self.G4, FC, 1024)
        q_fm = self.v3(self.G0, FC, T)
        attn_tm = self.v3(self.G3, TT, 1024)
        attn_fm = self.v3(self.G2, FC, T)

        wb2 = [self.v3(self.G4, FC, 512), self.v3(self.G4, FC, 512, off=4096)]
        wsrc6 = [(self.wqk, 0), (self.wqk, 512), (self.wqk, 1024), (self.wqk, 1536),
                 (self.wvo, 0), (self.wvo, 512)]
        ldq = {}
        rdtok = {}

        def wload(j):
            ws, coff = wsrc6[j]
            rt = rdtok.get(j % 2)
            if rt is not None:
                P.wait("gp", rt)
            ldq[j] = [dma("gp", wb2[j % 2][:, fc, :], ws[fc * 128:(fc + 1) * 128, coff:coff + 512])
                      for fc in range(FC)][-1]

        wload(0)
        wload(1)
        tok_xn = self.ph_ln(self.x_in, xn_tm)
        d_xn = self.ddump("xn", [T, D], BF16)
        if d_xn is not None:
            for tt in range(TT):
                P.wait("sp", tok_xn[tt])
                dma("sp", d_xn[tt * 128:(tt + 1) * 128, :], xn_tm[:, tt, :])
        P.wait("pe", self.P.tok("dsp"))
        self.ph_transpose_all(xn_tm, xn_fm, tok_xn)
        self.barrier()
        if upto == "ln":
            return

        dst6 = [q_tm, q_tm, k_tm, k_tm, v_tm, v_tm]
        tq_r = []
        tk_r = []
        rprev = None
        for j in range(6):
            dv = dst6[j][:, :, (j % 2) * 512:(j % 2) * 512 + 512]
            for tt in range(TT):
                bi = tt % 4
                pb = self.PB[bi]
                P.waits("pe", [ldq[j]])
                self.bank_wait(bi)
                tmm = None
                for fc in range(FC):
                    tmm = P.emit("pe", lambda E, tt=tt, fc=fc, pb=pb, j=j: E.matmul(
                        out=pb.ap(), lhsT=xn_fm[:, fc, tt * 128:(tt + 1) * 128],
                        rhs=wb2[j % 2][:, fc, :],
                        start=(fc == 0), stop=(fc == FC - 1)), "spe" if fc == FC - 1 else None)
                P.wait("act", tmm)
                tev = op("act", lambda E, tt=tt, dv=dv, pb=pb: E.copy(out=dv[:, tt, :], in_=pb.ap()), mark=True)
                self.bank_done(bi, tev)
                if j == 1:
                    P.wait("dve", tev)
                    rprev = self.rope_tile(q_tm, tt, rprev)
                    tq_r.append(rprev)
                elif j == 3:
                    P.wait("dve", tev)
                    rprev = self.rope_tile(k_tm, tt, rprev)
                    tk_r.append(rprev)
            rdtok[j % 2] = P.tok("spe")
            if j + 2 < 6:
                wload(j + 2)
        tv = P.tok("sact")
        d_q = self.ddump("q_tm", [T, D], BF16)
        if d_q is not None:
            for tt in range(TT):
                P.wait("sp", tq_r[tt])
                dma("sp", d_q[tt * 128:(tt + 1) * 128, :], q_tm[:, tt, :])
        d_k = self.ddump("k_tm", [T, D], BF16)
        if d_k is not None:
            for tt in range(TT):
                P.wait("sp", tk_r[tt])
                dma("sp", d_k[tt * 128:(tt + 1) * 128, :], k_tm[:, tt, :])
        self.barrier()
        if upto == "l0qk":
            return

        self.barrier()

        for h in range(16):
            pb = self.PB[h % 4]
            tmm = None
            for tt in range(TT):
                tmm = P.emit("pe", lambda E, h=h, tt=tt, pb=pb: E.matmul(
                    out=pb.ap()[0:64, 0:64],
                    lhsT=k_tm[:, tt, h * 64:(h + 1) * 64],
                    rhs=v_tm[:, tt, h * 64:(h + 1) * 64],
                    start=(tt == 0), stop=(tt == TT - 1)), "spe" if tt == TT - 1 else None)
            P.wait("dve", tmm)
            tev = op("dve", lambda E, h=h, pb=pb: E.tensor_copy(
                out=self.vkTf.ap()[(h % 2) * 64:(h % 2) * 64 + 64, (h // 2) * 65:(h // 2) * 65 + 64],
                in_=pb.ap()[0:64, 0:64]), mark=True)
            P.wait("pe", tev)
        P.wait("pe", self.t_ones)
        for g in range(2):
            pm = self.PM if g == 0 else self.PB[4]
            tks = None
            for tt in range(TT):
                tks = P.emit("pe", lambda E, g=g, tt=tt, pm=pm: E.matmul(
                    out=pm.ap()[0:1, 0:512], lhsT=self.onesb.ap()[:, 0:1], rhs=k_tm[:, tt, g * 512:(g + 1) * 512],
                    start=(tt == 0), stop=(tt == TT - 1)), "spe" if tt == TT - 1 else None)
            P.wait("act", tks)
            op("act", lambda E, g=g, pm=pm: E.copy(out=self.rowsc.ap()[0:1, g * 512:(g + 1) * 512], in_=pm.ap()[0:1, 0:512]), mark=(g == 1))
        t_ks = P.tok("sact")
        P.wait("sp", t_ks)
        tkd = dma("sp", self.ksumdr.unsqueeze(0), self.rowsc.ap()[0:1, 0:1024])
        P.wait("sp", tkd)
        P.wait("sp", P.tok("sdve"))
        tkl = dma("sp", self.vkTf.ap().rearrange("p (a b) -> p a b", a=8, b=65)[:, :, 64:65],
                  self.ksumdr.rearrange("(a p) -> p a", p=128).unsqueeze(2), allow_slow_non_contiguous=True)
        P.wait("sp", tkl)
        tvw = dma("sp", self.cc_vk_i, self.vkTf.ap())
        P.wait("gp", tvw)
        tcc = P.emit("gp", lambda E: E.collective_compute(
            "AllReduce", ALU.add, replica_groups=self.rg, ins=[self.cc_vk_i], outs=[self.cc_vk_o]), "scc", 1)
        self.ph_transpose_all(q_tm, q_fm, tq_r)
        P.wait("sp", tcc)
        tvr = dma("sp", self.vkTf.ap(), self.cc_vk_o)
        P.wait("dve", tvr)
        op("dve", lambda E: E.tensor_copy(out=self.vkTb.ap(), in_=self.vkTf.ap()), mark=True)
        d_vk = self.ddump("vkT", [128, 520])
        if d_vk is not None:
            P.wait("sp", P.tok("sdve"))
            dma("sp", d_vk, self.vkTf.ap())
        self.barrier()

        vkTb3 = self.vkTb.ap().rearrange("p (a b) -> p a b", a=8, b=65)
        st_ = self.statc.ap()
        for tt in range(TT):
            for pg in range(4):
                par, blk = pg % 2, pg // 2
                heads = [par + 2 * j + 8 * blk for j in range(4)]
                hb = par * 64
                pb = self.PB[pg]
                self.bank_wait(("na", pg))
                tmm = None
                for j, h in enumerate(heads):
                    tmm = P.emit("pe", lambda E, tt=tt, h=h, j=j, pb=pb, hb=hb: E.matmul(
                        out=pb.ap()[:, j * 65:j * 65 + 65],
                        lhsT=q_fm[hb:hb + 64, h // 2, tt * 128:(tt + 1) * 128],
                        rhs=vkTb3[hb:hb + 64, h // 2, :],
                        start=(j == 0), stop=(j == 3)), "spe" if j == 3 else None)
                P.wait("dve", tmm)
                p3 = pb.ap()[:, 0:260].rearrange("p (a c) -> p a c", c=65)
                op("dve", lambda E, p3=p3, pg=pg: E.tensor_scalar(
                    out=st_[:, 16 + 4 * pg:20 + 4 * pg], in0=p3[:, :, 64],
                    scalar1=self.cpool.ap()[:, 2:3], scalar2=None, op0=ALU.add))
                op("dve", lambda E, pg=pg: E.reciprocal(
                    out=st_[:, 16 + 4 * pg:20 + 4 * pg], in_=st_[:, 16 + 4 * pg:20 + 4 * pg]))
                dst3 = attn_tm[:, tt, :].rearrange("p (a c) -> p a c", c=128)[
                    :, 4 * blk:4 * blk + 4, par * 64:par * 64 + 64]
                rrep = st_[:, 16 + 4 * pg:20 + 4 * pg].unsqueeze(2).broadcast_to([128, 4, 64])
                tev = op("dve", lambda E, p3=p3, dst3=dst3, rrep=rrep: E.tensor_tensor(
                    out=dst3, in0=p3[:, :, 0:64], in1=rrep, op=ALU.mult), mark=True)
                self.bank_done(("na", pg), tev)
        self.barrier()
        d_at = self.ddump("attn_tm", [T, D], BF16)
        if d_at is not None:
            for tt in range(TT):
                dma("sp", d_at[tt * 128:(tt + 1) * 128, :], attn_tm[:, tt, :])

        self.ph_transpose_all(attn_tm, attn_fm)
        self.barrier()
        two = [dma("gp", wbuf[:, fc, :], self.wvo[fc * 128:(fc + 1) * 128, 1024:2048]) for fc in range(FC)]
        xstok = {}
        for tt in range(TT):
            bi = tt % 2
            xv = self.XS.ap() if bi == 0 else self.H0.ap()[:, 0:1024]
            lp = xstok.get(bi)
            if lp is not None:
                P.wait("sp", lp)
            tl = dma("sp", xv, self.x_in[tt * 128:(tt + 1) * 128, :])
            P.waits("pe", [two[-1]])
            tmm = None
            for g in range(2):
                pb = self.PB[2 * bi + g]
                self.bank_wait(2 * bi + g)
                for fc in range(FC):
                    tmm = P.emit("pe", lambda E, tt=tt, g=g, fc=fc, pb=pb: E.matmul(
                        out=pb.ap(), lhsT=attn_fm[:, fc, tt * 128:(tt + 1) * 128],
                        rhs=wbuf[:, fc, g * 512:(g + 1) * 512],
                        start=(fc == 0), stop=(fc == FC - 1)), "spe" if (g == 1 and fc == FC - 1) else None)
            P.waits("dve", [tmm, tl])
            op("dve", lambda E, bi=bi, xv=xv: E.tensor_tensor(out=xv[:, 0:512], in0=self.PB[2 * bi].ap(), in1=xv[:, 0:512], op=ALU.add))
            tev = op("dve", lambda E, bi=bi, xv=xv: E.tensor_tensor(out=xv[:, 512:1024], in0=self.PB[2 * bi + 1].ap(), in1=xv[:, 512:1024], op=ALU.add), mark=True)
            self.bank_done(2 * bi, tev)
            self.bank_done(2 * bi + 1, tev)
            P.waits("sp", [tev])
            xstok[bi] = dma("sp", self.xadr[tt * 128:(tt + 1) * 128, :], xv)
        self.barrier()
        d_ao = self.ddump("x_attn", [T, D])
        if d_ao is not None:
            for tt in range(TT):
                tl2 = dma("sp", self.XS.ap(), self.xadr[tt * 128:(tt + 1) * 128, :])
                P.wait("sp", tl2)
                t5 = dma("sp", d_ao[tt * 128:(tt + 1) * 128, :], self.XS.ap())
                P.wait("sp", t5)
        if upto == "l0attn":
            return

        TL = T // 2
        TTL = TL // 128
        NGL = TL // 512
        w1c = self.G4.ap()
        wq2T = self.v3(self.G2, 32, 512)
        d_hq = self.ddump("hq", [128, 32 * 256], BF16)
        for th in range(2):
            hq_a = self.v3(self.G0, 16, TL)
            hq_b = self.v3(self.G3, 16, TL)
            mm_prev = {}
            tr_prev = {}
            ev_tok = {}
            tq8_prev = {}
            sp_prev = {}

            def prep_w1(hc):
                w1v = w1c[:, (hc % 2) * 1024:(hc % 2) * 1024 + 1024]
                mp = mm_prev.get(hc % 2)
                if th == 1:
                    if mp is not None:
                        P.wait("gp", mp)
                    ev_tok[hc % 2] = dma("gp", w1v, self.w1tdr[:, hc * 1024:(hc + 1) * 1024])
                    return
                wb = self.wst.ap()[:, (hc % 2) * 1024:(hc % 2) * 1024 + 1024]
                tp = tr_prev.get(hc % 2)
                if tp is not None:
                    P.wait("gp", tp)
                tw = dma("gp", wb, self.w1[hc * 128:(hc + 1) * 128, :])
                P.wait("pe", tw)
                if mp is not None:
                    P.wait("act", mp)
                sp = sp_prev.get(hc % 2)
                if sp is not None:
                    P.wait("act", sp)
                ev_tok[hc % 2] = self.transp8([wb[:, fc * 128:(fc + 1) * 128] for fc in range(FC)],
                                              w1v.rearrange("p (a b) -> p a b", b=128))
                tr_prev[hc % 2] = P.tok("spe")
                P.wait("sp", ev_tok[hc % 2])
                sp_prev[hc % 2] = dma("sp", self.w1tdr[:, hc * 1024:(hc + 1) * 1024], w1v)

            for hci in range(33):
                if hci < 32:
                    prep_w1(hci)
                if hci == 0:
                    continue
                hc = hci - 1
                w1v = w1c[:, (hc % 2) * 1024:(hc % 2) * 1024 + 1024]
                h8h = self.h8t.ap()[:, (hc % 2) * 1024:(hc % 2) * 1024 + TL]
                P.wait("pe", ev_tok[hc % 2])
                t_prev = None
                tre = None
                for g in range(NGL):
                    bi = (hc * NGL + g) % 4
                    pb = self.PB[bi]
                    self.bank_wait(bi)
                    for fc in range(FC):
                        t_prev = P.emit("pe", lambda E, th=th, g=g, fc=fc, pb=pb, w1v=w1v: E.matmul(
                            out=pb.ap(), lhsT=w1v[:, fc * 128:(fc + 1) * 128],
                            rhs=xn_fm[:, fc, th * TL + g * 512:th * TL + (g + 1) * 512],
                            start=(fc == 0), stop=(fc == FC - 1)), "spe" if fc == FC - 1 else None)
                    P.wait("act", t_prev)
                    if g == 0:
                        t8p = tq8_prev.get(hc % 2)
                        if t8p is not None:
                            P.wait("act", t8p)
                    tre = op("act", lambda E, pb=pb, hc=hc, g=g, h8h=h8h: E.activation(
                        out=h8h[:, g * 512:(g + 1) * 512], in_=pb.ap(), func=AF.Relu,
                        bias=self.bcols.ap()[:, hc:hc + 1]), mark=True)
                    self.bank_done(bi, tre)
                mm_prev[hc % 2] = t_prev
                P.wait("dve", tre)
                dst = hq_a if hc < 16 else hq_b
                tq8 = op("dve", lambda E, dst=dst, hcl=hc % 16, h8h=h8h: E.tensor_copy(
                    out=dst[:, hcl, :], in_=h8h), mark=True)
                tq8_prev[hc % 2] = tq8
            self.barrier()
            if d_hq is not None and th == 0:
                for hc in range(32):
                    dst = hq_a if hc < 16 else hq_b
                    dma("sp", d_hq[:, hc * 256:(hc + 1) * 256], dst[:, hc % 16, 0:256])
                self.barrier()
            xstok = {}
            ftok = {}
            for half in range(2):
                if th == 0:
                    h8prev = {}
                    trprev = {}
                    for oc in range(4):
                        orow = half * 512 + oc * 128
                        for icg in range(8):
                            ti = oc * 8 + icg
                            hb8 = self.h8t.ap()[:, (ti % 2) * 512:(ti % 2) * 512 + 512]
                            wb = self.wst.ap()[:, (ti % 2) * 512:(ti % 2) * 512 + 512]
                            t8p = h8prev.get(ti % 2)
                            if t8p is not None:
                                P.wait("gp", t8p)
                            tw = dma("gp", hb8, self.w2[orow:orow + 128, icg * 512:(icg + 1) * 512])
                            P.wait("dve", tw)
                            tp9 = trprev.get(ti % 2)
                            if tp9 is not None:
                                P.wait("dve", tp9)
                            tq8 = op("dve", lambda E, wb=wb, hb8=hb8: E.tensor_copy(out=wb, in_=hb8), mark=True)
                            h8prev[ti % 2] = tq8
                            P.wait("pe", tq8)
                            self.transp8([wb[:, s2 * 128:(s2 + 1) * 128] for s2 in range(4)],
                                         wq2T[:, icg * 4:(icg + 1) * 4, oc * 128:(oc + 1) * 128])
                            trprev[ti % 2] = P.tok("spe")
                    P.wait("sp", P.tok("sact"))
                    dma("sp", self.wq2dr[half * 128:(half + 1) * 128, :], self.G2.ap()[:, 0:16384])
                    self.barrier()
                else:
                    if half == 1:
                        P.wait("gp", P.tok("spe"))
                    trl = dma("gp", self.G2.ap()[:, 0:16384], self.wq2dr[half * 128:(half + 1) * 128, :])
                    P.wait("pe", trl)
                for ttl in range(TTL):
                    tt = th * TTL + ttl
                    bi = ttl % 4
                    pb = self.PB[bi]
                    self.bank_wait(bi)
                    P.emit("pe", lambda E, pb=pb, half=half: E.matmul(
                        out=pb.ap(), lhsT=self.ones16.ap(), rhs=self.rowb2.ap()[0:1, half * 512:(half + 1) * 512],
                        start=True, stop=False), None)
                    tmm = None
                    for hc in range(32):
                        dst = hq_a if hc < 16 else hq_b
                        tmm = P.emit("pe", lambda E, ttl=ttl, hc=hc, pb=pb, dst=dst: E.matmul(
                            out=pb.ap(), lhsT=dst[:, hc % 16, ttl * 128:(ttl + 1) * 128],
                            rhs=wq2T[:, hc, :], start=False, stop=(hc == 31)), "spe" if hc == 31 else None)
                    src_x = self.xadr if half == 0 else self.x1dr
                    dst_x = self.x1dr if half == 0 else self.xadr
                    xv = self.XS.ap() if ttl % 2 == 0 else self.H0.ap()[:, 0:1024]
                    fv = self.f16t.ap()[:, (ttl % 2) * 512:(ttl % 2) * 512 + 512]
                    lp = xstok.get(ttl % 2)
                    if lp is not None:
                        P.wait("gp", lp)
                    tl = dma("gp", xv, src_x[tt * 128:(tt + 1) * 128, :])
                    P.wait("act", tmm)
                    fp = ftok.get(ttl % 2)
                    if fp is not None:
                        P.wait("act", fp)
                    tf = op("act", lambda E, pb=pb, fv=fv: E.copy(out=fv, in_=pb.ap()), mark=True)
                    self.bank_done(bi, tf)
                    P.waits("dve", [tf, tl])
                    tadd = op("dve", lambda E, half=half, xv=xv, fv=fv: E.tensor_tensor(
                        out=xv[:, half * 512:(half + 1) * 512], in0=fv,
                        in1=xv[:, half * 512:(half + 1) * 512], op=ALU.add), mark=True)
                    ftok[ttl % 2] = tadd
                    P.waits("sp", [tadd])
                    xstok[ttl % 2] = dma("sp", dst_x[tt * 128:(tt + 1) * 128, :], xv)
            self.barrier()
        d_x1 = self.ddump("x1", [T, D])
        if d_x1 is not None:
            for tt in range(TT):
                tl2 = dma("sp", self.XS.ap(), self.xadr[tt * 128:(tt + 1) * 128, :])
                P.wait("sp", tl2)
                t5 = dma("sp", d_x1[tt * 128:(tt + 1) * 128, :], self.XS.ap())
                P.wait("sp", t5)

    def build_l1(self, upto):
        P, op, dma = self.P, self.op, self.dma
        T, TT, TG = self.T, self.TT, self.TG
        NCH = T // 64
        xn1_tm = self.v3(self.G0, TT, 1024)
        xn1_fm = self.v3(self.G1, FC, T)
        xa_fm = self.v3(self.G2, FC, T)
        proj = self.v3(self.G3, 7, T)
        WT = self.v3(self.G4, FC, 896)
        qfm = self.v3(self.G0, 4, T)
        ktil = self.v3(self.G0, 4, T, off=4 * T)
        vtm = self.v3(self.G2, TT, 1024)
        khtm = self.v3(self.G4, 16, 512)
        ofm = self.v3(self.G3, FC, T)
        Qc = self.G3.ap()[0:64, 7 * T:7 * T + 512]
        Vc = self.G3.ap()[0:64, 7 * T + 512:7 * T + 1536]
        st = self.statc.ap()
        H0a = self.H0.ap()[:, 0:T]
        H0b = self.H0.ap()[:, 2048:2048 + T]

        op("dve", lambda E: E.memset(st[:, 30:31], 0.0))
        op("dve", lambda E: E.memset(self.Dsb.ap(), 0.0))
        op("dve", lambda E: E.memset(self.Esb.ap(), 0.0))
        op("dve", lambda E: E.memset(self.Msb.ap(), 0.0))
        op("dve", lambda E: E.memset(self.Csb.ap(), 0.0))
        P.wait("dve", self.t_b2q)
        op("dve", lambda E: E.memset(self.rowsc.ap()[0:1, 0:128], 1.0))
        dma("gp", self.bcols.ap()[:, 72:88], self.rmaskin)
        twtld = [dma("gp", WT[:, fc, :], self.l1wt[fc * 128:(fc + 1) * 128, :]) for fc in range(FC)]
        lnt = {}
        ftok = {}
        for tt in range(TT):
            xv = self.XS.ap() if tt % 2 == 0 else self.H0.ap()[:, 0:1024]
            w = st[:, 8:26] if tt % 2 == 0 else self.wlps.ap()[:, 0:18]
            lp = lnt.get(tt % 2)
            if lp is not None:
                P.wait("sp", lp)
            tl = dma("sp", xv, self.xadr[tt * 128:(tt + 1) * 128, :])
            P.wait("dve", tl)
            fo = ftok.get(tt % 2)
            if fo is not None:
                P.waits("dve", [fo, lnt[tt % 2]])
            op("dve", lambda E, xv=xv, w=w: E.bn_stats(out=w[:, 0:6], in_=xv[:, 0:512]))
            op("dve", lambda E, xv=xv, w=w: E.bn_stats(out=w[:, 6:12], in_=xv[:, 512:1024]))
            self.selfw("dve")
            t1 = op("dve", lambda E, w=w: E.bn_aggr(out=w[:, 12:14], in_=w[:, 0:12]))
            P.wait("act", t1)
            op("act", lambda E, w=w: E.activation(out=w[:, 14:15], in_=w[:, 13:14], func=AF.Ln, bias=self.cpool.ap()[:, 0:1]))
            self.selfw("act")
            t2 = op("act", lambda E, w=w: E.activation(out=w[:, 15:16], in_=w[:, 14:15], func=AF.Exp, scale=-0.5))
            P.wait("dve", t2)
            t3 = op("dve", lambda E, tt=tt, xv=xv, w=w: E.tensor_scalar(out=xn1_tm[:, tt, :], in0=xv,
                                                            scalar1=w[:, 12:13], scalar2=w[:, 15:16],
                                                            op0=ALU.subtract, op1=ALU.mult))
            lnt[tt % 2] = t3
            P.wait("dve", t3)
            op("dve", lambda E, tt=tt: E.tensor_reduce(out=st[:, 29:30], in_=xn1_tm[:, tt, :],
                                                       axis=mybir.AxisListType.X, op=ALU.max, apply_absolute_value=True))
            self.selfw("dve")
            tam = op("dve", lambda E: E.tensor_tensor(out=st[:, 30:31], in0=st[:, 30:31], in1=st[:, 29:30], op=ALU.max))
            op("dve", lambda E, w=w: E.tensor_tensor(out=w[:, 16:17], in0=w[:, 13:14], in1=w[:, 15:16], op=ALU.mult))
            self.selfw("dve")
            t4 = op("dve", lambda E, w=w: E.tensor_tensor(out=w[:, 16:17], in0=w[:, 16:17], in1=w[:, 15:16], op=ALU.mult))
            P.wait("act", t4)
            op("act", lambda E, w=w: E.activation(out=w[:, 17:18], in_=w[:, 16:17], func=AF.Ln, bias=self.cpool.ap()[:, 0:1]))
            self.selfw("act")
            tf = op("act", lambda E, w=w: E.activation(out=w[:, 17:18], in_=w[:, 17:18], func=AF.Exp, scale=-0.5))
            ftok[tt % 2] = tf
            P.wait("gp", t3)
            tsp = dma("gp", self.xn1dr[tt * 128:(tt + 1) * 128, :], xn1_tm[:, tt, :])
            P.wait("pe", t3)
            self.transp8([xn1_tm[:, tt, fc * 128:(fc + 1) * 128] for fc in range(FC)],
                         xn1_fm[:, :, tt * 128:(tt + 1) * 128])
            t_tr = P.tok("sact")
            P.waits("dve", [tsp, t_tr, tf])
            txa = op("dve", lambda E, tt=tt, w=w: E.tensor_scalar(out=xn1_tm[:, tt, :], in0=xn1_tm[:, tt, :],
                                                             scalar1=w[:, 17:18], scalar2=None, op0=ALU.mult))
            lnt[tt % 2] = txa
            P.wait("pe", txa)
            self.transp8([xn1_tm[:, tt, fc * 128:(fc + 1) * 128] for fc in range(FC)],
                         xa_fm[:, :, tt * 128:(tt + 1) * 128])
        self.barrier()

        tmm = P.emit("pe", lambda E: E.transpose(out=self.PM.ap()[0:1, 0:128], in_=st[:, 30:31], identity=self.identf.ap()), "spe", 1)
        P.wait("act", tmm)
        t5 = op("act", lambda E: E.copy(out=self.H0.ap()[0:1, 0:128], in_=self.PM.ap()[0:1, 0:128]))
        P.wait("dve", t5)
        op("dve", lambda E: E.memset(self.H0.ap()[0:1, 128:132], 0.0))
        self.selfw("dve")
        t6 = op("dve", lambda E: E.tensor_reduce(out=self.H0.ap()[0:1, 128:129], in_=self.H0.ap()[0:1, 0:128],
                                                 axis=mybir.AxisListType.X, op=ALU.max))
        P.wait("sp", t6)
        t7 = dma("sp", self.cc_a_i[0:1, 0:4], self.H0.ap()[0:1, 128:132])
        P.wait("gp", t7)
        tcc = P.emit("gp", lambda E: E.collective_compute("AllReduce", ALU.max, replica_groups=self.rg,
                                                          ins=[self.cc_a_i], outs=[self.cc_a_o]), "scc", 1)

        twt = []
        for fc in range(FC):
            P.wait("dve", twtld[fc])
            tg = op("dve", lambda E, fc=fc: E.tensor_scalar(out=WT[:, fc, :], in0=WT[:, fc, :],
                                                            scalar1=self.bcols.ap()[:, 40 + fc:41 + fc], scalar2=None, op0=ALU.mult))
            twt.append(tg)
        tbg = op("dve", lambda E: E.tensor_copy(out=self.wst.ap()[:, 0:8], in_=self.bcols.ap()[:, 56:64]))
        P.waits("pe", twt + [tbg, t5])
        for g in range(2):
            pm = self.PM if g == 0 else self.PB[4]
            nn = 512 if g == 0 else 384
            tmm = None
            for fc in range(FC):
                tmm = P.emit("pe", lambda E, g=g, fc=fc, pm=pm, nn=nn: E.matmul(
                    out=pm.ap()[0:1, 0:nn], lhsT=self.wst.ap()[:, fc:fc + 1], rhs=WT[:, fc, g * 512:g * 512 + nn],
                    start=(fc == 0), stop=(fc == FC - 1)), "spe" if fc == FC - 1 else None)
            P.wait("act", tmm)
            op("act", lambda E, g=g, pm=pm, nn=nn: E.copy(out=self.H0.ap()[0:1, 1024 + g * 512:1024 + g * 512 + nn], in_=pm.ap()[0:1, 0:nn]), mark=(g == 1))
        tb = P.tok("sact")
        P.wait("sp", tb)
        tbd = dma("sp", self.betadr.unsqueeze(0), self.H0.ap()[0:1, 1024:1920])
        P.wait("gp", tbd)
        tbl = dma("gp", self.bcols.ap()[:, 64:71], self.betadr.rearrange("(c p) -> p c", p=128), allow_slow_non_contiguous=True)
        twab = dma("gp", st[:, 0:4], self.l1wab.rearrange("(c p) -> p c", p=128), allow_slow_non_contiguous=True)
        P.waits("dve", [tbl, twab])
        tbl = op("dve", lambda E: E.tensor_tensor(out=self.bcols.ap()[:, 67:71], in0=self.bcols.ap()[:, 67:71], in1=st[:, 0:4], op=ALU.add))

        P.wait("sp", tcc)
        t8 = dma("sp", self.H0.ap()[0:1, 130:134], self.cc_a_o)
        P.waits("pe", [t8, P.tok("sdve"), P.tok("sact")])
        tmm = P.emit("pe", lambda E: E.matmul(out=self.PM.ap()[:, 0:1], lhsT=self.rowsc.ap()[0:1, 0:128],
                                              rhs=self.H0.ap()[0:1, 130:131], start=True, stop=True), "spe", 1)
        P.wait("dve", tmm)
        op("dve", lambda E: E.tensor_copy(out=st[:, 26:27], in_=self.PM.ap()[:, 0:1]))
        self.selfw("dve")
        op("dve", lambda E: E.tensor_scalar(out=st[:, 27:28], in0=st[:, 26:27], scalar1=float(1.0 / 32767.0),
                                            scalar2=self.cpool.ap()[:, 4:5], op0=ALU.mult, op1=ALU.max))
        self.selfw("dve")
        t9 = op("dve", lambda E: E.reciprocal(out=st[:, 28:29], in_=st[:, 27:28]))
        P.wait("pe", t9)

        for fc in range(FC):
            ta = op("dve", lambda E, fc=fc: E.tensor_scalar(out=H0a, in0=xn1_fm[:, fc, :], scalar1=st[:, 28:29],
                                                            scalar2=MAGIC, op0=ALU.mult, op1=ALU.add))
            P.wait("dve", ta)
            tb = op("dve", lambda E, fc=fc: E.tensor_scalar(out=xn1_fm[:, fc, :], in0=H0a, scalar1=MAGIC,
                                                            scalar2=st[:, 27:28], op0=ALU.subtract, op1=ALU.mult))
            P.wait("dve", tb)
        xq_fm = xn1_fm
        self.barrier()
        if upto == "l1ln":
            return

        P.waits("pe", [tbl])
        for oc in range(7):
            for g in range(TG):
                pb = self.PB[(oc * TG + g) % 4]
                tmm = None
                for fc in range(FC):
                    tmm = P.emit("pe", lambda E, oc=oc, g=g, fc=fc, pb=pb: E.matmul(
                        out=pb.ap(), lhsT=WT[:, fc, oc * 128:(oc + 1) * 128],
                        rhs=xa_fm[:, fc, g * 512:(g + 1) * 512],
                        start=(fc == 0), stop=(fc == FC - 1)), "spe" if fc == FC - 1 else None)
                P.wait("dve", tmm)
                tev = op("dve", lambda E, oc=oc, g=g, pb=pb: E.tensor_scalar(
                    out=proj[:, oc, g * 512:(g + 1) * 512], in0=pb.ap(),
                    scalar1=self.bcols.ap()[:, 64 + oc:65 + oc], scalar2=None, op0=ALU.add))
                P.wait("pe", tev)
        self.barrier()

        if upto == "l1proj":
            return
        tqc = dma("gp", Qc, self.l1qc)
        tvc = dma("gp", Vc, self.l1vc)

        P.waits("pe", [tqc, tvc])
        for h in range(8):
            hb = (h % 2) * 64
            for g in range(TG):
                pb = self.PB[g % 4]
                tmm = P.emit("pe", lambda E, h=h, g=g, pb=pb, hb=hb: E.matmul(
                    out=pb.ap()[hb:hb + 64, :], lhsT=Qc[:, h * 64:(h + 1) * 64],
                    rhs=proj[0:64, 0, g * 512:(g + 1) * 512], start=True, stop=True), "spe", 1)
                P.wait("dve", tmm)
                tev = op("dve", lambda E, h=h, g=g, pb=pb, hb=hb: E.tensor_copy(
                    out=qfm[hb:hb + 64, h // 2, g * 512:(g + 1) * 512], in_=pb.ap()[hb:hb + 64, :]))
                P.wait("pe", tev)
        for h in range(8):
            for tt in range(TT):
                pb = self.PB[tt % 4]
                tmm = P.emit("pe", lambda E, h=h, tt=tt, pb=pb: E.matmul(
                    out=pb.ap()[:, 0:128], lhsT=proj[0:64, 1, tt * 128:(tt + 1) * 128],
                    rhs=Vc[:, h * 128:(h + 1) * 128], start=True, stop=True), "spe", 1)
                P.wait("dve", tmm)
                tev = op("dve", lambda E, h=h, tt=tt, pb=pb: E.tensor_copy(
                    out=vtm[:, tt, h * 128:(h + 1) * 128], in_=pb.ap()[:, 0:128]))
                P.wait("pe", tev)
        op("act", lambda E: E.activation(out=self.H0.ap()[0:64, 0:T], in_=proj[0:64, 2, :], func=AF.Square))
        tk2 = op("act", lambda E: E.activation(out=self.H0.ap()[64:128, 0:T], in_=proj[64:128, 0, :], func=AF.Square))
        P.wait("pe", tk2)
        for g in range(TG):
            pb = self.PB[g % 4]
            tmm = P.emit("pe", lambda E, g=g, pb=pb: E.matmul(
                out=pb.ap()[0:1, :], lhsT=self.onesf.ap()[0:64, 0:1], rhs=self.H0.ap()[0:64, g * 512:(g + 1) * 512],
                start=True, stop=True), "spe", 1)
            P.wait("act", tmm)
            op("act", lambda E, g=g, pb=pb: E.copy(out=self.H0.ap()[0:1, 2048 + g * 512:2048 + (g + 1) * 512], in_=pb.ap()[0:1, :]), mark=(g == TG - 1))
        tss = P.tok("sact")
        P.wait("dve", tss)
        tq1 = op("dve", lambda E: E.tensor_scalar(out=self.H0.ap()[0:1, 2048:2048 + T], in0=self.H0.ap()[0:1, 2048:2048 + T],
                                                  scalar1=1e-24, scalar2=None, op0=ALU.max))
        P.wait("act", tq1)
        op("act", lambda E: E.activation(out=self.H0.ap()[0:1, 2048:2048 + T], in_=self.H0.ap()[0:1, 2048:2048 + T], func=AF.Ln))
        self.selfw("act")
        trn = op("act", lambda E: E.activation(out=self.H0.ap()[0:1, 2048:2048 + T], in_=self.H0.ap()[0:1, 2048:2048 + T], func=AF.Exp, scale=-0.5))
        P.wait("pe", trn)
        for g in range(TG):
            tmm = P.emit("pe", lambda E, g=g: E.matmul(
                out=self.PM.ap(), lhsT=self.rowsc.ap()[0:1, 0:128], rhs=self.H0.ap()[0:1, 2048 + g * 512:2048 + (g + 1) * 512],
                start=True, stop=True), "spe", 1)
            P.wait("dve", tmm)
            op("dve", lambda E, g=g: E.tensor_tensor(out=self.kndup.ap()[0:64, g * 512:(g + 1) * 512],
                                                     in0=proj[0:64, 2, g * 512:(g + 1) * 512], in1=self.PM.ap()[0:64, :], op=ALU.mult))
            self.selfw("dve")
            tev = op("dve", lambda E, g=g: E.tensor_tensor(out=self.kndup.ap()[64:128, g * 512:(g + 1) * 512],
                                                           in0=proj[64:128, 0, g * 512:(g + 1) * 512], in1=self.PM.ap()[64:128, :], op=ALU.mult))
            P.wait("pe", tev)
        self.barrier()
        d_kn = self.ddump("kndup", [128, T], BF16)
        if d_kn is not None:
            dma("sp", d_kn, self.kndup.ap()[:, 0:T])

        if upto == "l1heads":
            return
        for j in range(4):
            zj = proj[:, 3 + j, :]
            taz = op("act", lambda E, zj=zj: E.activation(out=H0a, in_=zj, func=AF.Abs))
            P.wait("act", taz)
            te1 = op("act", lambda E: E.activation(out=H0b, in_=H0a, func=AF.Exp, scale=-1.0))
            P.wait("act", te1)
            tl1 = op("act", lambda E: E.activation(out=H0b, in_=H0b, func=AF.Ln, bias=self.cpool.ap()[:, 1:2]))
            P.waits("dve", [taz, tl1])
            tlg = op("dve", lambda E, zj=zj: E.scalar_tensor_tensor(
                out=H0a, in0=zj, scalar=0.0, in1=H0b, op0=ALU.min, op1=ALU.subtract))
            P.wait("dve", tlg)
            tR = None
            for cc in range(NCH):
                tR = op("dve", lambda E, cc=cc: E.tensor_tensor_scan(
                    out=H0b[:, cc * 64:(cc + 1) * 64], data0=H0a[:, cc * 64:(cc + 1) * 64],
                    data1=H0a[:, cc * 64:(cc + 1) * 64], initial=0.0, op0=ALU.add, op1=ALU.bypass))
            P.waits("act", [tR])
            lastv = H0b.rearrange("p (c s) -> p c s", s=64)[:, :, 63]
            midv = H0b.rearrange("p (c s) -> p c s", s=64)[:, :, 31]
            tD = op("act", lambda E, j=j, lastv=lastv: E.activation(out=self.Dsb.ap()[:, j * 32:j * 32 + NCH], in_=lastv, func=AF.Exp))
            P.waits("dve", [tD])
            tRc = op("dve", lambda E, lastv=lastv, NCH=NCH: E.tensor_tensor_scan(
                out=st[:, 0:NCH], data0=lastv, data1=lastv, initial=0.0, op0=ALU.add, op1=ALU.bypass))
            tRc2 = op("dve", lambda E: E.tensor_scalar(out=st[:, 0:NCH], in0=st[:, 0:NCH], scalar1=-80.0, scalar2=None, op0=ALU.max))
            P.wait("act", tRc2)
            tC = op("act", lambda E, j=j: E.activation(out=self.Csb.ap()[:, j * 32:j * 32 + NCH], in_=st[:, 0:NCH], func=AF.Exp))
            P.wait("dve", tC)
            t_e = op("dve", lambda E, lastv=lastv, midv=midv: E.tensor_tensor(out=st[:, 0:NCH], in0=lastv, in1=midv, op=ALU.subtract))
            P.wait("act", t_e)
            op("act", lambda E, j=j: E.activation(out=self.Esb.ap()[:, j * 32:j * 32 + NCH], in_=st[:, 0:NCH], func=AF.Exp, bias=self.cpool.ap()[:, 6:7]))
            tM = op("act", lambda E, j=j, midv=midv: E.activation(out=self.Msb.ap()[:, j * 32:j * 32 + NCH], in_=midv, func=AF.Exp, bias=self.cpool.ap()[:, 6:7]))
            H0a3 = H0a.rearrange("p (c s) -> p c s", s=64)
            H0b3 = H0b.rearrange("p (c s) -> p c s", s=64)
            mrep = midv.unsqueeze(2).broadcast_to([128, NCH, 64])
            op("dve", lambda E, H0a3=H0a3, H0b3=H0b3, mrep=mrep: E.tensor_tensor(
                out=H0a3, in0=H0b3, in1=mrep, op=ALU.subtract))
            tclamp = op("dve", lambda E: E.tensor_scalar(out=H0a, in0=H0a, scalar1=66.0, scalar2=-66.0, op0=ALU.min, op1=ALU.max))
            P.wait("act", tclamp)
            texp = op("act", lambda E: E.activation(out=H0b, in_=H0a, func=AF.Exp, bias=self.cpool.ap()[:, 7:8]))
            P.wait("dve", texp)
            tqt = op("dve", lambda E, j=j: E.tensor_tensor(out=qfm[:, j, :], in0=qfm[:, j, :], in1=H0b, op=ALU.mult))
            P.wait("act", tqt)
            tem = op("act", lambda E: E.activation(out=H0b, in_=H0a, func=AF.Exp, scale=-1.0, bias=self.cpool.ap()[:, 7:8]))
            P.wait("dve", tem)
            tkt = op("dve", lambda E, j=j: E.tensor_tensor(out=ktil[:, j, :], in0=self.kndup.ap()[:, 0:T], in1=H0b, op=ALU.mult))
            pj3 = proj[:, 3 + j, :].rearrange("p (c s) -> p c s", s=64)
            kt3 = ktil[:, j, :].rearrange("p (c s) -> p c s", s=64)
            erep = self.Esb.ap()[:, j * 32:j * 32 + NCH].unsqueeze(2).broadcast_to([128, NCH, 64])
            op("dve", lambda E, pj3=pj3, kt3=kt3, erep=erep: E.tensor_tensor(
                out=pj3, in0=kt3, in1=erep, op=ALU.mult))
            tkh = P.tok("sdve")
            P.waits("act", [tkh])
            P.waits("pe", [tkh])
        self.barrier()

        d_qt = self.ddump("qt", [128, 4 * T], BF16)
        if d_qt is not None:
            dma("sp", d_qt, self.G0.ap()[:, 0:4 * T])
        d_kt = self.ddump("kt", [128, 4 * T], BF16)
        if d_kt is not None:
            dma("sp", d_kt, self.G0.ap()[:, 4 * T:8 * T])
        d_dec = self.ddump("dec", [128, 5 * 128])
        if d_dec is not None:
            dma("sp", d_dec[:, 0:128], self.Dsb.ap())
            dma("sp", d_dec[:, 128:256], self.Esb.ap())
            dma("sp", d_dec[:, 256:384], self.Msb.ap())
            dma("sp", d_dec[:, 384:512], self.Csb.ap())
        d_vt = self.ddump("vt", [128, TT * 1024], BF16)
        if d_vt is not None:
            dma("sp", d_vt, self.G2.ap()[:, 0:TT * 1024])
        if upto == "l1gate":
            return
        for tt in range(TT):
            self.transp8([proj[:, 3 + j, tt * 128:(tt + 1) * 128] for j in range(4)],
                         khtm[:, tt, 0:512].rearrange("p (a b) -> p a b", b=128))
        self.barrier()
        if upto == "l1khtm":
            return

        def state_pass(collect_o, nch=None):
            stt_last = None
            ti1_last = None
            intra_last = {}
            comb_last = None
            mask_tok = {}
            for cc in range(NCH if nch is None else nch):
                b2 = (cc % 2) * 64
                par = cc % 2
                tc = cc // 2
                t_sb16 = None
                if collect_o:
                    if ti1_last is not None:
                        P.wait("dve", ti1_last)
                    for j9 in range(4):
                        t_sb16 = op("dve", lambda E, j9=j9, cc=cc: E.tensor_scalar(
                            out=self.wst.ap()[:, 512 + j9 * 128:512 + (j9 + 1) * 128],
                            in0=self.Ssb.ap()[:, j9 * 128:(j9 + 1) * 128],
                            scalar1=self.Msb.ap()[:, j9 * 32 + cc:j9 * 32 + cc + 1], scalar2=None, op0=ALU.mult))
                if stt_last is not None:
                    P.wait("pe", stt_last)
                tU = None
                for par2 in range(2):
                    for j2 in range(4):
                        h = 2 * j2 + par2
                        tU = P.emit("pe", lambda E, h=h, par2=par2, j2=j2, b2=b2, tc=tc: E.matmul(
                            out=self.PM.ap()[par2 * 64:par2 * 64 + 64, j2 * 128:(j2 + 1) * 128],
                            lhsT=khtm[b2:b2 + 64, tc, h * 64:(h + 1) * 64],
                            rhs=vtm[b2:b2 + 64, tc, h * 128:(h + 1) * 128],
                            start=(j2 == 0), stop=(j2 == 3)), "spe" if j2 == 3 else None)
                if collect_o:
                    tsce = {}
                    for par2 in range(2):
                        hb = par2 * 64
                        bankS = self.PB[4 - par2]
                        mt = mask_tok.get((par, par2))
                        if mt is not None:
                            P.wait("pe", mt)
                        tt2 = None
                        for j2 in range(4):
                            tt2 = P.emit("pe", lambda E, j2=j2, hb=hb, b2=b2, cc=cc, bankS=bankS: E.matmul(
                                out=bankS.ap()[b2:b2 + 64, j2 * 64:(j2 + 1) * 64],
                                lhsT=ktil[hb:hb + 64, j2, cc * 64:(cc + 1) * 64],
                                rhs=qfm[hb:hb + 64, j2, cc * 64:(cc + 1) * 64],
                                start=(j2 == 0), stop=(j2 == 3)), "spe" if j2 == 3 else None)
                        tsce[par2] = tt2
                    tmsk = {}
                    for par2 in range(2):
                        bankS = self.PB[4 - par2]
                        P.wait("dve", tsce[par2])
                        it = intra_last.get(par)
                        if it is not None and par2 == 0:
                            P.wait("dve", it)
                        wsl = self.wst.ap()[b2:b2 + 64, par2 * 256:par2 * 256 + 256].rearrange(
                            "p (a c) -> p a c", c=64)
                        p4v = bankS.ap()[b2:b2 + 64, 0:256].rearrange("p (a c) -> p a c", c=64)
                        mv = self.maskt.ap()[b2:b2 + 64, :].unsqueeze(1).broadcast_to([64, 4, 64])
                        tmsk[par2] = op("dve", lambda E, wsl=wsl, p4v=p4v, mv=mv: E.tensor_tensor(
                            out=wsl, in0=p4v, in1=mv, op=ALU.mult))
                        mask_tok[(par, par2)] = tmsk[par2]
                    P.wait("dve", tU)
                    for j2 in range(4):
                        stt_last = op("dve", lambda E, j2=j2, cc=cc: E.scalar_tensor_tensor(
                            out=self.Ssb.ap()[:, j2 * 128:(j2 + 1) * 128],
                            in0=self.Ssb.ap()[:, j2 * 128:(j2 + 1) * 128],
                            scalar=self.Dsb.ap()[:, j2 * 32 + cc:j2 * 32 + cc + 1],
                            in1=self.PM.ap()[:, j2 * 128:(j2 + 1) * 128], op0=ALU.mult, op1=ALU.add))
                    P.wait("pe", t_sb16)
                    if comb_last is not None:
                        P.wait("pe", comb_last)
                    ti_o = {}
                    for par2 in range(2):
                        hb = par2 * 64
                        bankA = self.PB[par2]
                        tx2 = None
                        for j2 in range(4):
                            tx2 = P.emit("pe", lambda E, j2=j2, hb=hb, cc=cc, bankA=bankA: E.matmul(
                                out=bankA.ap()[:, j2 * 64:(j2 + 1) * 64],
                                lhsT=self.wst.ap()[hb:hb + 64, 512 + j2 * 128:512 + (j2 + 1) * 128],
                                rhs=qfm[hb:hb + 64, j2, cc * 64:(cc + 1) * 64],
                                start=(j2 == 0), stop=(j2 == 3)), "spe" if j2 == 3 else None)
                        ti_o[par2] = tx2
                    ti1_last = ti_o[1]
                    P.waits("pe", [tmsk[0], tmsk[1]])
                    tin = None
                    for h in range(8):
                        woff = (h % 2) * 256 + (h // 2) * 64
                        tin = P.emit("pe", lambda E, h=h, b2=b2, tc=tc, woff=woff: E.matmul(
                            out=self.PB[2].ap()[:, h * 64:(h + 1) * 64],
                            lhsT=vtm[b2:b2 + 64, tc, h * 128:(h + 1) * 128],
                            rhs=self.wst.ap()[b2:b2 + 64, woff:woff + 64],
                            start=(h == 0), stop=(h == 7)), "spe" if h == 7 else None)
                    intra_last[par] = tin
                    of4 = ofm[:, :, cc * 64:(cc + 1) * 64].rearrange("p (a b) c -> p a b c", b=2)
                    pb2v = self.PB[2].ap()[:, 0:512].rearrange("p (a b c) -> p a b c", b=2, c=64)
                    P.wait("act", ti_o[0])
                    ta0 = op("act", lambda E, of4=of4: E.copy(
                        out=of4[:, :, 0, :], in_=self.PB[0].ap()[:, 0:256].rearrange("p (a c) -> p a c", c=64)))
                    P.wait("act", ti_o[1])
                    ta1 = op("act", lambda E, of4=of4: E.copy(
                        out=of4[:, :, 1, :], in_=self.PB[1].ap()[:, 0:256].rearrange("p (a c) -> p a c", c=64)))
                    P.waits("dve", [ta0, tin])
                    op("dve", lambda E, of4=of4, pb2v=pb2v: E.scalar_tensor_tensor(
                        out=of4[:, :, 0, :], in0=pb2v[:, :, 0, :], scalar=1.0,
                        in1=of4[:, :, 0, :], op0=ALU.mult, op1=ALU.add))
                    P.wait("dve", ta1)
                    comb_last = op("dve", lambda E, of4=of4, pb2v=pb2v: E.scalar_tensor_tensor(
                        out=of4[:, :, 1, :], in0=pb2v[:, :, 1, :], scalar=1.0,
                        in1=of4[:, :, 1, :], op0=ALU.mult, op1=ALU.add))
                if not collect_o:
                    P.wait("dve", tU)
                    for j2 in range(4):
                        stt_last = op("dve", lambda E, j2=j2, cc=cc: E.scalar_tensor_tensor(
                            out=self.Ssb.ap()[:, j2 * 128:(j2 + 1) * 128],
                            in0=self.Ssb.ap()[:, j2 * 128:(j2 + 1) * 128],
                            scalar=self.Dsb.ap()[:, j2 * 32 + cc:j2 * 32 + cc + 1],
                            in1=self.PM.ap()[:, j2 * 128:(j2 + 1) * 128], op0=ALU.mult, op1=ALU.add))
            P.waits("dve", [stt_last])
            P.waits("pe", [P.tok("sdve")])

        if upto == "l1u":
            for cc in range(NCH):
                b2 = (cc % 2) * 64
                tc = cc // 2
                for j2 in range(4):
                    pmU = self.PM.ap()[:, (j2 % 2) * 128:(j2 % 2) * 128 + 128]
                    tU = None
                    for hh in range(2):
                        h = j2 * 2 + hh
                        hb = (h % 2) * 64
                        tU = P.emit("pe", lambda E, h=h, hb=hb, b2=b2, tc=tc, pmU=pmU: E.matmul(
                            out=pmU[hb:hb + 64, :], lhsT=khtm[b2:b2 + 64, tc, h * 64:(h + 1) * 64],
                            rhs=vtm[b2:b2 + 64, tc, h * 128:(h + 1) * 128], start=True, stop=True), "spe", 1)
                    P.wait("dve", tU)
                    tcp = op("dve", lambda E, j2=j2, cc=cc, pmU=pmU: E.scalar_tensor_tensor(
                        out=self.Ssb.ap()[:, j2 * 128:(j2 + 1) * 128], in0=self.Ssb.ap()[:, j2 * 128:(j2 + 1) * 128],
                        scalar=self.Dsb.ap()[:, j2 * 32 + cc:j2 * 32 + cc + 1],
                        in1=pmU, op0=ALU.mult, op1=ALU.add))
                    P.wait("pe", tcp)
            self.barrier()
            return
        tz = op("dve", lambda E: E.memset(self.Ssb.ap(), 0.0))
        P.wait("pe", tz)
        if upto.startswith("l1sp"):
            state_pass(collect_o=(upto == "l1spo"), nch=4)
            self.barrier()
            return
        state_pass(collect_o=False)
        self.barrier()
        if upto == "l1scan0":
            return

        d_m = self.ddump("mcore", [128, 512])
        if d_m is not None:
            dma("sp", d_m, self.Ssb.ap())
        t1 = dma("sp", self.cc_g_i[:, 0:512], self.Ssb.ap())
        t2 = dma("sp", self.cc_g_i[:, 512:516], self.Csb.ap().rearrange("p (j c) -> p j c", c=32)[:, :, NCH - 1], allow_slow_non_contiguous=True)
        P.wait("gp", t2)
        tcc = P.emit("gp", lambda E: E.collective_compute("AllGather", ALU.bypass, replica_groups=self.rg,
                                                          ins=[self.cc_g_i], outs=[self.cc_g_o]), "scc", 1)
        P.wait("dve", tcc)
        P.wait("sp", tcc)
        tz = op("dve", lambda E: E.memset(self.Ssb.ap(), 0.0))
        nco = len(self.rg[0])
        for r in range(nco):
            tgl = dma("sp", self.Gsb.ap(), self.cc_g_o[r * 128:(r + 1) * 128, :])
            P.waits("dve", [tgl, tz])
            tdp = op("dve", lambda E, r=r: E.tensor_scalar(out=st[:, 0:4], in0=self.Gsb.ap()[:, 512:516],
                                                           scalar1=self.bcols.ap()[:, 72 + r:73 + r],
                                                           scalar2=self.bcols.ap()[:, 80 + r:81 + r],
                                                           op0=ALU.mult, op1=ALU.add))
            tmp = op("dve", lambda E, r=r: E.tensor_scalar(out=self.Gsb.ap()[:, 0:512], in0=self.Gsb.ap()[:, 0:512],
                                                           scalar1=self.bcols.ap()[:, 72 + r:73 + r], scalar2=None, op0=ALU.mult))
            P.waits("dve", [tdp, tmp])
            tz = None
            for j2 in range(4):
                tz = op("dve", lambda E, j2=j2: E.scalar_tensor_tensor(
                    out=self.Ssb.ap()[:, j2 * 128:(j2 + 1) * 128], in0=self.Ssb.ap()[:, j2 * 128:(j2 + 1) * 128],
                    scalar=st[:, j2:j2 + 1], in1=self.Gsb.ap()[:, j2 * 128:(j2 + 1) * 128], op0=ALU.mult, op1=ALU.add))
                P.wait("dve", tz)
            P.wait("sp", tz)
        self.barrier()

        if upto == "l1scan1":
            return
        state_pass(collect_o=True)
        self.barrier()
        d_gla = self.ddump("gla", [T, D], BF16)
        if d_gla is not None:
            for ocx in range(8):
                dma("sp", d_gla[ocx * 128:(ocx + 1) * 128, 0:1024] if T >= 1024 else d_gla[ocx * 128:(ocx + 1) * 128, 0:T], ofm[:, ocx, 0:1024 if T >= 1024 else T])
            self.barrier()

        two = [dma("gp", self.v3(self.G4, FC, 1024)[:, fc, :], self.l1wo[fc * 128:(fc + 1) * 128, :]) for fc in range(FC)]
        wo1b = self.v3(self.G4, FC, 1024)
        xstok = {}
        for tt in range(TT):
            bi = tt % 2
            xv = self.XS.ap() if bi == 0 else self.H0.ap()[:, 0:1024]
            wv = self.wst.ap()[:, bi * 1024:bi * 1024 + 1024]
            lp = xstok.get(bi)
            if lp is not None:
                P.wait("sp", lp)
            tl = dma("sp", xv, self.xadr[tt * 128:(tt + 1) * 128, :])
            tn = dma("sp", wv, self.xn1dr[tt * 128:(tt + 1) * 128, :])
            tmm = None
            P.waits("pe", [two[-1]])
            for g in range(2):
                pb = self.PB[2 * bi + g]
                self.bank_wait(2 * bi + g)
                for fc in range(FC):
                    tmm = P.emit("pe", lambda E, tt=tt, g=g, fc=fc, pb=pb: E.matmul(
                        out=pb.ap(), lhsT=ofm[:, fc, tt * 128:(tt + 1) * 128],
                        rhs=wo1b[:, fc, g * 512:(g + 1) * 512],
                        start=(fc == 0), stop=(fc == FC - 1)), "spe" if (g == 1 and fc == FC - 1) else None)
            P.waits("dve", [tmm, tl, tn])
            op("dve", lambda E, xv=xv, wv=wv: E.tensor_tensor(out=xv, in0=xv, in1=wv, op=ALU.add))
            op("dve", lambda E, bi=bi, xv=xv: E.tensor_tensor(out=xv[:, 0:512], in0=self.PB[2 * bi].ap(), in1=xv[:, 0:512], op=ALU.add))
            tev = op("dve", lambda E, bi=bi, xv=xv: E.tensor_tensor(out=xv[:, 512:1024], in0=self.PB[2 * bi + 1].ap(), in1=xv[:, 512:1024], op=ALU.add))
            self.bank_done(2 * bi, tev)
            self.bank_done(2 * bi + 1, tev)
            P.waits("sp", [tev])
            xstok[bi] = dma("sp", self.x1dr[tt * 128:(tt + 1) * 128, :], xv)
        self.barrier()
        d_x2p = self.ddump("x2p", [T, D])
        if d_x2p is not None:
            for tt in range(TT):
                tl2 = dma("sp", self.XS.ap(), self.x1dr[tt * 128:(tt + 1) * 128, :])
                P.wait("sp", tl2)
                t5 = dma("sp", d_x2p[tt * 128:(tt + 1) * 128, :], self.XS.ap())
                P.wait("sp", t5)
        if upto == "l1attn":
            return
        self.build_l1ffn(upto)

    def build_l1ffn(self, upto="full"):
        P, op, dma = self.P, self.op, self.dma
        T, TT, TG = self.T, self.TT, self.TG
        st = self.statc.ap()
        xq_fm = self.v3(self.G1, FC, T)
        H0f = self.H0.ap()
        gv = [self.v3(self.G0, 8, T), self.v3(self.G2, 8, T), self.v3(self.G3, 8, T)]

        def gch(c):
            return gv[c // 8][:, c % 8, :]

        ubf = self.kndup.ap()[:, 0:2 * T]
        MQ = float(3 << 22)
        XA = mybir.AxisListType.X
        cb = 4096
        W = T // 2

        def Rg(i):
            return H0f[:, i * W:(i + 1) * W]

        op("dve", lambda E: E.memset(st[:, 0:2], 0.0))

        wtok = {}

        def prep_tile(wrows, b, dst3, dst_wait=None, src_b=None, dq="gp"):
            if src_b is None:
                src_b = b
            srcv = H0f[:, src_b * 1024:(src_b + 1) * 1024] if src_b < 2 else self.XS.ap()
            src3 = srcv.rearrange("p (k i) -> p k i", i=32)
            q2 = H0f[:, 2048 + b * 1024:2048 + (b + 1) * 1024]
            q3 = q2.rearrange("p (k i) -> p k i", i=32)
            sc = self.wlps.ap()[:, b * 32:(b + 1) * 32]
            rc = self.wlpr.ap()[:, b * 32:(b + 1) * 32]
            tr = wtok.get(("src", src_b))
            if tr is not None:
                P.wait(dq, tr)
            tw = dma(dq, srcv, wrows)
            P.wait("dve", tw)
            op("dve", lambda E, src3=src3, sc=sc: E.tensor_reduce(
                out=sc.unsqueeze(2), in_=src3, axis=XA, op=ALU.max, apply_absolute_value=True))
            op("dve", lambda E, sc=sc: E.tensor_scalar(
                out=sc, in0=sc, scalar1=float(1.0 / 7.0), scalar2=self.cpool.ap()[:, 4:5],
                op0=ALU.mult, op1=ALU.max))
            op("dve", lambda E, sc=sc, rc=rc: E.reciprocal(out=rc, in_=sc))
            tq = wtok.get(("q", b))
            if tq is not None:
                P.wait("dve", tq)
            rrep = rc.unsqueeze(2).broadcast_to([128, 32, 32])
            srep = sc.unsqueeze(2).broadcast_to([128, 32, 32])
            t4 = op("dve", lambda E, q3=q3, src3=src3, rrep=rrep: E.tensor_tensor(
                out=q3, in0=src3, in1=rrep, op=ALU.mult))
            wtok[("src", src_b)] = t4
            P.wait("act", t4)
            t5 = op("act", lambda E, q2=q2: E.activation(out=q2, in_=q2, func=AF.Copy, bias=MQ))
            P.wait("dve", t5)
            t6 = op("dve", lambda E, q3=q3, srep=srep: E.scalar_tensor_tensor(
                out=q3, in0=q3, scalar=MQ, in1=srep, op0=ALU.subtract, op1=ALU.mult))
            P.wait("pe", t6)
            tmm = None
            tev = None
            for half in range(2):
                bank = self.PM if half == 0 else self.PB[4]
                key = ("wq", half)
                self.bank_wait(key)
                for i in range(4):
                    fc = half * 4 + i
                    pv = bank.ap()[:, i * 128:(i + 1) * 128]
                    tmm = P.emit("pe", lambda E, fc=fc, pv=pv, q2=q2, st_=(i == 0), sp_=(i == 3): E.matmul(
                        out=pv, lhsT=q2[:, fc * 128:(fc + 1) * 128], rhs=self.identf.ap(),
                        start=st_, stop=sp_, is_transpose=True), "spe" if i == 3 else None)
                P.wait("act", tmm)
                if dst_wait is not None and half == 0:
                    P.wait("act", dst_wait)
                bv = bank.ap()[:, 0:512].rearrange("p (a b) -> p a b", b=128)
                tev = op("act", lambda E, dst3=dst3, half=half, bv=bv: E.copy(
                    out=dst3[:, half * 4:(half + 1) * 4, :], in_=bv))
                self.bank_done(key, tev)
            wtok[("q", b)] = tmm
            return tev

        def mm_tile(b, which, oc, parts=15):
            wb = self.wst.ap()[:, b * 1024:(b + 1) * 1024]
            tprep = wtok.get(("wstready", b))
            if tprep is not None:
                P.wait("pe", tprep)
            last = None
            for g in range(TG):
                pb = self.PB[g % 4]
                self.bank_wait(g % 4)
                tmm = None
                for fc in range(FC):
                    tmm = P.emit("pe", lambda E, g=g, fc=fc, pb=pb, wb=wb: E.matmul(
                        out=pb.ap(), lhsT=wb[:, fc * 128:(fc + 1) * 128],
                        rhs=xq_fm[:, fc, g * 512:(g + 1) * 512],
                        start=(fc == 0), stop=(fc == FC - 1)), "spe" if fc == FC - 1 else None)
                last = tmm
                tev = trd = None
                if parts & 2:
                    P.wait("act", tmm)
                    if which == 0:
                        if g == 0:
                            tub = wtok.get(("ubf", oc % 2))
                            if tub is not None:
                                P.waits("act", tub)
                        dstv = ubf[:, (oc % 2) * T + g * 512:(oc % 2) * T + (g + 1) * 512]
                    else:
                        dstv = gch(oc)[:, g * 512:(g + 1) * 512]
                    tev = op("act", lambda E, dstv=dstv, pb=pb: E.copy(out=dstv, in_=pb.ap()))
                if parts & 4:
                    P.wait("dve", tev)
                    trd = op("dve", lambda E, dstv=dstv: E.tensor_reduce(
                        out=st[:, 2:3], in_=dstv, axis=XA, op=ALU.max, apply_absolute_value=True))
                    op("dve", lambda E, which=which: E.tensor_tensor(
                        out=st[:, which:which + 1], in0=st[:, which:which + 1], in1=st[:, 2:3], op=ALU.max))
                self.bank_done(g % 4, [tev if tev is not None else tmm])
                if parts & 8 and which == 0 and g == TG - 1:
                    sq = "gp" if oc % 2 == 0 else "sp"
                    P.wait(sq, tev)
                    tst = dma(sq, self.updr[oc * 128:(oc + 1) * 128, 0:T],
                              ubf[:, (oc % 2) * T:(oc % 2) * T + T])
                    wtok[("ubf", oc % 2)] = [tst, trd] if trd is not None else [tst]
            return last

        wsrcs = [self.wup, self.wgate]
        order = []
        for oc in range(24):
            order.append((0, oc))
            order.append((1, oc))
        wstmm = {}
        prev = None
        lim = int(upto[4:]) if upto.startswith("l1fu") else None
        parts = 15
        if upto.startswith("l1fm"):
            lim, parts = 2, int(upto[4:])
        for i, (which, oc) in enumerate(order):
            if lim is not None and i >= lim:
                return
            b = i % 2
            dst3 = self.wst.ap()[:, b * 1024:(b + 1) * 1024].rearrange("p (a c) -> p a c", c=128)
            wtok[("wstready", b)] = prep_tile(wsrcs[which][oc * 128:(oc + 1) * 128, :], b, dst3,
                                              dst_wait=wstmm.get(b), src_b=i % 3,
                                              dq="gp" if i % 2 == 0 else "sp")
            if prev is not None:
                pw, po, pb_ = prev
                wstmm[pb_] = mm_tile(pb_, pw, po, parts)
            prev = (which, oc, b)
        pw, po, pb_ = prev
        tok_ulast = mm_tile(pb_, pw, po)
        if upto == "l1f0":
            return

        def amax_part1(cols):
            P.wait("pe", P.tok("sdve"))
            tz9 = op("dve", lambda E: E.memset(H0f[0:1, cb + 136:cb + 140], 0.0))
            t5 = None
            self.bank_wait(("wq", 0))
            for i, c in enumerate(cols):
                tmm = P.emit("pe", lambda E, c=c: E.transpose(
                    out=self.PM.ap()[0:1, 0:128], in_=st[:, c:c + 1], identity=self.identf.ap()), "spe", 1)
                P.wait("act", tmm)
                t5 = op("act", lambda E, i=i: E.copy(
                    out=H0f[0:1, cb + 140 + i * 256:cb + 140 + i * 256 + 128], in_=self.PM.ap()[0:1, 0:128]))
                P.waits("dve", [t5, tz9])
                t6 = op("dve", lambda E, i=i: E.tensor_reduce(
                    out=H0f[0:1, cb + 136 + i:cb + 137 + i],
                    in_=H0f[0:1, cb + 140 + i * 256:cb + 140 + i * 256 + 128], axis=XA, op=ALU.max))
                P.wait("sp", t6)
                P.wait("pe", t6)
            self.bank_done(("wq", 0), t5)
            t7 = dma("sp", self.cc_a_i[0:1, 0:4], H0f[0:1, cb + 136:cb + 140])
            P.wait("gp", t7)
            tcc = P.emit("gp", lambda E: E.collective_compute(
                "AllReduce", ALU.max, replica_groups=self.rg, ins=[self.cc_a_i], outs=[self.cc_a_o]), "scc", 1)
            return tcc

        def amax_part2(cols, outbase, tcc):
            P.wait("sp", tcc)
            t8 = dma("sp", H0f[0:1, cb + 130:cb + 130 + 4], self.cc_a_o)
            trc = None
            for i in range(len(cols)):
                P.waits("pe", [t8, P.tok("sdve"), P.tok("sact")])
                self.bank_wait(("wq", 0))
                tmm = P.emit("pe", lambda E, i=i: E.matmul(
                    out=self.PM.ap()[:, 0:1], lhsT=self.rowsc.ap()[0:1, 0:128],
                    rhs=H0f[0:1, cb + 130 + i:cb + 131 + i], start=True, stop=True), "spe", 1)
                P.wait("dve", tmm)
                tcp = op("dve", lambda E, i=i, outbase=outbase: E.tensor_scalar(
                    out=st[:, outbase + 2 * i:outbase + 2 * i + 1], in0=self.PM.ap()[:, 0:1],
                    scalar1=float(1.0 / 32767.0), scalar2=self.cpool.ap()[:, 4:5], op0=ALU.mult, op1=ALU.max))
                trc = op("dve", lambda E, i=i, outbase=outbase: E.reciprocal(
                    out=st[:, outbase + 2 * i + 1:outbase + 2 * i + 2],
                    in_=st[:, outbase + 2 * i:outbase + 2 * i + 1]))
                self.bank_done(("wq", 0), tcp)
                P.wait("pe", trc)
            return trc

        tcc1 = amax_part1([0, 1])

        if 8 * T >= 16384:
            bases = [(self.G1, 0), (self.G1, 8192), (self.G4, 0)]
        else:
            bases = [(self.G1, 0), (self.G4, 0), (self.G0, 8 * T)]
        sgv = [self.v3(t, 8, 1024, off=o) for (t, o) in bases]

        def wdT(ic):
            return sgv[ic // 8][:, ic % 8, :]

        for p in range(3):
            for rt in range(8):
                i9 = 48 + p * 8 + rt
                dst3 = sgv[p][:, :, rt * 128:(rt + 1) * 128]
                prep_tile(self.wdown[rt * 128:(rt + 1) * 128, p * 1024:(p + 1) * 1024],
                          i9 % 2, dst3,
                          dst_wait=tok_ulast if p * 8 + rt < 2 else None, src_b=i9 % 3,
                          dq="gp" if i9 % 2 == 0 else "sp")
        amax_part2([0, 1], 8, tcc1)
        if upto == "l1f1":
            return

        t17 = op("dve", lambda E: E.tensor_scalar(out=st[:, 17:18], in0=st[:, 10:11],
                                                  scalar1=-MQ, scalar2=None, op0=ALU.mult))
        op("dve", lambda E: E.memset(st[:, 3:4], 0.0))
        P.wait("act", t17)
        chtok = {}
        for ci in range(48):
            c, hh = ci // 2, ci % 2
            gsl = gch(c)[:, hh * W:(hh + 1) * W]
            Xv, Yv = Rg(hh), Rg(2 + hh)
            tx = chtok.get(hh)
            if tx is not None:
                P.wait("act", tx)
            op("act", lambda E, gsl=gsl, Xv=Xv: E.activation(
                out=Xv, in_=gsl, func=AF.Copy, scale=st[:, 11:12], bias=MQ))
            t2 = op("act", lambda E, Xv=Xv, Yv=Yv: E.activation(
                out=Yv, in_=Xv, func=AF.Sigmoid, scale=st[:, 10:11], bias=st[:, 17:18]))
            t3 = op("act", lambda E, Yv=Yv: E.activation(
                out=Yv, in_=Yv, func=AF.Copy, scale=65536.0, bias=MQ))
            P.wait("dve", t2)
            op("dve", lambda E, Xv=Xv: E.tensor_scalar(
                out=Xv, in0=Xv, scalar1=MQ, scalar2=None, op0=ALU.subtract))
            P.wait("dve", t3)
            op("dve", lambda E, Yv=Yv: E.tensor_scalar(
                out=Yv, in0=Yv, scalar1=MQ, scalar2=65535.0, op0=ALU.subtract, op1=ALU.min))
            t6 = op("dve", lambda E, gsl=gsl, Xv=Xv, Yv=Yv: E.scalar_tensor_tensor(
                out=gsl, in0=Yv, scalar=float(2.0 ** -16), in1=Xv, op0=ALU.mult, op1=ALU.mult))
            op("dve", lambda E, gsl=gsl: E.tensor_reduce(
                out=st[:, 2:3], in_=gsl, axis=XA, op=ALU.max, apply_absolute_value=True))
            op("dve", lambda E: E.tensor_tensor(out=st[:, 3:4], in0=st[:, 3:4], in1=st[:, 2:3], op=ALU.max))
            chtok[hh] = t6

        if upto == "l1f2":
            return
        op("dve", lambda E: E.tensor_tensor(out=st[:, 3:4], in0=st[:, 3:4], in1=st[:, 10:11], op=ALU.mult))
        tcc2 = amax_part1([3])
        h0t, h1t = wtok.get(("ubf", 0)), wtok.get(("ubf", 1))
        uptok = {0: h0t, 1: h0t, 2: h1t, 3: h1t}
        ldtok = {}

        def upload(u):
            c9, h9 = u // 2, u % 2
            sl = u % 4
            tu = uptok.get(sl)
            if tu is not None:
                P.waits("gp", tu if isinstance(tu, list) else [tu])
            ldtok[u] = dma("gp", ubf[:, sl * W:(sl + 1) * W],
                           self.updr[c9 * 128:(c9 + 1) * 128, h9 * W:(h9 + 1) * W])

        for u in range(4):
            upload(u)
        amax_part2([3], 12, tcc2)

        op("dve", lambda E: E.memset(st[:, 4:5], 0.0))
        op("dve", lambda E: E.tensor_tensor(out=st[:, 16:17], in0=st[:, 8:9], in1=st[:, 12:13], op=ALU.mult))
        t18 = op("dve", lambda E: E.tensor_tensor(out=st[:, 18:19], in0=st[:, 10:11], in1=st[:, 13:14], op=ALU.mult))
        P.wait("act", t18)
        for ci in range(48):
            c, hh = ci // 2, ci % 2
            sl = ci % 4
            gsl = gch(c)[:, hh * W:(hh + 1) * W]
            usl = ubf[:, sl * W:(sl + 1) * W]
            Xv, Yv = Rg(hh), Rg(2 + hh)
            tx = chtok.get(hh)
            if tx is not None:
                P.wait("act", tx)
            P.wait("act", ldtok[ci])
            t1 = op("act", lambda E, usl=usl, Xv=Xv: E.activation(
                out=Xv, in_=usl, func=AF.Copy, scale=st[:, 9:10], bias=MQ))
            uptok[sl] = t1
            if ci + 4 < 48:
                upload(ci + 4)
            t2 = op("act", lambda E, gsl=gsl, Yv=Yv: E.activation(
                out=Yv, in_=gsl, func=AF.Copy, scale=st[:, 18:19], bias=MQ))
            P.wait("dve", t1)
            op("dve", lambda E, Xv=Xv: E.tensor_scalar(
                out=Xv, in0=Xv, scalar1=MQ, scalar2=st[:, 16:17], op0=ALU.subtract, op1=ALU.mult))
            P.wait("dve", t2)
            t4 = op("dve", lambda E, gsl=gsl, Xv=Xv, Yv=Yv: E.scalar_tensor_tensor(
                out=gsl, in0=Yv, scalar=MQ, in1=Xv, op0=ALU.subtract, op1=ALU.mult))
            op("dve", lambda E, gsl=gsl: E.tensor_reduce(
                out=st[:, 2:3], in_=gsl, axis=XA, op=ALU.max, apply_absolute_value=True))
            op("dve", lambda E: E.tensor_tensor(out=st[:, 4:5], in0=st[:, 4:5], in1=st[:, 2:3], op=ALU.max))
            chtok[hh] = t4

        if upto == "l1f3":
            return
        tcc3 = amax_part1([4])
        trc3 = amax_part2([4], 14, tcc3)

        P.wait("act", trc3)
        for ci in range(48):
            c, hh = ci // 2, ci % 2
            gsl = gch(c)[:, hh * W:(hh + 1) * W]
            Xv = Rg(hh)
            tx = chtok.get(hh)
            if tx is not None:
                P.wait("act", tx)
            t1 = op("act", lambda E, gsl=gsl, Xv=Xv: E.activation(
                out=Xv, in_=gsl, func=AF.Copy, scale=st[:, 15:16], bias=MQ))
            P.wait("dve", t1)
            t2 = op("dve", lambda E, gsl=gsl, Xv=Xv: E.tensor_scalar(
                out=gsl, in0=Xv, scalar1=MQ, scalar2=st[:, 14:15], op0=ALU.subtract, op1=ALU.mult))
            chtok[hh] = t2
        self.barrier()
        if upto == "l1f4":
            return

        d_off = self.ddump("offn", [24 * 128, 256], BF16)
        if d_off is not None:
            for c in range(24):
                t9 = dma("sp", d_off[c * 128:(c + 1) * 128, :], gch(c)[:, 0:256])
                P.wait("sp", t9)

        xstok = {}
        for tt in range(TT):
            bi = tt % 2
            xv = self.XS.ap() if bi == 0 else H0f[:, 0:1024]
            lp = xstok.get(bi)
            if lp is not None:
                P.wait("sp", lp)
            tlx = dma("sp", xv, self.x1dr[tt * 128:(tt + 1) * 128, :])
            tmm = None
            for g in range(2):
                pb = self.PB[2 * bi + g]
                self.bank_wait(2 * bi + g)
                for ic in range(24):
                    tmm = P.emit("pe", lambda E, g=g, ic=ic, pb=pb, tt=tt: E.matmul(
                        out=pb.ap(), lhsT=gch(ic)[:, tt * 128:(tt + 1) * 128],
                        rhs=wdT(ic)[:, g * 512:(g + 1) * 512],
                        start=(ic == 0), stop=(ic == 23)), "spe" if ic == 23 else None)
            P.waits("dve", [tmm, tlx])
            op("dve", lambda E, bi=bi, xv=xv: E.tensor_tensor(
                out=xv[:, 0:512], in0=self.PB[2 * bi].ap(), in1=xv[:, 0:512], op=ALU.add))
            tev = op("dve", lambda E, bi=bi, xv=xv: E.tensor_tensor(
                out=xv[:, 512:1024], in0=self.PB[2 * bi + 1].ap(), in1=xv[:, 512:1024], op=ALU.add))
            self.bank_done(2 * bi, tev)
            self.bank_done(2 * bi + 1, tev)
            P.waits("sp", [tev])
            xstok[bi] = dma("sp", self.y_out[tt * 128:(tt + 1) * 128, :], xv)
        self.barrier()

    def finish(self):
        P = self.P
        toks = [P.tok(s) for s in ("dsp", "dgp", "scc", "spe", "sdve", "sact", "spool")]
        P.waits("sp", toks)
        with self.nc.Block() as block:
            @block.sync
            def _(E):
                for f in P.ops["sp"]:
                    f(E)

            @block.gpsimd
            def _(E):
                for f in P.ops["gp"]:
                    f(E)

            @block.tensor
            def _(E):
                for f in P.ops["pe"]:
                    f(E)

            @block.vector
            def _(E):
                for f in P.ops["dve"]:
                    f(E)

            @block.scalar
            def _(E):
                for f in P.ops["act"]:
                    f(E)


def build(T=2048, upto="full", dump=(), rg_cores=NCORES):
    b = Builder(T, rg_cores, dump)
    with ExitStack() as stk:
        b.stk = stk
        b.declare_io()
        b.alloc()
        b.ph_init()
        b.build_l0(upto)
        if upto.startswith("l1") or upto == "full":
            b.build_l1(upto)
        b.finish()
    return b.nc, b.dumps



def rope_tables(T, core, ncores=NCORES):
    S = T * ncores
    pos = np.arange(core * T, (core + 1) * T, dtype=np.float64)
    freqs = 1.0 / 10000.0 ** (np.arange(0, 64, 2, dtype=np.float64) / 64.0)
    ang = pos[:, None] * freqs[None, :]
    return np.cos(ang).astype(np.float32), np.sin(ang).astype(np.float32)


def perm_qk():
    perm = np.zeros(1024, dtype=np.int64)
    for h in range(16):
        for p in range(2):
            for i in range(32):
                perm[h * 64 + p * 32 + i] = h * 64 + 2 * i + p
    return perm


def prep_core_inputs(inputs, core, T=2048):
    perm = perm_qk()
    x = np.ascontiguousarray(np.asarray(inputs["L0_attn_x"]).reshape(-1, 1024)[core * T:(core + 1) * T])
    wq = np.asarray(inputs["L0_attn_Wq"])[:, perm]
    wk = np.asarray(inputs["L0_attn_Wk"])[:, perm]
    wv = np.asarray(inputs["L0_attn_Wv"])
    wo = np.asarray(inputs["L0_attn_Wo"])
    cos, sin = rope_tables(T, core)
    ident = np.eye(128, dtype=ml_dtypes.bfloat16)
    mask = np.zeros((128, 64), dtype=np.float32)
    tri = np.triu(np.ones((64, 64), dtype=np.float32)) * np.float32(np.exp(44.0))
    mask[0:64] = tri
    mask[64:128] = tri
    wq1 = np.asarray(inputs["L1_attn_Wq"]).T
    wk1 = np.asarray(inputs["L1_attn_Wk"]).T
    wv1 = np.asarray(inputs["L1_attn_Wv"]).T
    wa1 = np.asarray(inputs["L1_attn_Wa_w"]).T
    l1wt = np.ascontiguousarray(np.concatenate([wq1, wk1, wv1, wk1, wk1, wk1, wa1], axis=1))
    qc = np.asarray(inputs["L1_attn_Qc"])
    vc = np.asarray(inputs["L1_attn_Vc"])
    l1qc = np.ascontiguousarray(qc.transpose(1, 0, 2).reshape(64, 512))
    l1vc = np.ascontiguousarray(vc.transpose(1, 0, 2).reshape(64, 1024))
    rmask = np.zeros((128, 16), dtype=np.float32)
    for r in range(8):
        rmask[:, r] = 1.0 if r < core else 0.0
        rmask[:, 8 + r] = 0.0 if r < core else 1.0
    return {
        "x": x,
        "rmaskin": rmask,
        "wqk": np.ascontiguousarray(np.concatenate([wq, wk], axis=1)),
        "wvo": np.ascontiguousarray(np.concatenate([wv, wo], axis=1)),
        "w1": np.asarray(inputs["L0_ffn_w1"]),
        "b1v": np.asarray(inputs["L0_ffn_b1"]),
        "w2": np.asarray(inputs["L0_ffn_w2"]),
        "b2v": np.asarray(inputs["L0_ffn_b2"]),
        "cos32": cos, "sin32": sin,
        "identin": ident, "maskin": mask,
        "l1wt": l1wt,
        "l1qc": l1qc,
        "l1vc": l1vc,
        "l1wo": np.ascontiguousarray(np.asarray(inputs["L1_attn_Wo"]).T),
        "l1wab": np.asarray(inputs["L1_attn_Wa_b"]),
        "l1gamma": np.asarray(inputs["L1_attn_gamma"]),
        "l1beta": np.asarray(inputs["L1_attn_beta"]),
        "wgate": np.asarray(inputs["L1_ffn_w_gate"]),
        "wup": np.asarray(inputs["L1_ffn_w_up"]),
        "wdown": np.asarray(inputs["L1_ffn_w_down"]),
    }


def kernel(**inputs):
    T = 2048
    nc, _ = build(T=T, upto="full")
    in_maps = [prep_core_inputs(inputs, c, T) for c in range(NCORES)]
    res = run_bass_kernel_spmd(nc, in_maps, core_ids=list(range(NCORES)))
    y = np.concatenate([res.results[c]["y"] for c in range(NCORES)], axis=0)
    return y.reshape(1, 16384, 1024)
